# version 08
# baseline (speedup 1.0000x reference)
import numpy as np
import ml_dtypes
from contextlib import ExitStack

import concourse.bass as bass
import concourse.mybir as mybir
from concourse.bass_utils import run_bass_kernel_spmd
from concourse.aot_env import allow_ici_all_to_all

F32 = mybir.dt.float32
BF16 = mybir.dt.bfloat16
AF = mybir.ActivationFunctionType
ALU = mybir.AluOpType
AX = mybir.AxisListType

BF = ml_dtypes.bfloat16

NB = 4
S = 2048
D = 2048
DC = 16
H2 = 2
TC = 8
DFF = 8192
FG = 16
OG = 8
INV_SQRT_HD = 0.08838834764831845
EPS_LN = 1e-5
EPS_BL = 1e-6
CLIP_C = (128.0 - 1e-6) / 128.0
NWELEM = float(DFF * D)


class Sem:
    __slots__ = ("h", "v", "name", "dma")

    def __init__(self, nc, name, dma=False):
        self.h = nc.alloc_semaphore(name)
        self.v = 0
        self.name = name
        self.dma = dma


class Prog:

    ENGS = ("t", "v", "s", "g", "sp")

    def __init__(self, ve_sem, ac_sem):
        self.ops = {e: [] for e in self.ENGS}
        self.ve = ve_sem
        self.ac = ac_sem

    def w(self, e, sem, v):
        if v is None or v <= 0:
            return
        self.ops[e].append(lambda E, s=sem, vv=v: E.wait_ge(s.h, vv))

    def o(self, e, build, sem=None, n=1):
        if e == "v" and not (sem is not None and sem.dma):
            if self.ve.v > 0:
                self.w("v", self.ve, self.ve.v)
            if sem is None:
                sem = self.ve
            assert sem is self.ve
        elif e == "s" and not (sem is not None and sem.dma):
            if self.ac.v > 0:
                self.w("s", self.ac, self.ac.v)
            if sem is None:
                sem = self.ac
            assert sem is self.ac
        if sem is None:
            self.ops[e].append(build)
            return None
        if sem.dma:
            n = 16
        sem.v += n
        self.ops[e].append(lambda E, b=build, s=sem, nn=n: b(E).then_inc(s.h, nn))
        return sem.v

    def replay(self, nc):
        with nc.Block() as blk:
            for name, attr in (("t", "tensor"), ("v", "vector"), ("s", "scalar"),
                               ("g", "gpsimd"), ("sp", "sync")):
                if self.ops[name]:
                    fns = self.ops[name]

                    def body(E, fns=fns):
                        for f in fns:
                            f(E)
                    getattr(blk, attr)(body)


def build_nc():
    nc = bass.Bass("TRN2", target_bir_lowering=False)

    xt = nc.dram_tensor("xt", [NB, 4, 128, DC, 512], F32, kind="ExternalInput")
    xo = nc.dram_tensor("xo", [TC, 128, D], F32, kind="ExternalInput")
    wqk = nc.dram_tensor("wqk", [H2, DC, 128, 256], F32, kind="ExternalInput")
    wv = nc.dram_tensor("wv", [DC, 128, 256], F32, kind="ExternalInput")
    wout = nc.dram_tensor("wout", [DC, 128, D], F32, kind="ExternalInput")
    wfc = nc.dram_tensor("wfc", [FG, DC, 128, 512], F32, kind="ExternalInput")
    wproj = nc.dram_tensor("wproj", [OG, 64, 128, 256], F32, kind="ExternalInput")
    tri = nc.dram_tensor("tri", [128, 128], BF16, kind="ExternalInput")
    out = nc.dram_tensor("out", [TC, 128, D], F32, kind="ExternalOutput")

    d_a2a_in = nc.dram_tensor("d_a2a_in", [8, H2, 128, 1024], BF16)
    d_a2a_out = nc.dram_tensor("d_a2a_out", [8, H2, 128, 1024], BF16)
    d_xhat = nc.dram_tensor("d_xhat", [1024, D], BF16)
    d_hhat = nc.dram_tensor("d_hhat", [TC, 128, DFF], BF16)
    d_x1 = nc.dram_tensor("d_x1", [TC, 128, D], F32)
    d_proj = nc.dram_tensor("d_proj", [TC, 128, D], F32)

    mm = Sem(nc, "mm")
    ve = Sem(nc, "ve")
    ac = Sem(nc, "ac")
    cc = Sem(nc, "cc")
    sq = Sem(nc, "sq", dma=True)
    q_wq = Sem(nc, "q_wq", dma=True)
    q_tri = Sem(nc, "q_tri", dma=True)
    q_xq = [Sem(nc, "q_xq0", dma=True), Sem(nc, "q_xq1", dma=True)]
    q_gb = [Sem(nc, "q_gb0", dma=True), Sem(nc, "q_gb1", dma=True)]
    q_av = [Sem(nc, "q_av0", dma=True), Sem(nc, "q_av1", dma=True)]
    q_lda = Sem(nc, "q_lda", dma=True)
    q_ldx = Sem(nc, "q_ldx", dma=True)
    q_wo = [Sem(nc, "q_wo0", dma=True), Sem(nc, "q_wo1", dma=True)]
    q_xh = [Sem(nc, "q_xh0", dma=True), Sem(nc, "q_xh1", dma=True)]
    q_tr = Sem(nc, "q_tr", dma=True)
    q_wc = [Sem(nc, "q_wc0", dma=True), Sem(nc, "q_wc1", dma=True)]
    q_hh = Sem(nc, "q_hh", dma=True)
    q_ps = [Sem(nc, "q_ps0", dma=True), Sem(nc, "q_ps1", dma=True)]
    q_fn = [Sem(nc, "q_fn0", dma=True), Sem(nc, "q_fn1", dma=True)]
    q_st = [Sem(nc, "q_st0", dma=True), Sem(nc, "q_st1", dma=True)]

    ctx0 = ExitStack()
    sb = lambda n, s, d: ctx0.enter_context(nc.sbuf_tensor(n, s, d))

    ones_b = sb("ones_b", [128, 128], BF16)
    onesf_col = sb("onesf_col", [128, 1], F32)
    onesf_row = sb("onesf_row", [1, 128], F32)
    tri_s = sb("tri_s", [128, 128], BF16)
    eps_ln = sb("eps_ln", [128, 1], F32)
    eps_bl = sb("eps_bl", [128, 1], F32)
    gacc_f = sb("gacc_f", [128, 1], F32)
    gacc_p = sb("gacc_p", [128, 1], F32)
    gpart = sb("gpart", [128, 2], F32)
    tpos_f = sb("tpos_f", [128, 1], F32)
    tneg_f = sb("tneg_f", [128, 1], F32)
    gneg_f = sb("gneg_f", [128, 1], F32)
    tpos_p = sb("tpos_p", [128, 1], F32)
    tneg_p = sb("tneg_p", [128, 1], F32)
    gneg_p = sb("gneg_p", [128, 1], F32)
    sc11 = sb("sc11", [1, 1], F32)
    st_m = sb("st_m", [128, 1], F32)
    st_sq = sb("st_sq", [128, 1], F32)
    st_var = sb("st_var", [128, 1], F32)
    st_rstd = sb("st_rstd", [128, 1], F32)
    st_mb = sb("st_mb", [128, 1], F32)
    st_g = sb("st_g", [128, 1], F32)
    st_gp = sb("st_gp", [128, 1], F32)
    st_gn = sb("st_gn", [128, 1], F32)
    st_lo = sb("st_lo", [128, 1], F32)
    st_hi = sb("st_hi", [128, 1], F32)
    st_d1 = sb("st_d1", [128, 1], F32)
    st_d2 = sb("st_d2", [128, 1], F32)

    psum = ExitStack()
    pbig = psum.enter_context(nc.psum_tensor("pbig", [128, 2048], F32))
    pA = psum.enter_context(nc.psum_tensor("pA", [128, 512], F32))
    pB = psum.enter_context(nc.psum_tensor("pB", [128, 512], F32))
    pC = psum.enter_context(nc.psum_tensor("pC", [128, 512], F32))
    pD = psum.enter_context(nc.psum_tensor("pD", [128, 512], F32))

    cp = {}
    P = None

    def ln_stats(src_ap, eps_tile, dump_ap, rsem, rval):
        n = src_ap.shape[-1]
        P.w("v", rsem, rval)
        P.o("v", lambda v, a=src_ap: v.reduce_sum(st_m[:, 0:1], a, AX.X), ve)
        c2 = P.o("v", lambda v, a=src_ap, d=dump_ap: v.scalar_tensor_tensor(
            d, a, 1.0, a, op0=ALU.bypass, op1=ALU.mult,
            accum_out=st_sq[:, 0:1]), ve)
        P.w("s", ve, c2)
        P.o("s", lambda s, nn=n: s.activation(
            st_m[:, 0:1], st_m[:, 0:1], AF.Identity, scale=1.0 / nn), ac)
        P.o("s", lambda s, nn=n: s.activation(
            st_sq[:, 0:1], st_sq[:, 0:1], AF.Identity, scale=1.0 / nn), ac)
        c4 = P.o("s", lambda s: s.activation(
            st_var[:, 0:1], st_m[:, 0:1], AF.Square), ac)
        P.w("v", ac, c4)
        c5 = P.o("v", lambda v: v.tensor_sub(
            st_var[:, 0:1], st_var[:, 0:1], st_sq[:, 0:1]), ve)
        P.w("s", ve, c5)
        P.o("s", lambda s, e=eps_tile: s.activation(
            st_rstd[:, 0:1], st_var[:, 0:1], AF.Ln,
            bias=e[:, 0:1], scale=-1.0), ac)
        crs = P.o("s", lambda s: s.activation(
            st_rstd[:, 0:1], st_rstd[:, 0:1], AF.Exp, scale=-0.5), ac)
        P.w("v", ac, crs)
        cmb = P.o("v", lambda v: v.tensor_scalar(
            st_mb[:, 0:1], st_m[:, 0:1], st_rstd[:, 0:1], -1.0,
            op0=ALU.mult, op1=ALU.mult), ve)
        return cmb

    def gamma_chunk(gacc, gbuf_t, gdump_t, src_ap, ch):
        slot = ch % 2
        if ch >= 2:
            P.w("sp", ac, cp[("gst", id(gacc), ch - 2)])
        ldv = P.o("sp", lambda sp, a=src_ap, b=slot, g=gbuf_t:
                  sp.dma_start(g[:, b, :].rearrange(
                      "p (a f) -> p a f", a=a.shape[1]), a), q_gb[slot])
        P.w("s", q_gb[slot], ldv)
        if ch >= 2:
            P.w("s", ve, cp[("gadd", id(gacc), ch - 2)])
        cp[("gst", id(gacc), ch)] = P.o(
            "s", lambda s, b=slot, g=gbuf_t, d=gdump_t, ps=slot:
            s.activation(d[:, :], g[:, b, :], AF.Abs,
                         accum_out=gpart[:, ps:ps + 1]), ac)
        P.w("v", ac, cp[("gst", id(gacc), ch)])
        cp[("gadd", id(gacc), ch)] = P.o(
            "v", lambda v, ps=slot, ga=gacc:
            v.tensor_add(ga[:, 0:1], ga[:, 0:1], gpart[:, ps:ps + 1]), ve)
        cp[("gsum", id(gacc))] = ve.v

    def gamma_final(gacc, tpos, tneg, gneg):
        P.w("t", ve, cp[("gsum", id(gacc))])
        P.o("t", lambda t, ga=gacc: t.matmul(
            pD[0:1, 0:1], ga[:, 0:1], onesf_col[:, 0:1],
            start=True, stop=True), mm)
        s1 = mm.v
        P.w("s", mm, s1)
        s2 = P.o("s", lambda s: s.copy(sc11[:, :], pD[0:1, 0:1]), ac)
        P.w("t", ac, s2)
        P.o("t", lambda t: t.matmul(pD[:, 1:2], onesf_row[:, :], sc11[:, :],
                                    start=True, stop=True), mm)
        s3 = mm.v
        P.w("v", mm, s3)
        P.o("v", lambda v, tp=tpos: v.tensor_scalar(
            tp[:, 0:1], pD[:, 1:2], 0.5 / NWELEM, 0.5 * EPS_BL,
            op0=ALU.mult, op1=ALU.add), ve)
        P.o("v", lambda v, gn=gneg: v.tensor_scalar(
            gn[:, 0:1], pD[:, 1:2], -1.0 / NWELEM, None, op0=ALU.mult), ve)
        cfin = P.o("v", lambda v, tp=tpos, tn=tneg: v.tensor_scalar(
            tn[:, 0:1], tp[:, 0:1], -1.0, None, op0=ALU.mult), ve)
        cp[("gfin", id(gacc))] = cfin
        return cfin

    ctxq = ExitStack()
    sbq = lambda n, s, d: ctxq.enter_context(nc.sbuf_tensor(n, s, d))
    xq = sbq("xq", [128, 2, DC, 512], BF16)
    wqk_s = sbq("wqk_s", [128, H2, DC, 256], BF16)
    wv_s = sbq("wv_s", [128, DC, 256], BF16)
    kT_s = sbq("kT_s", [128, H2, S], BF16)
    qT_s = sbq("qT_s", [128, H2, S], BF16)
    v_s = sbq("v_s", [128, 16, 256], BF16)
    pt = sbq("pt", [128, 4, 512], BF16)
    lln = sbq("lln", [128, 512], F32)
    lrec = sbq("lrec", [128, 512], F32)
    avst = sbq("avst", [128, 2, 512], BF16)
    gbuf = sbq("gbuf", [128, 2, 2048], F32)
    gdump = sbq("gdump", [128, 2048], F32)

    P = Prog(ve, ac)

    for f in (lambda g: g.memset(ones_b[:, :], 1.0),
              lambda g: g.memset(onesf_col[:, :], 1.0),
              lambda g: g.memset(onesf_row[:, :], 1.0),
              lambda g: g.memset(eps_ln[:, :], EPS_LN),
              lambda g: g.memset(eps_bl[:, :], EPS_BL),
              lambda g: g.memset(gacc_f[:, :], 0.0)):
        P.o("g", f)
    CONST_READY = P.o("g", lambda g: g.memset(gacc_p[:, :], 0.0), cc)
    P.o("g", lambda g: g.dma_start(
        wqk_s[:, :, :, :], wqk.ap().rearrange("h dc p f -> p h dc f")), q_wq)
    WQKV_LD = P.o("g", lambda g: g.dma_start(
        wv_s[:, :, :], wv.ap().rearrange("dc p f -> p dc f")), q_wq)
    TRI_LD = P.o("sp", lambda sp: sp.dma_start(tri_s[:, :], tri.ap()), q_tri)
    for e in ("t", "v", "s"):
        P.w(e, cc, CONST_READY)
    P.w("v", q_tri, TRI_LD)

    pair_hist = []
    avdma_hist = []
    qkv_ping = [0]

    NGCH = 64

    def wfc_chunk_ap(ch):
        fg, pr = divmod(ch, 4)
        return wfc.ap()[fg, 4 * pr:4 * pr + 4, :, :].rearrange(
            "dc p f -> p dc f")

    gamma_f_pos = [0]

    def emit_gamma_f(n):
        for _ in range(n):
            ch = gamma_f_pos[0]
            if ch >= NGCH:
                return
            gamma_chunk(gacc_f, gbuf, gdump, wfc_chunk_ap(ch), ch)
            gamma_f_pos[0] += 1

    def q_batch(b):
        for i in range(4):
            gidx = b * 4 + i
            slot = gidx % 2
            if gidx >= 2:
                P.w("g", mm, cp[("xq_done", gidx - 2)])
            ldv = P.o("g", lambda g, bb=b, ii=i: g.dma_start(
                xq[:, ii % 2, :, :], xt.ap()[bb, ii, :, :, :]), q_xq[slot])
            P.w("t", q_xq[slot], ldv)
            if gidx == 0:
                P.w("t", q_wq, WQKV_LD)
            for h2 in range(H2):
                for qk in range(2):
                    dst = (qT_s, kT_s)[qk]
                    bank = (pC, pD)[qkv_ping[0] % 2]
                    qkv_ping[0] += 1
                    P.w("t", ve, cp.get(("evac", id(bank))))
                    for dc in range(DC):
                        P.o("t", lambda t, bk=bank, h=h2, q=qk, d=dc, ii=i:
                            t.matmul(bk[:, 0:512],
                                     wqk_s[:, h, d, q * 128:(q + 1) * 128],
                                     xq[:, ii % 2, d, :],
                                     start=(d == 0), stop=(d == DC - 1)),
                            mm if dc == DC - 1 else None)
                    P.w("v", mm, mm.v)
                    cp[("evac", id(bank))] = P.o(
                        "v", lambda v, bk=bank, ds=dst, h=h2, ii=i:
                        v.tensor_copy(ds[:, h, ii * 512:(ii + 1) * 512],
                                      bk[:, 0:512]), ve)
            for t128 in range(4):
                bank = (pC, pD)[qkv_ping[0] % 2]
                qkv_ping[0] += 1
                P.w("t", ve, cp.get(("evac", id(bank))))
                for dc in range(DC):
                    P.o("t", lambda t, bk=bank, d=dc, ii=i, tk=t128:
                        t.matmul(bk[:, 0:256],
                                 xq[:, ii % 2, d, tk * 128:(tk + 1) * 128],
                                 wv_s[:, d, :],
                                 start=(d == 0), stop=(d == DC - 1)),
                        mm if dc == DC - 1 else None)
                P.w("v", mm, mm.v)
                cp[("evac", id(bank))] = P.o(
                    "v", lambda v, bk=bank, kb=i * 4 + t128:
                    v.tensor_copy(v_s[:, kb, :], bk[:, 0:256]), ve)
            cp[("xq_done", gidx)] = mm.v
        qkv_done = ve.v

        for h2 in range(H2):
            P.w("t", ve, qkv_done)
            for s4 in range(4):
                kbN = 4 * (s4 + 1)
                for pr in range(kbN // 2):
                    k0, k1 = 2 * pr, 2 * pr + 1
                    if len(pair_hist) >= 2:
                        P.w("t", ac, pair_hist[-2][0])
                    for kb in (k0, k1):
                        qq = kb - 4 * s4
                        sfx = 128 * qq if qq >= 0 else 0
                        P.o("t", lambda t, h=h2, s=s4, kkb=kb, sf=sfx:
                            t.matmul(pbig[:, (kkb % 4) * 512 + sf:
                                             (kkb % 4) * 512 + 512],
                                     kT_s[:, h, kkb * 128:(kkb + 1) * 128],
                                     qT_s[:, h, s * 512 + sf:(s + 1) * 512],
                                     start=True, stop=True),
                            mm if kb == k1 else None)
                    cps = mm.v
                    P.w("s", mm, cps)
                    if len(pair_hist) >= 2:
                        P.w("s", mm, pair_hist[-2][1])
                    cpe = P.o("s", lambda s, kk=k0: s.activation(
                        pt[:, kk % 4:kk % 4 + 2, :],
                        pbig[:, (kk % 4) * 512:(kk % 4) * 512 + 1024],
                        AF.Exp, scale=INV_SQRT_HD), ac)
                    cpt = None
                    for kb in (k0, k1):
                        qq = kb - 4 * s4
                        if 0 <= qq <= 3:
                            P.w("v", ac, cpe)
                            cpt = P.o("v", lambda v, kkb=kb, q=qq:
                                      v.tensor_mul(
                                          pt[:, kkb % 4, q * 128:(q + 1) * 128],
                                          pt[:, kkb % 4, q * 128:(q + 1) * 128],
                                          tri_s[:, :]), ve)
                    P.w("t", ac, cpe)
                    if cpt is not None:
                        P.w("t", ve, cpt)
                    if pr == 0:
                        P.w("t", ve, cp.get("avl_used"))
                        P.w("t", ac, cp.get("l_used"))
                    for kb in (k0, k1):
                        qq = kb - 4 * s4
                        sfx = 128 * qq if qq >= 0 else 0
                        fst, lst = (kb == 0), (kb == kbN - 1)
                        P.o("t", lambda t, kkb=kb, h=h2, sf=sfx, f=fst, l=lst:
                            t.matmul(pA[:, sf:512],
                                     v_s[:, kkb, h * 128:(h + 1) * 128],
                                     pt[:, kkb % 4, sf:512],
                                     start=f, stop=l), None)
                        P.o("t", lambda t, kkb=kb, sf=sfx, f=fst, l=lst:
                            t.matmul(pB[:, sf:512], ones_b[:, :],
                                     pt[:, kkb % 4, sf:512],
                                     start=f, stop=l),
                            mm if kb == k1 else None)
                    pair_hist.append((cpe, mm.v))
                P.w("s", mm, pair_hist[-1][1])
                cp["l_used"] = P.o("s", lambda s: s.activation(
                    lln[:, :], pB[:, 0:512], AF.Ln), ac)
                clr = P.o("s", lambda s: s.activation(
                    lrec[:, :], lln[:, :], AF.Exp, scale=-1.0), ac)
                P.w("v", ac, clr)
                if len(avdma_hist) >= 2:
                    P.w("v", avdma_hist[-2][0], avdma_hist[-2][1])
                cp["avl_used"] = P.o("v", lambda v, s=s4: v.tensor_mul(
                    avst[:, s % 2, :], pA[:, 0:512], lrec[:, :]), ve)
                P.w("sp", ve, cp["avl_used"])
                j = 2 * b + (s4 // 2)
                off = (s4 % 2) * 512
                aslot = len(avdma_hist) % 2
                dv = P.o("sp", lambda sp, jj=j, h=h2, of=off, s=s4:
                         sp.dma_start(d_a2a_in.ap()[jj, h, :, of:of + 512],
                                      avst[:, s % 2, :]), q_av[aslot])
                avdma_hist.append((q_av[aslot], dv))

    for b in range(NB):
        q_batch(b)
        emit_gamma_f(16)
    emit_gamma_f(NGCH)
    gamma_final(gacc_f, tpos_f, tneg_f, gneg_f)

    P.replay(nc)
    ctxq.close()

    ctxw = ExitStack()
    sbw = lambda n, s, d: ctxw.enter_context(nc.sbuf_tensor(n, s, d))
    avT_all = sbw("avT_all", [128, 16, 1024], BF16)
    stream = sbw("stream", [128, TC, D], F32)
    woutb = sbw("woutb", [128, 2, DC, 512], BF16)
    xn_st = sbw("xn_st", [128, D], F32)
    xh_st = sbw("xh_st", [128, 2, D], BF16)
    gbuf2 = sbw("gbuf2", [128, 2, 2048], F32)
    gdump2 = sbw("gdump2", [128, 2048], F32)

    P = Prog(ve, ac)

    P.w("g", q_av[0], q_av[0].v)
    P.w("g", q_av[1], q_av[1].v)

    def _cc_call(g):
        with allow_ici_all_to_all():
            return g.collective_compute(
                "AllToAll", ALU.bypass, replica_groups=[list(range(8))],
                ins=[d_a2a_in.ap().rearrange("a b c d -> a (b c d)")],
                outs=[d_a2a_out.ap().rearrange("a b c d -> a (b c d)")])

    CC_DONE = P.o("g", _cc_call, cc)

    def wproj_chunk_ap(ch):
        og, fr = divmod(ch, 8)
        return wproj.ap()[og, 8 * fr:8 * fr + 8, :, :].rearrange(
            "fc p f -> p fc f")

    for ch in range(NGCH):
        gamma_chunk(gacc_p, gbuf2, gdump2, wproj_chunk_ap(ch), ch)
    gamma_final(gacc_p, tpos_p, tneg_p, gneg_p)
    cp[("evac", id(pD))] = cp[("gfin", id(gacc_p))]
    cp[("evac", id(pC))] = None

    P.w("sp", cc, CC_DONE)
    for j in range(8):
        for h2 in range(H2):
            AVT_LD = P.o("sp", lambda sp, jj=j, h=h2: sp.dma_start(
                avT_all[:, jj * 2 + h, :], d_a2a_out.ap()[jj, h, :, :]), q_lda)
    for tc in range(TC):
        XO_LD = P.o("sp", lambda sp, t=tc: sp.dma_start(
            stream[:, t, :], xo.ap()[t, :, :]), q_ldx)

    for og in range(4):
        slot = og % 2
        if og >= 2:
            P.w("g", mm, cp[("wout_mm", og - 2)])
        ldv = P.o("g", lambda g, o=og: g.dma_start(
            woutb[:, o % 2, :, :],
            wout.ap()[:, :, o * 512:(o + 1) * 512].rearrange(
                "dvc p f -> p dvc f")), q_wo[slot])
        P.w("t", q_wo[slot], ldv)
        if og == 0:
            P.w("t", q_lda, AVT_LD)
        for tc in range(TC):
            bank = (pC, pD)[tc % 2]
            P.w("t", ve, cp.get(("evac", id(bank))))
            for dvc in range(DC):
                P.o("t", lambda t, bk=bank, d=dvc, tt=tc, o=og:
                    t.matmul(bk[:, 0:512],
                             avT_all[:, d, tt * 128:(tt + 1) * 128],
                             woutb[:, o % 2, d, :],
                             start=(d == 0), stop=(d == DC - 1)),
                    mm if dvc == DC - 1 else None)
            P.w("v", mm, mm.v)
            if og == 0 and tc == 0:
                P.w("v", q_ldx, XO_LD)
            cp[("evac", id(bank))] = P.o(
                "v", lambda v, bk=bank, tt=tc, o=og:
                v.tensor_add(stream[:, tt, o * 512:(o + 1) * 512],
                             bk[:, 0:512],
                             stream[:, tt, o * 512:(o + 1) * 512]), ve)
        cp[("wout_mm", og)] = mm.v
    RESID1 = ve.v

    cxh_prev = None
    for tc in range(TC):
        if cxh_prev is not None:
            P.w("s", ve, cxh_prev)
        cmb = ln_stats(stream[:, tc, :], eps_ln, xn_st[:, :], ve, RESID1)
        P.w("s", ve, cmb)
        cx1 = P.o("s", lambda s, t=tc: s.activation(
            stream[:, t, :], stream[:, t, :], AF.Identity,
            bias=st_mb[:, 0:1], scale=st_rstd[:, 0:1]), ac)
        cmb2 = ln_stats(stream[:, tc, :], eps_bl, xn_st[:, :], ac, cx1)
        P.w("s", ve, cmb2)
        cxn = P.o("s", lambda s, t=tc: s.activation(
            xn_st[:, :], stream[:, t, :], AF.Identity,
            bias=st_mb[:, 0:1], scale=st_rstd[:, 0:1]), ac)
        P.w("sp", ac, cxn)
        P.o("sp", lambda sp, t=tc: sp.dma_start(
            d_x1.ap()[t, :, :], stream[:, t, :]), sq)
        P.w("v", ac, cxn)
        cg1 = P.o("v", lambda v: v.reduce_max(st_g[:, 0:1], xn_st[:, :], AX.X,
                                              apply_absolute_value=True), ve)
        P.w("s", ve, cg1)
        P.o("s", lambda s: s.activation(st_gp[:, 0:1], st_g[:, 0:1],
                                        AF.Identity, scale=CLIP_C), ac)
        cg2 = P.o("s", lambda s: s.activation(st_gn[:, 0:1], st_g[:, 0:1],
                                              AF.Identity, scale=-CLIP_C), ac)
        P.w("v", ac, cg2)
        if tc >= 2:
            P.w("v", q_xh[tc % 2], cp[("xhat_dma", tc - 2)])
        cxh = P.o("v", lambda v, t=tc: v.tensor_scalar(
            xh_st[:, t % 2, :], xn_st[:, :], st_gn[:, 0:1], st_gp[:, 0:1],
            op0=ALU.max, op1=ALU.min), ve)
        cxh_prev = cxh
        P.w("sp", ve, cxh)
        cp[("xhat_dma", tc)] = P.o("sp", lambda sp, t=tc: sp.dma_start(
            d_xhat.ap()[t * 128:(t + 1) * 128, :], xh_st[:, t % 2, :]),
            q_xh[tc % 2])
    X1_SPILLED = sq.v

    P.replay(nc)
    ctxw.close()

    ctxh = ExitStack()
    htens = ctxh.enter_context(nc.sbuf_tensor("htens", [128, TC, DFF], BF16))
    hsump = ctxh.enter_context(nc.sbuf_tensor("hsump", [128, TC, FG], F32))
    hsqp = ctxh.enter_context(nc.sbuf_tensor("hsqp", [128, TC, FG], F32))
    hmaxp = ctxh.enter_context(nc.sbuf_tensor("hmaxp", [128, TC, FG], F32))
    hminp = ctxh.enter_context(nc.sbuf_tensor("hminp", [128, TC, FG], F32))

    ctxf1 = ExitStack()
    sbf1 = lambda n, s, d: ctxf1.enter_context(nc.sbuf_tensor(n, s, d))
    xhatT = sbf1("xhatT", [128, DC, 1024], BF16)
    ternf = sbf1("ternf", [128, 2, DC, 512], BF16)
    wch = sbf1("wch", [128, 2, 512], F32)
    tsa = sbf1("tsa", [128, 512], F32)
    hsqd = sbf1("hsqd", [128, 512], F32)

    P = Prog(ve, ac)
    P.w("sp", q_xh[0], q_xh[0].v)
    P.w("sp", q_xh[1], q_xh[1].v)
    for dc in range(DC):
        XT_LD = P.o("sp", lambda sp, d=dc: sp.dma_start(
            xhatT[:, d, :], d_xhat.ap()[:, d * 128:(d + 1) * 128],
            transpose=True), q_tr)

    for fg in range(FG):
        for chh in range(16):
            gidx = fg * 16 + chh
            slot = gidx % 2
            if gidx >= 2:
                P.w("sp", ve, cp[("ternf_ch", gidx - 2)])
            ldv = P.o("sp", lambda sp, f=fg, c=chh, b=slot: sp.dma_start(
                wch[:, b, :],
                wfc.ap()[f, c, :, :]), q_wc[slot])
            P.w("v", q_wc[slot], ldv)
            if fg >= 2 and chh == 0:
                P.w("v", mm, cp[("fc_mm", fg - 2)])
            P.o("v", lambda v, b=slot: v.tensor_scalar(
                tsa[:, :], wch[:, b, :], tpos_f[:, 0:1], None,
                op0=ALU.is_ge), ve)
            cp[("ternf_ch", gidx)] = P.o(
                "v", lambda v, b=slot, f=fg, c=chh:
                v.scalar_tensor_tensor(
                    ternf[:, f % 2, c, :],
                    wch[:, b, :], tneg_f[:, 0:1], tsa[:, :],
                    op0=ALU.is_le, op1=ALU.subtract), ve)
        tern_done = ve.v

        P.w("t", ve, tern_done)
        if fg == 0:
            P.w("t", q_tr, XT_LD)
        for tc in range(TC):
            bank = (pC, pD)[tc % 2]
            P.w("t", ac, cp.get(("fc_evac", id(bank))))
            for dc in range(DC):
                P.o("t", lambda t, bk=bank, d=dc, tt=tc, f=fg:
                    t.matmul(bk[:, 0:512],
                             xhatT[:, d, tt * 128:(tt + 1) * 128],
                             ternf[:, f % 2, d, :],
                             start=(d == 0), stop=(d == DC - 1)),
                    mm if dc == DC - 1 else None)
            P.w("s", mm, mm.v)
            cp[("fc_evac", id(bank))] = P.o(
                "s", lambda s, bk=bank, tt=tc, f=fg: s.activation(
                    htens[:, tt, f * 512:(f + 1) * 512], bk[:, 0:512],
                    AF.Gelu, scale=gneg_f[:, 0:1],
                    accum_out=hsump[:, tt, f:f + 1]), ac)
            csq = P.o("s", lambda s, tt=tc, f=fg: s.activation(
                hsqd[:, :], htens[:, tt, f * 512:(f + 1) * 512],
                AF.Square, accum_out=hsqp[:, tt, f:f + 1]), ac)
            P.w("v", ac, csq)
            P.o("v", lambda v, tt=tc, f=fg: v.tensor_reduce(
                hmaxp[:, tt, f:f + 1], htens[:, tt, f * 512:(f + 1) * 512],
                AX.X, ALU.max), ve)
            P.o("v", lambda v, tt=tc, f=fg: v.tensor_reduce(
                hminp[:, tt, f:f + 1], htens[:, tt, f * 512:(f + 1) * 512],
                AX.X, ALU.min), ve)
        cp[("fc_mm", fg)] = mm.v
    F1_DONE_AC = ac.v
    F1_DONE_VE = ve.v

    P.replay(nc)
    ctxf1.close()

    ctx15 = ExitStack()
    hn_st = ctx15.enter_context(nc.sbuf_tensor("hn_st", [128, DFF], F32))
    hh_st = ctx15.enter_context(nc.sbuf_tensor("hh_st", [128, DFF], BF16))
    q_tr2 = Sem(nc, "q_tr2", dma=True)

    P = Prog(ve, ac)
    for tc in range(TC):
        if tc == 0:
            P.w("v", ac, F1_DONE_AC)
        P.o("v", lambda v, t=tc: v.reduce_sum(
            st_m[:, 0:1], hsump[:, t, :], AX.X), ve)
        P.o("v", lambda v, t=tc: v.reduce_sum(
            st_sq[:, 0:1], hsqp[:, t, :], AX.X), ve)
        P.o("v", lambda v, t=tc: v.reduce_max(
            st_d1[:, 0:1], hmaxp[:, t, :], AX.X), ve)
        c2 = P.o("v", lambda v, t=tc: v.tensor_reduce(
            st_d2[:, 0:1], hminp[:, t, :], AX.X, ALU.min), ve)
        P.w("s", ve, c2)
        P.o("s", lambda s: s.activation(
            st_m[:, 0:1], st_m[:, 0:1], AF.Identity, scale=1.0 / DFF), ac)
        P.o("s", lambda s: s.activation(
            st_sq[:, 0:1], st_sq[:, 0:1], AF.Identity, scale=1.0 / DFF), ac)
        c4 = P.o("s", lambda s: s.activation(
            st_var[:, 0:1], st_m[:, 0:1], AF.Square), ac)
        P.w("v", ac, c4)
        P.o("v", lambda v: v.tensor_sub(
            st_var[:, 0:1], st_var[:, 0:1], st_sq[:, 0:1]), ve)
        P.o("v", lambda v: v.tensor_sub(
            st_d1[:, 0:1], st_d1[:, 0:1], st_m[:, 0:1]), ve)
        P.o("v", lambda v: v.scalar_tensor_tensor(
            st_d2[:, 0:1], st_m[:, 0:1], 1.0, st_d2[:, 0:1],
            op0=ALU.bypass, op1=ALU.subtract), ve)
        c5 = P.o("v", lambda v: v.tensor_max(
            st_d1[:, 0:1], st_d1[:, 0:1], st_d2[:, 0:1]), ve)
        P.w("s", ve, c5)
        P.o("s", lambda s, e=eps_bl: s.activation(
            st_rstd[:, 0:1], st_var[:, 0:1], AF.Ln,
            bias=e[:, 0:1], scale=-1.0), ac)
        P.o("s", lambda s: s.activation(
            st_rstd[:, 0:1], st_rstd[:, 0:1], AF.Exp, scale=-0.5), ac)
        c6 = P.o("s", lambda s: s.activation(
            st_gp[:, 0:1], st_d1[:, 0:1], AF.Identity, scale=CLIP_C), ac)
        P.w("v", ac, c6)
        P.o("v", lambda v: v.tensor_sub(
            st_lo[:, 0:1], st_m[:, 0:1], st_gp[:, 0:1]), ve)
        P.o("v", lambda v: v.tensor_add(
            st_hi[:, 0:1], st_m[:, 0:1], st_gp[:, 0:1]), ve)
        P.o("v", lambda v: v.tensor_scalar(
            st_gn[:, 0:1], st_m[:, 0:1], -1.0, None, op0=ALU.mult), ve)
        c_cl1 = P.o("v", lambda v, t=tc: v.tensor_scalar(
            hn_st[:, :], htens[:, t, :], st_lo[:, 0:1], st_hi[:, 0:1],
            op0=ALU.max, op1=ALU.min), ve)
        if tc >= 1:
            P.w("v", q_hh, cp[("hhat_dma", tc - 1)])
        chh2 = P.o("v", lambda v: v.tensor_scalar(
            hh_st[:, :], hn_st[:, :], st_gn[:, 0:1], st_rstd[:, 0:1],
            op0=ALU.add, op1=ALU.mult), ve)
        P.w("sp", ve, chh2)
        chd = P.o("sp", lambda sp, t=tc: sp.dma_start(
            d_hhat.ap()[t, :, :], hh_st[:, :]), q_hh)
        cp[("hhat_dma", tc)] = chd
        P.w("sp", ve, c_cl1)
        P.w("sp", q_hh, chd)
        P.w("s", ve, c_cl1)
        P.w("s", q_hh, chd)
        for fc in range(64):
            eng = "sp" if fc % 2 == 0 else "s"
            qq_tr = q_tr if fc % 2 == 0 else q_tr2
            P.o(eng, lambda E, t=tc, f=fc: E.dma_start(
                htens[:, t, f * 128:(f + 1) * 128],
                d_hhat.ap()[t, :, f * 128:(f + 1) * 128],
                transpose=True), qq_tr)
    F15_DONE = q_tr.v
    F15_DONE2 = q_tr2.v

    P.replay(nc)
    ctx15.close()

    ctxf2 = ExitStack()
    sbf2 = lambda n, s, d: ctxf2.enter_context(nc.sbuf_tensor(n, s, d))
    ternp = sbf2("ternp", [128, 2, 64, 256], BF16)
    wch2 = sbf2("wch2", [128, 2, 512], F32)
    tsa2 = sbf2("tsa2", [128, 512], F32)
    pst = sbf2("pst", [128, 2, 512], F32)

    P = Prog(ve, ac)
    pst_hist = []
    for pg in range(OG // 2):
        for chh in range(64):
            og = 2 * pg + chh // 32
            c = chh % 32
            tslot = chh // 32
            gidx = pg * 64 + chh
            wslot = gidx % 2
            if gidx >= 2:
                P.w("sp", ve, cp[("ternp_ch", gidx - 2)])
            if pg >= 1 and chh == 0:
                P.w("sp", mm, cp[("proj_mm", pg - 1)])
            ldv = P.o("sp", lambda sp, o=og, cx=c, b=wslot: sp.dma_start(
                wch2[:, b, :].rearrange("p (a f) -> p a f", a=2),
                wproj.ap()[o, 2 * cx:2 * cx + 2, :, :].rearrange(
                    "fc p f -> p fc f")), q_wc[wslot])
            P.w("v", q_wc[wslot], ldv)
            P.o("v", lambda v, b=wslot: v.tensor_scalar(
                tsa2[:, :], wch2[:, b, :], tpos_p[:, 0:1], None,
                op0=ALU.is_ge), ve)
            cp[("ternp_ch", gidx)] = P.o(
                "v", lambda v, b=wslot, ts=tslot, cx=c:
                v.scalar_tensor_tensor(
                    ternp[:, ts, 2 * cx:2 * cx + 2, :].rearrange(
                        "p a f -> p (a f)"),
                    wch2[:, b, :], tneg_p[:, 0:1], tsa2[:, :],
                    op0=ALU.is_le, op1=ALU.subtract), ve)
        ternp_done = ve.v

        P.w("t", ve, ternp_done)
        if pg == 0:
            P.w("t", q_tr, F15_DONE)
            P.w("t", q_tr2, F15_DONE2)
        for tc in range(TC):
            bank = (pC, pD)[tc % 2]
            P.w("t", ve, cp.get(("pj_evac", id(bank))))
            for fc in range(64):
                P.o("t", lambda t, bk=bank, f=fc, tt=tc:
                    t.matmul(bk[:, 0:512],
                             htens[:, tt, f * 128:(f + 1) * 128],
                             ternp[:, :, f, :],
                             start=(f == 0), stop=(f == 63)),
                    mm if fc == 63 else None)
            P.w("v", mm, mm.v)
            if len(pst_hist) >= 2:
                P.w("v", pst_hist[-2][0], pst_hist[-2][1])
            cpe = P.o("v", lambda v, bk=bank, tt=tc: v.tensor_scalar(
                pst[:, tt % 2, :], bk[:, 0:512], gneg_p[:, 0:1], None,
                op0=ALU.mult), ve)
            cp[("pj_evac", id(bank))] = cpe
            P.w("sp", ve, cpe)
            pslot = tc % 2
            dv = P.o("sp", lambda sp, tt=tc, p=pg: sp.dma_start(
                d_proj.ap()[tt, :, p * 512:(p + 1) * 512],
                pst[:, tt % 2, :]), q_ps[pslot])
            pst_hist.append((q_ps[pslot], dv))
        cp[("proj_mm", pg)] = mm.v

    P.replay(nc)
    ctxf2.close()
    ctxh.close()

    ctxn = ExitStack()
    sbn = lambda n, s, d: ctxn.enter_context(nc.sbuf_tensor(n, s, d))
    fa = sbn("fa", [128, 2, D], F32)
    fb = sbn("fb", [128, 2, D], F32)
    fsq = sbn("fsq", [128, D], F32)
    fo = sbn("fo", [128, 2, D], F32)

    P = Prog(ve, ac)
    P.w("sp", q_ps[0], q_ps[0].v)
    P.w("sp", q_ps[1], q_ps[1].v)
    P.w("sp", sq, X1_SPILLED)
    for tc in range(TC):
        fslot = tc % 2
        if tc >= 2:
            P.w("sp", ac, cp[("fin_ln", tc - 2)])
        P.o("sp", lambda sp, t=tc: sp.dma_start(
            fa[:, t % 2, :], d_proj.ap()[t, :, :]), q_fn[fslot])
        ldv = P.o("sp", lambda sp, t=tc: sp.dma_start(
            fb[:, t % 2, :], d_x1.ap()[t, :, :]), q_fn[fslot])
        P.w("v", q_fn[fslot], ldv)
        cadd = P.o("v", lambda v, t=tc: v.tensor_add(
            fa[:, t % 2, :], fa[:, t % 2, :], fb[:, t % 2, :]), ve)
        cmb = ln_stats(fa[:, tc % 2, :], eps_ln, fsq[:, :], ve, cadd)
        P.w("s", ve, cmb)
        if tc >= 2:
            P.w("s", q_st[tc % 2], cp[("out_dma", tc - 2)])
        cfo = P.o("s", lambda s, t=tc: s.activation(
            fo[:, t % 2, :], fa[:, t % 2, :], AF.Identity,
            bias=st_mb[:, 0:1], scale=st_rstd[:, 0:1]), ac)
        cp[("fin_ln", tc)] = cfo
        P.w("sp", ac, cfo)
        cp[("out_dma", tc)] = P.o("sp", lambda sp, t=tc: sp.dma_start(
            out.ap()[t, :, :], fo[:, t % 2, :]), q_st[tc % 2])
    P.w("sp", q_st[0], q_st[0].v)
    P.w("sp", q_st[1], q_st[1].v)

    P.replay(nc)
    ctxn.close()
    psum.close()
    ctx0.close()
    return nc


_NC_CACHE = None


def _get_nc():
    global _NC_CACHE
    if _NC_CACHE is None:
        _NC_CACHE = build_nc()
    return _NC_CACHE


def kernel(L0_attn_x, L0_attn_w_qkv, L0_attn_w_out, L0_ffn_w_fc, L0_ffn_w_proj):
    x = np.ascontiguousarray(np.asarray(L0_attn_x, dtype=np.float32))
    w_qkv = np.ascontiguousarray(np.asarray(L0_attn_w_qkv, dtype=np.float32))
    w_out = np.ascontiguousarray(np.asarray(L0_attn_w_out, dtype=np.float32))
    w_fc = np.ascontiguousarray(np.asarray(L0_ffn_w_fc, dtype=np.float32))
    w_proj = np.ascontiguousarray(np.asarray(L0_ffn_w_proj, dtype=np.float32))

    xt = np.ascontiguousarray(
        x.transpose(0, 2, 1).reshape(NB, DC, 128, 4, 512).transpose(0, 3, 2, 1, 4))
    wout_dev = np.ascontiguousarray(w_out.reshape(DC, 128, D))
    wfc_dev = np.ascontiguousarray(
        w_fc.T.reshape(DC, 128, FG, 512).transpose(2, 0, 1, 3))
    wproj_dev = np.ascontiguousarray(
        w_proj.T.reshape(64, 128, OG, 256).transpose(2, 0, 1, 3))
    tri_m = np.ascontiguousarray(
        (np.arange(128)[None, :] >= np.arange(128)[:, None]).astype(BF))

    in_maps = []
    for c in range(8):
        b, hf = divmod(c, 2)
        xo_c = np.ascontiguousarray(
            x[b, hf * 1024:(hf + 1) * 1024].reshape(TC, 128, D))
        heads = [2 * c, 2 * c + 1]
        wqk_c = np.ascontiguousarray(np.stack([
            np.concatenate([w_qkv[:, h * 128:(h + 1) * 128],
                            w_qkv[:, D + h * 128:D + (h + 1) * 128]], axis=1)
            for h in heads]).reshape(H2, DC, 128, 256))
        wv_c = np.ascontiguousarray(np.concatenate(
            [w_qkv[:, 2 * D + h * 128:2 * D + (h + 1) * 128] for h in heads],
            axis=1).reshape(DC, 128, 256))
        in_maps.append({
            "xt": xt, "xo": xo_c, "wqk": wqk_c, "wv": wv_c,
            "wout": wout_dev, "wfc": wfc_dev, "wproj": wproj_dev, "tri": tri_m,
        })

    nc = _get_nc()
    res = run_bass_kernel_spmd(nc, in_maps, core_ids=list(range(8)))

    y = np.empty((NB, S, D), dtype=np.float32)
    for c in range(8):
        b, hf = divmod(c, 2)
        y[b, hf * 1024:(hf + 1) * 1024] = res.results[c]["out"].reshape(1024, D)
    return y


# version 09
# speedup vs baseline: 1.0835x; 1.0835x over previous
import numpy as np
import ml_dtypes
from contextlib import ExitStack

import concourse.bass as bass
import concourse.mybir as mybir
from concourse.bass_utils import run_bass_kernel_spmd
from concourse.aot_env import allow_ici_all_to_all

F32 = mybir.dt.float32
BF16 = mybir.dt.bfloat16
AF = mybir.ActivationFunctionType
ALU = mybir.AluOpType
AX = mybir.AxisListType

BF = ml_dtypes.bfloat16

NB = 4
S = 2048
D = 2048
DC = 16
H2 = 2
TC = 8
DFF = 8192
FG = 16
OG = 8
INV_SQRT_HD = 0.08838834764831845
EPS_LN = 1e-5
EPS_BL = 1e-6
CLIP_C = (128.0 - 1e-6) / 128.0
NWELEM = float(DFF * D)


class Sem:
    __slots__ = ("h", "v", "name", "dma")

    def __init__(self, nc, name, dma=False):
        self.h = nc.alloc_semaphore(name)
        self.v = 0
        self.name = name
        self.dma = dma


class Prog:

    ENGS = ("t", "v", "s", "g", "sp")

    def __init__(self, ve_sem, ac_sem):
        self.ops = {e: [] for e in self.ENGS}
        self.ve = ve_sem
        self.ac = ac_sem

    def w(self, e, sem, v):
        if v is None or v <= 0:
            return
        self.ops[e].append(lambda E, s=sem, vv=v: E.wait_ge(s.h, vv))

    def o(self, e, build, sem=None, n=1):
        if e == "v" and not (sem is not None and sem.dma):
            if self.ve.v > 0:
                self.w("v", self.ve, self.ve.v)
            if sem is None:
                sem = self.ve
            assert sem is self.ve
        elif e == "s" and not (sem is not None and sem.dma):
            if self.ac.v > 0:
                self.w("s", self.ac, self.ac.v)
            if sem is None:
                sem = self.ac
            assert sem is self.ac
        if sem is None:
            self.ops[e].append(build)
            return None
        if sem.dma:
            n = 16
        sem.v += n
        self.ops[e].append(lambda E, b=build, s=sem, nn=n: b(E).then_inc(s.h, nn))
        return sem.v

    def replay(self, nc):
        with nc.Block() as blk:
            for name, attr in (("t", "tensor"), ("v", "vector"), ("s", "scalar"),
                               ("g", "gpsimd"), ("sp", "sync")):
                if self.ops[name]:
                    fns = self.ops[name]

                    def body(E, fns=fns):
                        for f in fns:
                            f(E)
                    getattr(blk, attr)(body)


def build_nc():
    nc = bass.Bass("TRN2", target_bir_lowering=False)

    xt = nc.dram_tensor("xt", [NB, 4, 128, DC, 512], F32, kind="ExternalInput")
    xo = nc.dram_tensor("xo", [TC, 128, D], F32, kind="ExternalInput")
    wqk = nc.dram_tensor("wqk", [H2, DC, 128, 256], F32, kind="ExternalInput")
    wv = nc.dram_tensor("wv", [DC, 128, 256], F32, kind="ExternalInput")
    wout = nc.dram_tensor("wout", [DC, 128, D], F32, kind="ExternalInput")
    wfc = nc.dram_tensor("wfc", [FG, DC, 128, 512], F32, kind="ExternalInput")
    wproj = nc.dram_tensor("wproj", [OG, 64, 128, 256], F32, kind="ExternalInput")
    tri = nc.dram_tensor("tri", [128, 128], BF16, kind="ExternalInput")
    out = nc.dram_tensor("out", [TC, 128, D], F32, kind="ExternalOutput")

    d_a2a_in = nc.dram_tensor("d_a2a_in", [8, H2, 128, 1024], BF16)
    d_a2a_out = nc.dram_tensor("d_a2a_out", [8, H2, 128, 1024], BF16)
    d_xhat = nc.dram_tensor("d_xhat", [1024, D], BF16)
    d_hhat = nc.dram_tensor("d_hhat", [TC, 128, DFF], BF16)
    d_x1 = nc.dram_tensor("d_x1", [TC, 128, D], F32)
    d_proj = nc.dram_tensor("d_proj", [TC, 128, D], F32)

    mm = Sem(nc, "mm")
    ve = Sem(nc, "ve")
    ac = Sem(nc, "ac")
    cc = Sem(nc, "cc")
    po = Sem(nc, "po")
    sq = Sem(nc, "sq", dma=True)
    q_wq = Sem(nc, "q_wq", dma=True)
    q_tri = Sem(nc, "q_tri", dma=True)
    q_xq = [Sem(nc, "q_xq0", dma=True), Sem(nc, "q_xq1", dma=True)]
    q_gb = [Sem(nc, "q_gb0", dma=True), Sem(nc, "q_gb1", dma=True)]
    q_av = [Sem(nc, "q_av0", dma=True), Sem(nc, "q_av1", dma=True)]
    q_lda = Sem(nc, "q_lda", dma=True)
    q_ldx = Sem(nc, "q_ldx", dma=True)
    q_wo = [Sem(nc, "q_wo0", dma=True), Sem(nc, "q_wo1", dma=True)]
    q_xh = [Sem(nc, "q_xh0", dma=True), Sem(nc, "q_xh1", dma=True)]
    q_tr = Sem(nc, "q_tr", dma=True)
    q_wc = [Sem(nc, "q_wc0", dma=True), Sem(nc, "q_wc1", dma=True)]
    q_hh = Sem(nc, "q_hh", dma=True)
    q_ps = [Sem(nc, "q_ps0", dma=True), Sem(nc, "q_ps1", dma=True)]
    q_fn = [Sem(nc, "q_fn0", dma=True), Sem(nc, "q_fn1", dma=True)]
    q_st = [Sem(nc, "q_st0", dma=True), Sem(nc, "q_st1", dma=True)]

    ctx0 = ExitStack()
    sb = lambda n, s, d: ctx0.enter_context(nc.sbuf_tensor(n, s, d))

    ones_b = sb("ones_b", [128, 128], BF16)
    onesf_col = sb("onesf_col", [128, 1], F32)
    onesf_row = sb("onesf_row", [1, 128], F32)
    tri_s = sb("tri_s", [128, 128], BF16)
    eps_ln = sb("eps_ln", [128, 1], F32)
    eps_bl = sb("eps_bl", [128, 1], F32)
    gacc_f = sb("gacc_f", [128, 1], F32)
    gacc_p = sb("gacc_p", [128, 1], F32)
    gpart = sb("gpart", [128, 2], F32)
    tpos_f = sb("tpos_f", [128, 1], F32)
    tneg_f = sb("tneg_f", [128, 1], F32)
    gneg_f = sb("gneg_f", [128, 1], F32)
    tpos_p = sb("tpos_p", [128, 1], F32)
    tneg_p = sb("tneg_p", [128, 1], F32)
    gneg_p = sb("gneg_p", [128, 1], F32)
    sc11 = sb("sc11", [1, 1], F32)
    st_m = sb("st_m", [128, 1], F32)
    st_sq = sb("st_sq", [128, 1], F32)
    st_var = sb("st_var", [128, 1], F32)
    st_rstd = sb("st_rstd", [128, 1], F32)
    st_mb = sb("st_mb", [128, 1], F32)
    st_g = sb("st_g", [128, 1], F32)
    st_gp = sb("st_gp", [128, 1], F32)
    st_gn = sb("st_gn", [128, 1], F32)
    st_lo = sb("st_lo", [128, 1], F32)
    st_hi = sb("st_hi", [128, 1], F32)
    st_d1 = sb("st_d1", [128, 1], F32)
    st_d2 = sb("st_d2", [128, 1], F32)
    st8_m = sb("st8_m", [128, TC], F32)
    st8_sq = sb("st8_sq", [128, TC], F32)
    st8_var = sb("st8_var", [128, TC], F32)
    st8_d1 = sb("st8_d1", [128, TC], F32)
    st8_d2 = sb("st8_d2", [128, TC], F32)
    st8_r = sb("st8_r", [128, TC], F32)
    st8_c = sb("st8_c", [128, TC], F32)
    st8_lo = sb("st8_lo", [128, TC], F32)
    st8_hi = sb("st8_hi", [128, TC], F32)
    st8_nm = sb("st8_nm", [128, TC], F32)

    psum = ExitStack()
    pbig = psum.enter_context(nc.psum_tensor("pbig", [128, 2048], F32))
    pA = psum.enter_context(nc.psum_tensor("pA", [128, 512], F32))
    pB = psum.enter_context(nc.psum_tensor("pB", [128, 512], F32))
    pC = psum.enter_context(nc.psum_tensor("pC", [128, 512], F32))
    pD = psum.enter_context(nc.psum_tensor("pD", [128, 512], F32))

    cp = {}
    P = None

    def ln_stats(src_ap, eps_tile, dump_ap, rsem, rval):
        n = src_ap.shape[-1]
        P.w("v", rsem, rval)
        P.o("v", lambda v, a=src_ap: v.reduce_sum(st_m[:, 0:1], a, AX.X), ve)
        c2 = P.o("v", lambda v, a=src_ap, d=dump_ap: v.scalar_tensor_tensor(
            d, a, 1.0, a, op0=ALU.bypass, op1=ALU.mult,
            accum_out=st_sq[:, 0:1]), ve)
        P.w("s", ve, c2)
        P.o("s", lambda s, nn=n: s.activation(
            st_m[:, 0:1], st_m[:, 0:1], AF.Identity, scale=1.0 / nn), ac)
        P.o("s", lambda s, nn=n: s.activation(
            st_sq[:, 0:1], st_sq[:, 0:1], AF.Identity, scale=1.0 / nn), ac)
        c4 = P.o("s", lambda s: s.activation(
            st_var[:, 0:1], st_m[:, 0:1], AF.Square), ac)
        P.w("v", ac, c4)
        c5 = P.o("v", lambda v: v.tensor_sub(
            st_var[:, 0:1], st_var[:, 0:1], st_sq[:, 0:1]), ve)
        P.w("s", ve, c5)
        P.o("s", lambda s, e=eps_tile: s.activation(
            st_rstd[:, 0:1], st_var[:, 0:1], AF.Ln,
            bias=e[:, 0:1], scale=-1.0), ac)
        crs = P.o("s", lambda s: s.activation(
            st_rstd[:, 0:1], st_rstd[:, 0:1], AF.Exp, scale=-0.5), ac)
        P.w("v", ac, crs)
        cmb = P.o("v", lambda v: v.tensor_scalar(
            st_mb[:, 0:1], st_m[:, 0:1], st_rstd[:, 0:1], -1.0,
            op0=ALU.mult, op1=ALU.mult), ve)
        return cmb

    def gamma_chunk(gacc, gbuf_t, gdump_t, src_ap, ch):
        slot = ch % 2
        if ch >= 2:
            P.w("sp", ac, cp[("gst", id(gacc), ch - 2)])
        ldv = P.o("sp", lambda sp, a=src_ap, b=slot, g=gbuf_t:
                  sp.dma_start(g[:, b, :].rearrange(
                      "p (a f) -> p a f", a=a.shape[1]), a), q_gb[slot])
        P.w("s", q_gb[slot], ldv)
        if ch >= 2:
            P.w("s", ve, cp[("gadd", id(gacc), ch - 2)])
        cp[("gst", id(gacc), ch)] = P.o(
            "s", lambda s, b=slot, g=gbuf_t, d=gdump_t, ps=slot:
            s.activation(d[:, :], g[:, b, :], AF.Abs,
                         accum_out=gpart[:, ps:ps + 1]), ac)
        P.w("v", ac, cp[("gst", id(gacc), ch)])
        cp[("gadd", id(gacc), ch)] = P.o(
            "v", lambda v, ps=slot, ga=gacc:
            v.tensor_add(ga[:, 0:1], ga[:, 0:1], gpart[:, ps:ps + 1]), ve)
        cp[("gsum", id(gacc))] = ve.v

    def gamma_final(gacc, tpos, tneg, gneg):
        P.w("t", ve, cp[("gsum", id(gacc))])
        P.o("t", lambda t, ga=gacc: t.matmul(
            pD[0:1, 0:1], ga[:, 0:1], onesf_col[:, 0:1],
            start=True, stop=True), mm)
        s1 = mm.v
        P.w("s", mm, s1)
        s2 = P.o("s", lambda s: s.copy(sc11[:, :], pD[0:1, 0:1]), ac)
        P.w("t", ac, s2)
        P.o("t", lambda t: t.matmul(pD[:, 1:2], onesf_row[:, :], sc11[:, :],
                                    start=True, stop=True), mm)
        s3 = mm.v
        P.w("v", mm, s3)
        P.o("v", lambda v, tp=tpos: v.tensor_scalar(
            tp[:, 0:1], pD[:, 1:2], 0.5 / NWELEM, 0.5 * EPS_BL,
            op0=ALU.mult, op1=ALU.add), ve)
        P.o("v", lambda v, gn=gneg: v.tensor_scalar(
            gn[:, 0:1], pD[:, 1:2], -1.0 / NWELEM, None, op0=ALU.mult), ve)
        cfin = P.o("v", lambda v, tp=tpos, tn=tneg: v.tensor_scalar(
            tn[:, 0:1], tp[:, 0:1], -1.0, None, op0=ALU.mult), ve)
        cp[("gfin", id(gacc))] = cfin
        return cfin

    ctxq = ExitStack()
    sbq = lambda n, s, d: ctxq.enter_context(nc.sbuf_tensor(n, s, d))
    xq = sbq("xq", [128, 2, DC, 512], BF16)
    wqk_s = sbq("wqk_s", [128, H2, DC, 256], BF16)
    wv_s = sbq("wv_s", [128, DC, 256], BF16)
    kT_s = sbq("kT_s", [128, H2, S], BF16)
    qT_s = sbq("qT_s", [128, H2, S], BF16)
    v_s = sbq("v_s", [128, 16, 256], BF16)
    pt = sbq("pt", [128, 4, 512], BF16)
    lln = sbq("lln", [128, 512], F32)
    lrec = sbq("lrec", [128, 512], F32)
    avst = sbq("avst", [128, 2, 512], BF16)
    gbuf = sbq("gbuf", [128, 2, 2048], F32)
    gdump = sbq("gdump", [128, 2048], F32)

    P = Prog(ve, ac)

    for f in (lambda g: g.memset(ones_b[:, :], 1.0),
              lambda g: g.memset(onesf_col[:, :], 1.0),
              lambda g: g.memset(onesf_row[:, :], 1.0),
              lambda g: g.memset(eps_ln[:, :], EPS_LN),
              lambda g: g.memset(eps_bl[:, :], EPS_BL),
              lambda g: g.memset(gacc_f[:, :], 0.0)):
        P.o("g", f)
    CONST_READY = P.o("g", lambda g: g.memset(gacc_p[:, :], 0.0), cc)
    P.o("g", lambda g: g.dma_start(
        wqk_s[:, :, :, :], wqk.ap().rearrange("h dc p f -> p h dc f")), q_wq)
    WQKV_LD = P.o("g", lambda g: g.dma_start(
        wv_s[:, :, :], wv.ap().rearrange("dc p f -> p dc f")), q_wq)
    TRI_LD = P.o("sp", lambda sp: sp.dma_start(tri_s[:, :], tri.ap()), q_tri)
    for e in ("t", "v", "s"):
        P.w(e, cc, CONST_READY)
    P.w("v", q_tri, TRI_LD)

    pair_hist = []
    avdma_hist = []
    qkv_ping = [0]

    NGCH = 64

    def wfc_chunk_ap(ch):
        fg, pr = divmod(ch, 4)
        return wfc.ap()[fg, 4 * pr:4 * pr + 4, :, :].rearrange(
            "dc p f -> p dc f")

    gamma_f_pos = [0]

    def emit_gamma_f(n):
        for _ in range(n):
            ch = gamma_f_pos[0]
            if ch >= NGCH:
                return
            gamma_chunk(gacc_f, gbuf, gdump, wfc_chunk_ap(ch), ch)
            gamma_f_pos[0] += 1

    def q_batch(b):
        for i in range(4):
            gidx = b * 4 + i
            slot = gidx % 2
            if gidx >= 2:
                P.w("g", mm, cp[("xq_done", gidx - 2)])
            ldv = P.o("g", lambda g, bb=b, ii=i: g.dma_start(
                xq[:, ii % 2, :, :], xt.ap()[bb, ii, :, :, :]), q_xq[slot])
            P.w("t", q_xq[slot], ldv)
            if gidx == 0:
                P.w("t", q_wq, WQKV_LD)
            for h2 in range(H2):
                for qk in range(2):
                    dst = (qT_s, kT_s)[qk]
                    bank = (pC, pD)[qkv_ping[0] % 2]
                    qkv_ping[0] += 1
                    P.w("t", ve, cp.get(("evac", id(bank))))
                    for dc in range(DC):
                        P.o("t", lambda t, bk=bank, h=h2, q=qk, d=dc, ii=i:
                            t.matmul(bk[:, 0:512],
                                     wqk_s[:, h, d, q * 128:(q + 1) * 128],
                                     xq[:, ii % 2, d, :],
                                     start=(d == 0), stop=(d == DC - 1)),
                            mm if dc == DC - 1 else None)
                    P.w("v", mm, mm.v)
                    cp[("evac", id(bank))] = P.o(
                        "v", lambda v, bk=bank, ds=dst, h=h2, ii=i:
                        v.tensor_copy(ds[:, h, ii * 512:(ii + 1) * 512],
                                      bk[:, 0:512]), ve)
            for t128 in range(4):
                bank = (pC, pD)[qkv_ping[0] % 2]
                qkv_ping[0] += 1
                P.w("t", ve, cp.get(("evac", id(bank))))
                for dc in range(DC):
                    P.o("t", lambda t, bk=bank, d=dc, ii=i, tk=t128:
                        t.matmul(bk[:, 0:256],
                                 xq[:, ii % 2, d, tk * 128:(tk + 1) * 128],
                                 wv_s[:, d, :],
                                 start=(d == 0), stop=(d == DC - 1)),
                        mm if dc == DC - 1 else None)
                P.w("v", mm, mm.v)
                cp[("evac", id(bank))] = P.o(
                    "v", lambda v, bk=bank, kb=i * 4 + t128:
                    v.tensor_copy(v_s[:, kb, :], bk[:, 0:256]), ve)
            cp[("xq_done", gidx)] = mm.v
        qkv_done = ve.v

        for h2 in range(H2):
            P.w("t", ve, qkv_done)
            for s4 in range(4):
                kbN = 4 * (s4 + 1)
                for pr in range(kbN // 2):
                    k0, k1 = 2 * pr, 2 * pr + 1
                    if len(pair_hist) >= 2:
                        P.w("t", ac, pair_hist[-2][0])
                    for kb in (k0, k1):
                        qq = kb - 4 * s4
                        sfx = 128 * qq if qq >= 0 else 0
                        P.o("t", lambda t, h=h2, s=s4, kkb=kb, sf=sfx:
                            t.matmul(pbig[:, (kkb % 4) * 512 + sf:
                                             (kkb % 4) * 512 + 512],
                                     kT_s[:, h, kkb * 128:(kkb + 1) * 128],
                                     qT_s[:, h, s * 512 + sf:(s + 1) * 512],
                                     start=True, stop=True),
                            mm if kb == k1 else None)
                    cps = mm.v
                    P.w("s", mm, cps)
                    if len(pair_hist) >= 2:
                        P.w("s", mm, pair_hist[-2][1])
                    cpe = P.o("s", lambda s, kk=k0: s.activation(
                        pt[:, kk % 4:kk % 4 + 2, :],
                        pbig[:, (kk % 4) * 512:(kk % 4) * 512 + 1024],
                        AF.Exp, scale=INV_SQRT_HD), ac)
                    cpt = None
                    for kb in (k0, k1):
                        qq = kb - 4 * s4
                        if 0 <= qq <= 3:
                            P.w("v", ac, cpe)
                            cpt = P.o("v", lambda v, kkb=kb, q=qq:
                                      v.tensor_mul(
                                          pt[:, kkb % 4, q * 128:(q + 1) * 128],
                                          pt[:, kkb % 4, q * 128:(q + 1) * 128],
                                          tri_s[:, :]), ve)
                    P.w("t", ac, cpe)
                    if cpt is not None:
                        P.w("t", ve, cpt)
                    if pr == 0:
                        P.w("t", ve, cp.get("avl_used"))
                        P.w("t", ac, cp.get("l_used"))
                    for kb in (k0, k1):
                        qq = kb - 4 * s4
                        sfx = 128 * qq if qq >= 0 else 0
                        fst, lst = (kb == 0), (kb == kbN - 1)
                        P.o("t", lambda t, kkb=kb, h=h2, sf=sfx, f=fst, l=lst:
                            t.matmul(pA[:, sf:512],
                                     v_s[:, kkb, h * 128:(h + 1) * 128],
                                     pt[:, kkb % 4, sf:512],
                                     start=f, stop=l), None)
                        P.o("t", lambda t, kkb=kb, sf=sfx, f=fst, l=lst:
                            t.matmul(pB[:, sf:512], ones_b[:, :],
                                     pt[:, kkb % 4, sf:512],
                                     start=f, stop=l),
                            mm if kb == k1 else None)
                    pair_hist.append((cpe, mm.v))
                P.w("s", mm, pair_hist[-1][1])
                cp["l_used"] = P.o("s", lambda s: s.activation(
                    lln[:, :], pB[:, 0:512], AF.Ln), ac)
                clr = P.o("s", lambda s: s.activation(
                    lrec[:, :], lln[:, :], AF.Exp, scale=-1.0), ac)
                P.w("v", ac, clr)
                if len(avdma_hist) >= 2:
                    P.w("v", avdma_hist[-2][0], avdma_hist[-2][1])
                cp["avl_used"] = P.o("v", lambda v, s=s4: v.tensor_mul(
                    avst[:, s % 2, :], pA[:, 0:512], lrec[:, :]), ve)
                P.w("sp", ve, cp["avl_used"])
                j = 2 * b + (s4 // 2)
                off = (s4 % 2) * 512
                aslot = len(avdma_hist) % 2
                dv = P.o("sp", lambda sp, jj=j, h=h2, of=off, s=s4:
                         sp.dma_start(d_a2a_in.ap()[jj, h, :, of:of + 512],
                                      avst[:, s % 2, :]), q_av[aslot])
                avdma_hist.append((q_av[aslot], dv))

    for b in range(NB):
        q_batch(b)
        emit_gamma_f(16)
    emit_gamma_f(NGCH)
    gamma_final(gacc_f, tpos_f, tneg_f, gneg_f)

    P.replay(nc)
    ctxq.close()

    ctxw = ExitStack()
    sbw = lambda n, s, d: ctxw.enter_context(nc.sbuf_tensor(n, s, d))
    avT_all = sbw("avT_all", [128, 16, 1024], BF16)
    stream = sbw("stream", [128, TC, D], F32)
    woutb = sbw("woutb", [128, 2, DC, 512], BF16)
    xn_st = sbw("xn_st", [128, D], F32)
    xh_st = sbw("xh_st", [128, 2, D], BF16)
    gbuf2 = sbw("gbuf2", [128, 2, 2048], F32)
    gdump2 = sbw("gdump2", [128, 2048], F32)

    P = Prog(ve, ac)

    P.w("g", q_av[0], q_av[0].v)
    P.w("g", q_av[1], q_av[1].v)

    def _cc_call(g):
        with allow_ici_all_to_all():
            return g.collective_compute(
                "AllToAll", ALU.bypass, replica_groups=[list(range(8))],
                ins=[d_a2a_in.ap().rearrange("a b c d -> a (b c d)")],
                outs=[d_a2a_out.ap().rearrange("a b c d -> a (b c d)")])

    CC_DONE = P.o("g", _cc_call, cc)

    def wproj_chunk_ap(ch):
        og, fr = divmod(ch, 8)
        return wproj.ap()[og, 8 * fr:8 * fr + 8, :, :].rearrange(
            "fc p f -> p fc f")

    for ch in range(NGCH):
        gamma_chunk(gacc_p, gbuf2, gdump2, wproj_chunk_ap(ch), ch)
    gamma_final(gacc_p, tpos_p, tneg_p, gneg_p)
    cp[("evac", id(pD))] = cp[("gfin", id(gacc_p))]
    cp[("evac", id(pC))] = None

    P.w("sp", cc, CC_DONE)
    for j in range(8):
        for h2 in range(H2):
            AVT_LD = P.o("sp", lambda sp, jj=j, h=h2: sp.dma_start(
                avT_all[:, jj * 2 + h, :], d_a2a_out.ap()[jj, h, :, :]), q_lda)
    for tc in range(TC):
        XO_LD = P.o("sp", lambda sp, t=tc: sp.dma_start(
            stream[:, t, :], xo.ap()[t, :, :]), q_ldx)

    for og in range(4):
        slot = og % 2
        if og >= 2:
            P.w("g", mm, cp[("wout_mm", og - 2)])
        ldv = P.o("g", lambda g, o=og: g.dma_start(
            woutb[:, o % 2, :, :],
            wout.ap()[:, :, o * 512:(o + 1) * 512].rearrange(
                "dvc p f -> p dvc f")), q_wo[slot])
        P.w("t", q_wo[slot], ldv)
        if og == 0:
            P.w("t", q_lda, AVT_LD)
        for tc in range(TC):
            bank = (pC, pD)[tc % 2]
            P.w("t", ve, cp.get(("evac", id(bank))))
            for dvc in range(DC):
                P.o("t", lambda t, bk=bank, d=dvc, tt=tc, o=og:
                    t.matmul(bk[:, 0:512],
                             avT_all[:, d, tt * 128:(tt + 1) * 128],
                             woutb[:, o % 2, d, :],
                             start=(d == 0), stop=(d == DC - 1)),
                    mm if dvc == DC - 1 else None)
            P.w("v", mm, mm.v)
            if og == 0 and tc == 0:
                P.w("v", q_ldx, XO_LD)
            cp[("evac", id(bank))] = P.o(
                "v", lambda v, bk=bank, tt=tc, o=og:
                v.tensor_add(stream[:, tt, o * 512:(o + 1) * 512],
                             bk[:, 0:512],
                             stream[:, tt, o * 512:(o + 1) * 512]), ve)
        cp[("wout_mm", og)] = mm.v
    RESID1 = ve.v

    cxh_prev = None
    for tc in range(TC):
        if cxh_prev is not None:
            P.w("s", ve, cxh_prev)
        cmb = ln_stats(stream[:, tc, :], eps_ln, xn_st[:, :], ve, RESID1)
        P.w("s", ve, cmb)
        cx1 = P.o("s", lambda s, t=tc: s.activation(
            stream[:, t, :], stream[:, t, :], AF.Identity,
            bias=st_mb[:, 0:1], scale=st_rstd[:, 0:1]), ac)
        cmb2 = ln_stats(stream[:, tc, :], eps_bl, xn_st[:, :], ac, cx1)
        P.w("s", ve, cmb2)
        cxn = P.o("s", lambda s, t=tc: s.activation(
            xn_st[:, :], stream[:, t, :], AF.Identity,
            bias=st_mb[:, 0:1], scale=st_rstd[:, 0:1]), ac)
        P.w("sp", ac, cxn)
        P.o("sp", lambda sp, t=tc: sp.dma_start(
            d_x1.ap()[t, :, :], stream[:, t, :]), sq)
        P.w("v", ac, cxn)
        cg1 = P.o("v", lambda v: v.reduce_max(st_g[:, 0:1], xn_st[:, :], AX.X,
                                              apply_absolute_value=True), ve)
        P.w("s", ve, cg1)
        P.o("s", lambda s: s.activation(st_gp[:, 0:1], st_g[:, 0:1],
                                        AF.Identity, scale=CLIP_C), ac)
        cg2 = P.o("s", lambda s: s.activation(st_gn[:, 0:1], st_g[:, 0:1],
                                              AF.Identity, scale=-CLIP_C), ac)
        P.w("v", ac, cg2)
        if tc >= 2:
            P.w("v", q_xh[tc % 2], cp[("xhat_dma", tc - 2)])
        cxh = P.o("v", lambda v, t=tc: v.tensor_scalar(
            xh_st[:, t % 2, :], xn_st[:, :], st_gn[:, 0:1], st_gp[:, 0:1],
            op0=ALU.max, op1=ALU.min), ve)
        cxh_prev = cxh
        P.w("sp", ve, cxh)
        cp[("xhat_dma", tc)] = P.o("sp", lambda sp, t=tc: sp.dma_start(
            d_xhat.ap()[t * 128:(t + 1) * 128, :], xh_st[:, t % 2, :]),
            q_xh[tc % 2])
    X1_SPILLED = sq.v

    P.replay(nc)
    ctxw.close()

    ctxh = ExitStack()
    htens = ctxh.enter_context(nc.sbuf_tensor("htens", [128, TC, DFF], BF16))
    hsump = ctxh.enter_context(nc.sbuf_tensor("hsump", [128, TC, FG], F32))
    hsqp = ctxh.enter_context(nc.sbuf_tensor("hsqp", [128, TC, FG], F32))
    hmaxp = ctxh.enter_context(nc.sbuf_tensor("hmaxp", [128, TC, FG], BF16))
    hminp = ctxh.enter_context(nc.sbuf_tensor("hminp", [128, TC, FG], BF16))

    ctxf1 = ExitStack()
    sbf1 = lambda n, s, d: ctxf1.enter_context(nc.sbuf_tensor(n, s, d))
    xhatT = sbf1("xhatT", [128, DC, 1024], BF16)
    ternf = sbf1("ternf", [128, 2, DC, 512], BF16)
    wch = sbf1("wch", [128, 2, 1024], F32)
    tsa_t = sbf1("tsa_t", [128, 1024], F32)
    tsa = tsa_t[:, :]

    P = Prog(ve, ac)
    P.w("sp", q_xh[0], q_xh[0].v)
    P.w("sp", q_xh[1], q_xh[1].v)
    for dc in range(DC):
        XT_LD = P.o("sp", lambda sp, d=dc: sp.dma_start(
            xhatT[:, d, :], d_xhat.ap()[:, d * 128:(d + 1) * 128],
            transpose=True), q_tr)

    for fg in range(FG):
        for chh in range(8):
            gidx = fg * 8 + chh
            slot = gidx % 2
            if gidx >= 2:
                P.w("sp", ve, cp[("ternf_ch", gidx - 2)])
            ldv = P.o("sp", lambda sp, f=fg, c=chh, b=slot: sp.dma_start(
                wch[:, b, :].rearrange("p (a f) -> p a f", a=2),
                wfc.ap()[f, 2 * c:2 * c + 2, :, :].rearrange(
                    "dc p f -> p dc f")), q_wc[slot])
            P.w("v", q_wc[slot], ldv)
            if fg >= 2 and chh == 0:
                P.w("v", mm, cp[("fc_mm", fg - 2)])
            P.o("v", lambda v, b=slot: v.tensor_scalar(
                tsa, wch[:, b, :], tpos_f[:, 0:1], None,
                op0=ALU.is_ge), ve)
            cp[("ternf_ch", gidx)] = P.o(
                "v", lambda v, b=slot, f=fg, c=chh:
                v.scalar_tensor_tensor(
                    ternf[:, f % 2, 2 * c:2 * c + 2, :].rearrange(
                        "p a f -> p (a f)"),
                    wch[:, b, :], tneg_f[:, 0:1], tsa,
                    op0=ALU.is_le, op1=ALU.subtract), ve)
        tern_done = ve.v

        P.w("t", ve, tern_done)
        if fg == 0:
            P.w("t", q_tr, XT_LD)
        for tc in range(TC):
            bank = (pC, pD)[tc % 2]
            P.w("t", ac, cp.get(("fc_evac", id(bank))))
            for dc in range(DC):
                P.o("t", lambda t, bk=bank, d=dc, tt=tc, f=fg:
                    t.matmul(bk[:, 0:512],
                             xhatT[:, d, tt * 128:(tt + 1) * 128],
                             ternf[:, f % 2, d, :],
                             start=(d == 0), stop=(d == DC - 1)),
                    mm if dc == DC - 1 else None)
            P.w("s", mm, mm.v)
            cp[("fc_evac", id(bank))] = P.o(
                "s", lambda s, bk=bank, tt=tc, f=fg: s.activation(
                    htens[:, tt, f * 512:(f + 1) * 512], bk[:, 0:512],
                    AF.Gelu, scale=gneg_f[:, 0:1],
                    accum_out=hsump[:, tt, f:f + 1]), ac)
            csq = P.o("s", lambda s, tt=tc, f=fg: s.activation(
                pbig[:, 0:512], htens[:, tt, f * 512:(f + 1) * 512],
                AF.Square, accum_out=hsqp[:, tt, f:f + 1]), ac)
            P.w("v", ac, csq)
            P.o("v", lambda v, tt=tc, f=fg: v.tensor_reduce(
                hmaxp[:, tt, f:f + 1], htens[:, tt, f * 512:(f + 1) * 512],
                AX.X, ALU.max), ve)
            P.o("v", lambda v, tt=tc, f=fg: v.tensor_reduce(
                hminp[:, tt, f:f + 1], htens[:, tt, f * 512:(f + 1) * 512],
                AX.X, ALU.min), ve)
        cp[("fc_mm", fg)] = mm.v
    F1_DONE_AC = ac.v
    F1_DONE_VE = ve.v

    P.replay(nc)
    ctxf1.close()

    ctx15 = ExitStack()
    hn_st = ctx15.enter_context(nc.sbuf_tensor("hn_st", [128, DFF], F32))
    hh_st = ctx15.enter_context(nc.sbuf_tensor("hh_st", [128, DFF], BF16))
    q_tr2 = Sem(nc, "q_tr2", dma=True)

    P = Prog(ve, ac)
    P.w("v", ac, F1_DONE_AC)
    for tc in range(TC):
        P.o("v", lambda v, t=tc: v.reduce_sum(
            st8_m[:, t:t + 1], hsump[:, t, :], AX.X), ve)
        P.o("v", lambda v, t=tc: v.reduce_sum(
            st8_sq[:, t:t + 1], hsqp[:, t, :], AX.X), ve)
        P.o("v", lambda v, t=tc: v.reduce_max(
            st8_d1[:, t:t + 1], hmaxp[:, t, :], AX.X), ve)
        P.o("v", lambda v, t=tc: v.tensor_reduce(
            st8_d2[:, t:t + 1], hminp[:, t, :], AX.X, ALU.min), ve)
    c2 = ve.v
    P.w("s", ve, c2)
    P.o("s", lambda s: s.activation(
        st8_m[:, :], st8_m[:, :], AF.Identity, scale=1.0 / DFF), ac)
    P.o("s", lambda s: s.activation(
        st8_sq[:, :], st8_sq[:, :], AF.Identity, scale=1.0 / DFF), ac)
    c4 = P.o("s", lambda s: s.activation(
        st8_var[:, :], st8_m[:, :], AF.Square), ac)
    P.w("v", ac, c4)
    P.o("v", lambda v: v.tensor_sub(st8_var[:, :], st8_var[:, :],
                                    st8_sq[:, :]), ve)
    P.o("v", lambda v: v.tensor_sub(st8_d1[:, :], st8_d1[:, :],
                                    st8_m[:, :]), ve)
    P.o("v", lambda v: v.scalar_tensor_tensor(
        st8_d2[:, :], st8_m[:, :], 1.0, st8_d2[:, :],
        op0=ALU.bypass, op1=ALU.subtract), ve)
    c5 = P.o("v", lambda v: v.tensor_max(st8_d1[:, :], st8_d1[:, :],
                                         st8_d2[:, :]), ve)
    P.w("s", ve, c5)
    P.o("s", lambda s: s.activation(
        st8_r[:, :], st8_var[:, :], AF.Ln,
        bias=eps_bl[:, 0:1], scale=-1.0), ac)
    P.o("s", lambda s: s.activation(
        st8_r[:, :], st8_r[:, :], AF.Exp, scale=-0.5), ac)
    c6 = P.o("s", lambda s: s.activation(
        st8_c[:, :], st8_d1[:, :], AF.Identity, scale=CLIP_C), ac)
    P.w("v", ac, c6)
    P.o("v", lambda v: v.tensor_sub(st8_lo[:, :], st8_m[:, :],
                                    st8_c[:, :]), ve)
    P.o("v", lambda v: v.tensor_add(st8_hi[:, :], st8_m[:, :],
                                    st8_c[:, :]), ve)
    P.o("v", lambda v: v.tensor_scalar(
        st8_nm[:, :], st8_m[:, :], -1.0, None, op0=ALU.mult), ve)

    for tc in range(TC):
        c_cl1 = P.o("v", lambda v, t=tc: v.tensor_scalar(
            hn_st[:, :], htens[:, t, :], st8_lo[:, t:t + 1],
            st8_hi[:, t:t + 1], op0=ALU.max, op1=ALU.min), ve)
        if tc >= 1:
            P.w("v", q_hh, cp[("hhat_dma", tc - 1)])
        chh2 = P.o("v", lambda v, t=tc: v.tensor_scalar(
            hh_st[:, :], hn_st[:, :], st8_nm[:, t:t + 1],
            st8_r[:, t:t + 1], op0=ALU.add, op1=ALU.mult), ve)
        P.w("sp", ve, chh2)
        chd = P.o("sp", lambda sp, t=tc: sp.dma_start(
            d_hhat.ap()[t, :, :], hh_st[:, :]), q_hh)
        cp[("hhat_dma", tc)] = chd
        P.w("sp", ve, c_cl1)
        P.w("sp", q_hh, chd)
        P.w("s", ve, c_cl1)
        P.w("s", q_hh, chd)
        for fc in range(64):
            eng = "sp" if fc % 2 == 0 else "s"
            qq_tr = q_tr if fc % 2 == 0 else q_tr2
            P.o(eng, lambda E, t=tc, f=fc: E.dma_start(
                htens[:, t, f * 128:(f + 1) * 128],
                d_hhat.ap()[t, :, f * 128:(f + 1) * 128],
                transpose=True), qq_tr)
    F15_DONE = q_tr.v
    F15_DONE2 = q_tr2.v

    P.replay(nc)
    ctx15.close()

    ctxf2 = ExitStack()
    sbf2 = lambda n, s, d: ctxf2.enter_context(nc.sbuf_tensor(n, s, d))
    ternp = sbf2("ternp", [128, 2, 64, 256], BF16)
    wch2 = sbf2("wch2", [128, 2, 1024], F32)
    pst = sbf2("pst", [128, 2, 512], F32)
    tsa2 = pbig[:, 1024:2048]

    P = Prog(ve, ac)
    pst_hist = []
    for pg in range(OG // 2):
        for chh in range(32):
            og = 2 * pg + chh // 16
            c = chh % 16
            tslot = chh // 16
            gidx = pg * 32 + chh
            wslot = gidx % 2
            if gidx >= 2:
                P.w("sp", ve, cp[("ternp_ch", gidx - 2)])
            if pg >= 1 and chh == 0:
                P.w("sp", mm, cp[("proj_mm", pg - 1)])
            ldv = P.o("sp", lambda sp, o=og, cx=c, b=wslot: sp.dma_start(
                wch2[:, b, :].rearrange("p (a f) -> p a f", a=4),
                wproj.ap()[o, 4 * cx:4 * cx + 4, :, :].rearrange(
                    "fc p f -> p fc f")), q_wc[wslot])
            P.w("v", q_wc[wslot], ldv)
            P.o("v", lambda v, b=wslot: v.tensor_scalar(
                tsa2, wch2[:, b, :], tpos_p[:, 0:1], None,
                op0=ALU.is_ge), ve)
            cp[("ternp_ch", gidx)] = P.o(
                "v", lambda v, b=wslot, ts=tslot, cx=c:
                v.scalar_tensor_tensor(
                    ternp[:, ts, 4 * cx:4 * cx + 4, :].rearrange(
                        "p a f -> p (a f)"),
                    wch2[:, b, :], tneg_p[:, 0:1], tsa2,
                    op0=ALU.is_le, op1=ALU.subtract), ve)
        ternp_done = ve.v

        P.w("t", ve, ternp_done)
        if pg == 0:
            P.w("t", q_tr, F15_DONE)
            P.w("t", q_tr2, F15_DONE2)
        for tc in range(TC):
            bank = (pC, pD)[tc % 2]
            P.w("t", ve, cp.get(("pj_evac", id(bank))))
            for fc in range(64):
                P.o("t", lambda t, bk=bank, f=fc, tt=tc:
                    t.matmul(bk[:, 0:512],
                             htens[:, tt, f * 128:(f + 1) * 128],
                             ternp[:, :, f, :],
                             start=(f == 0), stop=(f == 63)),
                    mm if fc == 63 else None)
            P.w("v", mm, mm.v)
            if len(pst_hist) >= 2:
                P.w("v", pst_hist[-2][0], pst_hist[-2][1])
            cpe = P.o("v", lambda v, bk=bank, tt=tc: v.tensor_scalar(
                pst[:, tt % 2, :], bk[:, 0:512], gneg_p[:, 0:1], None,
                op0=ALU.mult), ve)
            cp[("pj_evac", id(bank))] = cpe
            P.w("sp", ve, cpe)
            pslot = tc % 2
            dv = P.o("sp", lambda sp, tt=tc, p=pg: sp.dma_start(
                d_proj.ap()[tt, :, p * 512:(p + 1) * 512],
                pst[:, tt % 2, :]), q_ps[pslot])
            pst_hist.append((q_ps[pslot], dv))
        cp[("proj_mm", pg)] = mm.v

    P.replay(nc)
    ctxf2.close()
    ctxh.close()

    ctxn = ExitStack()
    sbn = lambda n, s, d: ctxn.enter_context(nc.sbuf_tensor(n, s, d))
    fa = sbn("fa", [128, 2, D], F32)
    fb = sbn("fb", [128, 2, D], F32)
    fsq = sbn("fsq", [128, D], F32)
    fo = sbn("fo", [128, 2, D], F32)

    P = Prog(ve, ac)
    P.w("sp", q_ps[0], q_ps[0].v)
    P.w("sp", q_ps[1], q_ps[1].v)
    P.w("sp", sq, X1_SPILLED)
    for tc in range(TC):
        fslot = tc % 2
        if tc >= 2:
            P.w("sp", ac, cp[("fin_ln", tc - 2)])
        P.o("sp", lambda sp, t=tc: sp.dma_start(
            fa[:, t % 2, :], d_proj.ap()[t, :, :]), q_fn[fslot])
        ldv = P.o("sp", lambda sp, t=tc: sp.dma_start(
            fb[:, t % 2, :], d_x1.ap()[t, :, :]), q_fn[fslot])
        P.w("v", q_fn[fslot], ldv)
        cadd = P.o("v", lambda v, t=tc: v.tensor_add(
            fa[:, t % 2, :], fa[:, t % 2, :], fb[:, t % 2, :]), ve)
        cmb = ln_stats(fa[:, tc % 2, :], eps_ln, fsq[:, :], ve, cadd)
        P.w("s", ve, cmb)
        if tc >= 2:
            P.w("s", q_st[tc % 2], cp[("out_dma", tc - 2)])
        cfo = P.o("s", lambda s, t=tc: s.activation(
            fo[:, t % 2, :], fa[:, t % 2, :], AF.Identity,
            bias=st_mb[:, 0:1], scale=st_rstd[:, 0:1]), ac)
        cp[("fin_ln", tc)] = cfo
        P.w("sp", ac, cfo)
        cp[("out_dma", tc)] = P.o("sp", lambda sp, t=tc: sp.dma_start(
            out.ap()[t, :, :], fo[:, t % 2, :]), q_st[tc % 2])
    P.w("sp", q_st[0], q_st[0].v)
    P.w("sp", q_st[1], q_st[1].v)

    P.replay(nc)
    ctxn.close()
    psum.close()
    ctx0.close()
    return nc


_NC_CACHE = None


def _get_nc():
    global _NC_CACHE
    if _NC_CACHE is None:
        _NC_CACHE = build_nc()
    return _NC_CACHE


def kernel(L0_attn_x, L0_attn_w_qkv, L0_attn_w_out, L0_ffn_w_fc, L0_ffn_w_proj):
    x = np.ascontiguousarray(np.asarray(L0_attn_x, dtype=np.float32))
    w_qkv = np.ascontiguousarray(np.asarray(L0_attn_w_qkv, dtype=np.float32))
    w_out = np.ascontiguousarray(np.asarray(L0_attn_w_out, dtype=np.float32))
    w_fc = np.ascontiguousarray(np.asarray(L0_ffn_w_fc, dtype=np.float32))
    w_proj = np.ascontiguousarray(np.asarray(L0_ffn_w_proj, dtype=np.float32))

    xt = np.ascontiguousarray(
        x.transpose(0, 2, 1).reshape(NB, DC, 128, 4, 512).transpose(0, 3, 2, 1, 4))
    wout_dev = np.ascontiguousarray(w_out.reshape(DC, 128, D))
    wfc_dev = np.ascontiguousarray(
        w_fc.T.reshape(DC, 128, FG, 512).transpose(2, 0, 1, 3))
    wproj_dev = np.ascontiguousarray(
        w_proj.T.reshape(64, 128, OG, 256).transpose(2, 0, 1, 3))
    tri_m = np.ascontiguousarray(
        (np.arange(128)[None, :] >= np.arange(128)[:, None]).astype(BF))

    in_maps = []
    for c in range(8):
        b, hf = divmod(c, 2)
        xo_c = np.ascontiguousarray(
            x[b, hf * 1024:(hf + 1) * 1024].reshape(TC, 128, D))
        heads = [2 * c, 2 * c + 1]
        wqk_c = np.ascontiguousarray(np.stack([
            np.concatenate([w_qkv[:, h * 128:(h + 1) * 128],
                            w_qkv[:, D + h * 128:D + (h + 1) * 128]], axis=1)
            for h in heads]).reshape(H2, DC, 128, 256))
        wv_c = np.ascontiguousarray(np.concatenate(
            [w_qkv[:, 2 * D + h * 128:2 * D + (h + 1) * 128] for h in heads],
            axis=1).reshape(DC, 128, 256))
        in_maps.append({
            "xt": xt, "xo": xo_c, "wqk": wqk_c, "wv": wv_c,
            "wout": wout_dev, "wfc": wfc_dev, "wproj": wproj_dev, "tri": tri_m,
        })

    nc = _get_nc()
    res = run_bass_kernel_spmd(nc, in_maps, core_ids=list(range(8)))

    y = np.empty((NB, S, D), dtype=np.float32)
    for c in range(8):
        b, hf = divmod(c, 2)
        y[b, hf * 1024:(hf + 1) * 1024] = res.results[c]["out"].reshape(1024, D)
    return y


# version 12
# speedup vs baseline: 1.1539x; 1.0649x over previous
import numpy as np
import ml_dtypes
from contextlib import ExitStack

import concourse.bass as bass
import concourse.mybir as mybir
from concourse.bass_utils import run_bass_kernel_spmd
from concourse.aot_env import allow_ici_all_to_all

F32 = mybir.dt.float32
BF16 = mybir.dt.bfloat16
AF = mybir.ActivationFunctionType
ALU = mybir.AluOpType
AX = mybir.AxisListType

BF = ml_dtypes.bfloat16

NB = 4
S = 2048
D = 2048
DC = 16
H2 = 2
TC = 8
DFF = 8192
FG = 16
OG = 8
INV_SQRT_HD = 0.08838834764831845
EPS_LN = 1e-5
EPS_BL = 1e-6
CLIP_C = (128.0 - 1e-6) / 128.0
NWELEM = float(DFF * D)


class Sem:
    __slots__ = ("h", "v", "name", "dma")

    def __init__(self, nc, name, dma=False):
        self.h = nc.alloc_semaphore(name)
        self.v = 0
        self.name = name
        self.dma = dma


class Prog:

    ENGS = ("t", "v", "s", "g", "sp")

    def __init__(self, ve_sem, ac_sem):
        self.ops = {e: [] for e in self.ENGS}
        self.ve = ve_sem
        self.ac = ac_sem

    def w(self, e, sem, v):
        if v is None or v <= 0:
            return
        self.ops[e].append(lambda E, s=sem, vv=v: E.wait_ge(s.h, vv))

    def o(self, e, build, sem=None, n=1):
        if e == "v" and not (sem is not None and sem.dma):
            if self.ve.v > 0:
                self.w("v", self.ve, self.ve.v)
            if sem is None:
                sem = self.ve
            assert sem is self.ve
        elif e == "s" and not (sem is not None and sem.dma):
            if self.ac.v > 0:
                self.w("s", self.ac, self.ac.v)
            if sem is None:
                sem = self.ac
            assert sem is self.ac
        if sem is None:
            self.ops[e].append(build)
            return None
        if sem.dma:
            n = 16
        sem.v += n
        self.ops[e].append(lambda E, b=build, s=sem, nn=n: b(E).then_inc(s.h, nn))
        return sem.v

    def replay(self, nc):
        with nc.Block() as blk:
            for name, attr in (("t", "tensor"), ("v", "vector"), ("s", "scalar"),
                               ("g", "gpsimd"), ("sp", "sync")):
                if self.ops[name]:
                    fns = self.ops[name]

                    def body(E, fns=fns):
                        for f in fns:
                            f(E)
                    getattr(blk, attr)(body)


def build_nc():
    nc = bass.Bass("TRN2", target_bir_lowering=False)

    xt = nc.dram_tensor("xt", [NB, 4, 128, DC, 512], F32, kind="ExternalInput")
    xo = nc.dram_tensor("xo", [TC, 128, D], F32, kind="ExternalInput")
    wqk = nc.dram_tensor("wqk", [H2, DC, 128, 256], F32, kind="ExternalInput")
    wv = nc.dram_tensor("wv", [DC, 128, 256], F32, kind="ExternalInput")
    wout = nc.dram_tensor("wout", [DC, 128, D], F32, kind="ExternalInput")
    wfc = nc.dram_tensor("wfc", [FG, DC, 128, 512], F32, kind="ExternalInput")
    wproj = nc.dram_tensor("wproj", [OG, 64, 128, 256], F32, kind="ExternalInput")
    tri = nc.dram_tensor("tri", [128, 128], BF16, kind="ExternalInput")
    out = nc.dram_tensor("out", [TC, 128, D], F32, kind="ExternalOutput")

    d_a2a_in = nc.dram_tensor("d_a2a_in", [8, H2, 128, 1024], BF16)
    d_a2a_out = nc.dram_tensor("d_a2a_out", [8, H2, 128, 1024], BF16)
    d_xhat = nc.dram_tensor("d_xhat", [1024, D], BF16)
    d_hhat = nc.dram_tensor("d_hhat", [TC, 128, DFF], BF16)
    d_x1 = nc.dram_tensor("d_x1", [TC, 128, D], F32)
    d_proj = nc.dram_tensor("d_proj", [TC, 128, D], F32)

    mm = Sem(nc, "mm")
    ve = Sem(nc, "ve")
    ac = Sem(nc, "ac")
    cc = Sem(nc, "cc")
    po = Sem(nc, "po")
    sq = Sem(nc, "sq", dma=True)
    q_wq = Sem(nc, "q_wq", dma=True)
    q_tri = Sem(nc, "q_tri", dma=True)
    q_xq = [Sem(nc, "q_xq0", dma=True), Sem(nc, "q_xq1", dma=True)]
    q_gb = [Sem(nc, "q_gb0", dma=True), Sem(nc, "q_gb1", dma=True)]
    q_av = [Sem(nc, "q_av0", dma=True), Sem(nc, "q_av1", dma=True)]
    q_lda = Sem(nc, "q_lda", dma=True)
    q_ldx = Sem(nc, "q_ldx", dma=True)
    q_wo = [Sem(nc, "q_wo0", dma=True), Sem(nc, "q_wo1", dma=True)]
    q_xh = [Sem(nc, "q_xh0", dma=True), Sem(nc, "q_xh1", dma=True)]
    q_tr = Sem(nc, "q_tr", dma=True)
    q_wc = [Sem(nc, "q_wc0", dma=True), Sem(nc, "q_wc1", dma=True)]
    q_hh = Sem(nc, "q_hh", dma=True)
    q_ps = [Sem(nc, "q_ps0", dma=True), Sem(nc, "q_ps1", dma=True)]
    q_fn = [Sem(nc, "q_fn0", dma=True), Sem(nc, "q_fn1", dma=True)]
    q_st = [Sem(nc, "q_st0", dma=True), Sem(nc, "q_st1", dma=True)]

    ctx0 = ExitStack()
    sb = lambda n, s, d: ctx0.enter_context(nc.sbuf_tensor(n, s, d))

    ones_b = sb("ones_b", [128, 128], BF16)
    onesf_col = sb("onesf_col", [128, 1], F32)
    onesf_row = sb("onesf_row", [1, 128], F32)
    tri_s = sb("tri_s", [128, 128], BF16)
    eps_ln = sb("eps_ln", [128, 1], F32)
    eps_bl = sb("eps_bl", [128, 1], F32)
    gacc_f = sb("gacc_f", [128, 1], F32)
    gacc_p = sb("gacc_p", [128, 1], F32)
    gpart = sb("gpart", [128, 2], F32)
    tpos_f = sb("tpos_f", [128, 1], F32)
    tneg_f = sb("tneg_f", [128, 1], F32)
    gneg_f = sb("gneg_f", [128, 1], F32)
    tpos_p = sb("tpos_p", [128, 1], F32)
    tneg_p = sb("tneg_p", [128, 1], F32)
    gneg_p = sb("gneg_p", [128, 1], F32)
    sc11 = sb("sc11", [1, 1], F32)
    st_m = sb("st_m", [128, 1], F32)
    st_sq = sb("st_sq", [128, 1], F32)
    st_var = sb("st_var", [128, 1], F32)
    st_rstd = sb("st_rstd", [128, 1], F32)
    st_mb = sb("st_mb", [128, 1], F32)
    st_g = sb("st_g", [128, 1], F32)
    st_gp = sb("st_gp", [128, 1], F32)
    st_gn = sb("st_gn", [128, 1], F32)
    st_lo = sb("st_lo", [128, 1], F32)
    st_hi = sb("st_hi", [128, 1], F32)
    st_d1 = sb("st_d1", [128, 1], F32)
    st_d2 = sb("st_d2", [128, 1], F32)
    st8_m = sb("st8_m", [128, TC], F32)
    st8_sq = sb("st8_sq", [128, TC], F32)
    st8_var = sb("st8_var", [128, TC], F32)
    st8_d1 = sb("st8_d1", [128, TC], F32)
    st8_d2 = sb("st8_d2", [128, TC], F32)
    st8_r = sb("st8_r", [128, TC], F32)
    st8_c = sb("st8_c", [128, TC], F32)
    st8_lo = sb("st8_lo", [128, TC], F32)
    st8_hi = sb("st8_hi", [128, TC], F32)
    st8_nm = sb("st8_nm", [128, TC], F32)

    psum = ExitStack()
    pbig = psum.enter_context(nc.psum_tensor("pbig", [128, 2048], F32))
    pA = psum.enter_context(nc.psum_tensor("pA", [128, 512], F32))
    pB = psum.enter_context(nc.psum_tensor("pB", [128, 512], F32))
    pC = psum.enter_context(nc.psum_tensor("pC", [128, 512], F32))
    pD = psum.enter_context(nc.psum_tensor("pD", [128, 512], F32))

    cp = {}
    P = None

    def ln_stats(src_ap, eps_tile, dump_ap, rsem, rval):
        n = src_ap.shape[-1]
        P.w("v", rsem, rval)
        P.o("v", lambda v, a=src_ap: v.reduce_sum(st_m[:, 0:1], a, AX.X), ve)
        c2 = P.o("v", lambda v, a=src_ap, d=dump_ap: v.scalar_tensor_tensor(
            d, a, 1.0, a, op0=ALU.bypass, op1=ALU.mult,
            accum_out=st_sq[:, 0:1]), ve)
        P.w("s", ve, c2)
        P.o("s", lambda s, nn=n: s.activation(
            st_m[:, 0:1], st_m[:, 0:1], AF.Identity, scale=1.0 / nn), ac)
        P.o("s", lambda s, nn=n: s.activation(
            st_sq[:, 0:1], st_sq[:, 0:1], AF.Identity, scale=1.0 / nn), ac)
        c4 = P.o("s", lambda s: s.activation(
            st_var[:, 0:1], st_m[:, 0:1], AF.Square), ac)
        P.w("v", ac, c4)
        c5 = P.o("v", lambda v: v.tensor_sub(
            st_var[:, 0:1], st_var[:, 0:1], st_sq[:, 0:1]), ve)
        P.w("s", ve, c5)
        P.o("s", lambda s, e=eps_tile: s.activation(
            st_rstd[:, 0:1], st_var[:, 0:1], AF.Ln,
            bias=e[:, 0:1], scale=-1.0), ac)
        crs = P.o("s", lambda s: s.activation(
            st_rstd[:, 0:1], st_rstd[:, 0:1], AF.Exp, scale=-0.5), ac)
        P.w("v", ac, crs)
        cmb = P.o("v", lambda v: v.tensor_scalar(
            st_mb[:, 0:1], st_m[:, 0:1], st_rstd[:, 0:1], -1.0,
            op0=ALU.mult, op1=ALU.mult), ve)
        return cmb

    def gamma_chunk(gacc, gbuf_t, gdump_t, src_ap, ch):
        slot = ch % 2
        if ch >= 2:
            P.w("sp", ac, cp[("gst", id(gacc), ch - 2)])
        ldv = P.o("sp", lambda sp, a=src_ap, b=slot, g=gbuf_t:
                  sp.dma_start(g[:, b, :].rearrange(
                      "p (a f) -> p a f", a=a.shape[1]), a), q_gb[slot])
        P.w("s", q_gb[slot], ldv)
        if ch >= 2:
            P.w("s", ve, cp[("gadd", id(gacc), ch - 2)])
        cp[("gst", id(gacc), ch)] = P.o(
            "s", lambda s, b=slot, g=gbuf_t, d=gdump_t, ps=slot:
            s.activation(d[:, :], g[:, b, :], AF.Abs,
                         accum_out=gpart[:, ps:ps + 1]), ac)
        P.w("v", ac, cp[("gst", id(gacc), ch)])
        cp[("gadd", id(gacc), ch)] = P.o(
            "v", lambda v, ps=slot, ga=gacc:
            v.tensor_add(ga[:, 0:1], ga[:, 0:1], gpart[:, ps:ps + 1]), ve)
        cp[("gsum", id(gacc))] = ve.v

    def gamma_final(gacc, tpos, tneg, gneg):
        P.w("t", ve, cp[("gsum", id(gacc))])
        P.o("t", lambda t, ga=gacc: t.matmul(
            pD[0:1, 0:1], ga[:, 0:1], onesf_col[:, 0:1],
            start=True, stop=True), mm)
        s1 = mm.v
        P.w("s", mm, s1)
        s2 = P.o("s", lambda s: s.copy(sc11[:, :], pD[0:1, 0:1]), ac)
        P.w("t", ac, s2)
        P.o("t", lambda t: t.matmul(pD[:, 1:2], onesf_row[:, :], sc11[:, :],
                                    start=True, stop=True), mm)
        s3 = mm.v
        P.w("v", mm, s3)
        P.o("v", lambda v, tp=tpos: v.tensor_scalar(
            tp[:, 0:1], pD[:, 1:2], 0.5 / NWELEM, 0.5 * EPS_BL,
            op0=ALU.mult, op1=ALU.add), ve)
        P.o("v", lambda v, gn=gneg: v.tensor_scalar(
            gn[:, 0:1], pD[:, 1:2], -1.0 / NWELEM, None, op0=ALU.mult), ve)
        cfin = P.o("v", lambda v, tp=tpos, tn=tneg: v.tensor_scalar(
            tn[:, 0:1], tp[:, 0:1], -1.0, None, op0=ALU.mult), ve)
        cp[("gfin", id(gacc))] = cfin
        return cfin

    ctxq = ExitStack()
    sbq = lambda n, s, d: ctxq.enter_context(nc.sbuf_tensor(n, s, d))
    xq = sbq("xq", [128, 2, DC, 512], BF16)
    wqk_s = sbq("wqk_s", [128, H2, DC, 256], BF16)
    wv_s = sbq("wv_s", [128, DC, 256], BF16)
    kT_s = sbq("kT_s", [128, H2, S], BF16)
    qT_s = sbq("qT_s", [128, H2, S], BF16)
    v_s = sbq("v_s", [128, 16, 256], BF16)
    pt = sbq("pt", [128, 4, 512], BF16)
    lln = sbq("lln", [128, 512], F32)
    lrec = sbq("lrec", [128, 512], F32)
    avst = sbq("avst", [128, 2, 512], BF16)
    gbuf = sbq("gbuf", [128, 2, 2048], F32)
    gdump = sbq("gdump", [128, 2048], F32)

    P = Prog(ve, ac)

    for f in (lambda g: g.memset(ones_b[:, :], 1.0),
              lambda g: g.memset(onesf_col[:, :], 1.0),
              lambda g: g.memset(onesf_row[:, :], 1.0),
              lambda g: g.memset(eps_ln[:, :], EPS_LN),
              lambda g: g.memset(eps_bl[:, :], EPS_BL),
              lambda g: g.memset(gacc_f[:, :], 0.0)):
        P.o("g", f)
    CONST_READY = P.o("g", lambda g: g.memset(gacc_p[:, :], 0.0), cc)
    P.o("g", lambda g: g.dma_start(
        wqk_s[:, :, :, :], wqk.ap().rearrange("h dc p f -> p h dc f")), q_wq)
    WQKV_LD = P.o("g", lambda g: g.dma_start(
        wv_s[:, :, :], wv.ap().rearrange("dc p f -> p dc f")), q_wq)
    TRI_LD = P.o("sp", lambda sp: sp.dma_start(tri_s[:, :], tri.ap()), q_tri)
    for e in ("t", "v", "s"):
        P.w(e, cc, CONST_READY)
    P.w("v", q_tri, TRI_LD)

    pair_hist = []
    avdma_hist = []
    qkv_ping = [0]

    NGCH = 64

    def wfc_chunk_ap(ch):
        fg, pr = divmod(ch, 4)
        return wfc.ap()[fg, 4 * pr:4 * pr + 4, :, :].rearrange(
            "dc p f -> p dc f")

    gamma_f_pos = [0]

    def emit_gamma_f(n):
        for _ in range(n):
            ch = gamma_f_pos[0]
            if ch >= NGCH:
                return
            gamma_chunk(gacc_f, gbuf, gdump, wfc_chunk_ap(ch), ch)
            gamma_f_pos[0] += 1

    def q_batch(b):
        for i in range(4):
            gidx = b * 4 + i
            slot = gidx % 2
            if gidx >= 2:
                P.w("g", mm, cp[("xq_done", gidx - 2)])
            ldv = P.o("g", lambda g, bb=b, ii=i: g.dma_start(
                xq[:, ii % 2, :, :], xt.ap()[bb, ii, :, :, :]), q_xq[slot])
            P.w("t", q_xq[slot], ldv)
            if gidx == 0:
                P.w("t", q_wq, WQKV_LD)
            for h2 in range(H2):
                for qk in range(2):
                    dst = (qT_s, kT_s)[qk]
                    bank = (pC, pD)[qkv_ping[0] % 2]
                    qkv_ping[0] += 1
                    P.w("t", ve, cp.get(("evac", id(bank))))
                    for dc in range(DC):
                        P.o("t", lambda t, bk=bank, h=h2, q=qk, d=dc, ii=i:
                            t.matmul(bk[:, 0:512],
                                     wqk_s[:, h, d, q * 128:(q + 1) * 128],
                                     xq[:, ii % 2, d, :],
                                     start=(d == 0), stop=(d == DC - 1)),
                            mm if dc == DC - 1 else None)
                    P.w("v", mm, mm.v)
                    cp[("evac", id(bank))] = P.o(
                        "v", lambda v, bk=bank, ds=dst, h=h2, ii=i:
                        v.tensor_copy(ds[:, h, ii * 512:(ii + 1) * 512],
                                      bk[:, 0:512]), ve)
            for t128 in range(4):
                bank = (pC, pD)[qkv_ping[0] % 2]
                qkv_ping[0] += 1
                P.w("t", ve, cp.get(("evac", id(bank))))
                for dc in range(DC):
                    P.o("t", lambda t, bk=bank, d=dc, ii=i, tk=t128:
                        t.matmul(bk[:, 0:256],
                                 xq[:, ii % 2, d, tk * 128:(tk + 1) * 128],
                                 wv_s[:, d, :],
                                 start=(d == 0), stop=(d == DC - 1)),
                        mm if dc == DC - 1 else None)
                P.w("v", mm, mm.v)
                cp[("evac", id(bank))] = P.o(
                    "v", lambda v, bk=bank, kb=i * 4 + t128:
                    v.tensor_copy(v_s[:, kb, :], bk[:, 0:256]), ve)
            cp[("xq_done", gidx)] = mm.v
        qkv_done = ve.v

        for h2 in range(H2):
            P.w("t", ve, qkv_done)
            for s4 in range(4):
                kbN = 4 * (s4 + 1)
                for pr in range(kbN // 2):
                    k0, k1 = 2 * pr, 2 * pr + 1
                    if len(pair_hist) >= 2:
                        P.w("t", ac, pair_hist[-2][0])
                    for kb in (k0, k1):
                        qq = kb - 4 * s4
                        sfx = 128 * qq if qq >= 0 else 0
                        P.o("t", lambda t, h=h2, s=s4, kkb=kb, sf=sfx:
                            t.matmul(pbig[:, (kkb % 4) * 512 + sf:
                                             (kkb % 4) * 512 + 512],
                                     kT_s[:, h, kkb * 128:(kkb + 1) * 128],
                                     qT_s[:, h, s * 512 + sf:(s + 1) * 512],
                                     start=True, stop=True),
                            mm if kb == k1 else None)
                    cps = mm.v
                    P.w("s", mm, cps)
                    if len(pair_hist) >= 2:
                        P.w("s", mm, pair_hist[-2][1])
                    cpe = P.o("s", lambda s, kk=k0: s.activation(
                        pt[:, kk % 4:kk % 4 + 2, :],
                        pbig[:, (kk % 4) * 512:(kk % 4) * 512 + 1024],
                        AF.Exp, scale=INV_SQRT_HD), ac)
                    cpt = None
                    for kb in (k0, k1):
                        qq = kb - 4 * s4
                        if 0 <= qq <= 3:
                            P.w("v", ac, cpe)
                            cpt = P.o("v", lambda v, kkb=kb, q=qq:
                                      v.tensor_mul(
                                          pt[:, kkb % 4, q * 128:(q + 1) * 128],
                                          pt[:, kkb % 4, q * 128:(q + 1) * 128],
                                          tri_s[:, :]), ve)
                    P.w("t", ac, cpe)
                    if cpt is not None:
                        P.w("t", ve, cpt)
                    if pr == 0:
                        P.w("t", ve, cp.get("avl_used"))
                        P.w("t", ac, cp.get("l_used"))
                    for kb in (k0, k1):
                        qq = kb - 4 * s4
                        sfx = 128 * qq if qq >= 0 else 0
                        fst, lst = (kb == 0), (kb == kbN - 1)
                        P.o("t", lambda t, kkb=kb, h=h2, sf=sfx, f=fst, l=lst:
                            t.matmul(pA[:, sf:512],
                                     v_s[:, kkb, h * 128:(h + 1) * 128],
                                     pt[:, kkb % 4, sf:512],
                                     start=f, stop=l), None)
                        P.o("t", lambda t, kkb=kb, sf=sfx, f=fst, l=lst:
                            t.matmul(pB[:, sf:512], ones_b[:, :],
                                     pt[:, kkb % 4, sf:512],
                                     start=f, stop=l),
                            mm if kb == k1 else None)
                    pair_hist.append((cpe, mm.v))
                P.w("s", mm, pair_hist[-1][1])
                cp["l_used"] = P.o("s", lambda s: s.activation(
                    lln[:, :], pB[:, 0:512], AF.Ln), ac)
                clr = P.o("s", lambda s: s.activation(
                    lrec[:, :], lln[:, :], AF.Exp, scale=-1.0), ac)
                P.w("v", ac, clr)
                if len(avdma_hist) >= 2:
                    P.w("v", avdma_hist[-2][0], avdma_hist[-2][1])
                cp["avl_used"] = P.o("v", lambda v, s=s4: v.tensor_mul(
                    avst[:, s % 2, :], pA[:, 0:512], lrec[:, :]), ve)
                P.w("sp", ve, cp["avl_used"])
                j = 2 * b + (s4 // 2)
                off = (s4 % 2) * 512
                aslot = len(avdma_hist) % 2
                dv = P.o("sp", lambda sp, jj=j, h=h2, of=off, s=s4:
                         sp.dma_start(d_a2a_in.ap()[jj, h, :, of:of + 512],
                                      avst[:, s % 2, :]), q_av[aslot])
                avdma_hist.append((q_av[aslot], dv))

    for b in range(NB):
        q_batch(b)
        emit_gamma_f(16)
    emit_gamma_f(NGCH)
    gamma_final(gacc_f, tpos_f, tneg_f, gneg_f)

    P.replay(nc)
    ctxq.close()
    GBF_END = (q_gb[0].v, q_gb[1].v)
    GAMF_ABS_END = cp[("gst", id(gacc_f), NGCH - 1)]

    ctxw = ExitStack()
    sbw = lambda n, s, d: ctxw.enter_context(nc.sbuf_tensor(n, s, d))
    avT_all = sbw("avT_all", [128, 16, 1024], BF16)
    stream = sbw("stream", [128, TC, D], F32)
    woutb = sbw("woutb", [128, 2, DC, 512], BF16)
    xn_st = sbw("xn_st", [128, D], F32)
    xh_st = sbw("xh_st", [128, 2, D], BF16)
    gbuf2 = sbw("gbuf2", [128, 2, 2048], F32)
    gdump2 = sbw("gdump2", [128, 2048], F32)

    P = Prog(ve, ac)

    P.w("g", q_av[0], q_av[0].v)
    P.w("g", q_av[1], q_av[1].v)

    def _cc_call(g):
        with allow_ici_all_to_all():
            return g.collective_compute(
                "AllToAll", ALU.bypass, replica_groups=[list(range(8))],
                ins=[d_a2a_in.ap().rearrange("a b c d -> a (b c d)")],
                outs=[d_a2a_out.ap().rearrange("a b c d -> a (b c d)")])

    CC_DONE = P.o("g", _cc_call, cc)

    def wproj_chunk_ap(ch):
        og, fr = divmod(ch, 8)
        return wproj.ap()[og, 8 * fr:8 * fr + 8, :, :].rearrange(
            "fc p f -> p fc f")

    for ch in range(NGCH):
        gamma_chunk(gacc_p, gbuf2, gdump2, wproj_chunk_ap(ch), ch)
    gamma_final(gacc_p, tpos_p, tneg_p, gneg_p)
    cp[("evac", id(pD))] = cp[("gfin", id(gacc_p))]
    cp[("evac", id(pC))] = None

    P.w("sp", cc, CC_DONE)
    P.w("sp", q_gb[0], GBF_END[0])
    P.w("sp", q_gb[1], GBF_END[1])
    P.w("sp", ac, GAMF_ABS_END)
    for j in range(8):
        for h2 in range(H2):
            AVT_LD = P.o("sp", lambda sp, jj=j, h=h2: sp.dma_start(
                avT_all[:, jj * 2 + h, :], d_a2a_out.ap()[jj, h, :, :]), q_lda)
    for tc in range(TC):
        XO_LD = P.o("sp", lambda sp, t=tc: sp.dma_start(
            stream[:, t, :], xo.ap()[t, :, :]), q_ldx)

    for og in range(4):
        slot = og % 2
        if og >= 2:
            P.w("g", mm, cp[("wout_mm", og - 2)])
        ldv = P.o("g", lambda g, o=og: g.dma_start(
            woutb[:, o % 2, :, :],
            wout.ap()[:, :, o * 512:(o + 1) * 512].rearrange(
                "dvc p f -> p dvc f")), q_wo[slot])
        P.w("t", q_wo[slot], ldv)
        if og == 0:
            P.w("t", q_lda, AVT_LD)
        for tc in range(TC):
            bank = (pC, pD)[tc % 2]
            P.w("t", ve, cp.get(("evac", id(bank))))
            for dvc in range(DC):
                P.o("t", lambda t, bk=bank, d=dvc, tt=tc, o=og:
                    t.matmul(bk[:, 0:512],
                             avT_all[:, d, tt * 128:(tt + 1) * 128],
                             woutb[:, o % 2, d, :],
                             start=(d == 0), stop=(d == DC - 1)),
                    mm if dvc == DC - 1 else None)
            P.w("v", mm, mm.v)
            if og == 0 and tc == 0:
                P.w("v", q_ldx, XO_LD)
            cp[("evac", id(bank))] = P.o(
                "v", lambda v, bk=bank, tt=tc, o=og:
                v.tensor_add(stream[:, tt, o * 512:(o + 1) * 512],
                             bk[:, 0:512],
                             stream[:, tt, o * 512:(o + 1) * 512]), ve)
        cp[("wout_mm", og)] = mm.v
    RESID1 = ve.v

    cxh_prev = None
    for tc in range(TC):
        if cxh_prev is not None:
            P.w("s", ve, cxh_prev)
        cmb = ln_stats(stream[:, tc, :], eps_ln, xn_st[:, :], ve, RESID1)
        P.w("s", ve, cmb)
        cx1 = P.o("s", lambda s, t=tc: s.activation(
            stream[:, t, :], stream[:, t, :], AF.Identity,
            bias=st_mb[:, 0:1], scale=st_rstd[:, 0:1]), ac)
        cmb2 = ln_stats(stream[:, tc, :], eps_bl, xn_st[:, :], ac, cx1)
        P.w("s", ve, cmb2)
        cxn = P.o("s", lambda s, t=tc: s.activation(
            xn_st[:, :], stream[:, t, :], AF.Identity,
            bias=st_mb[:, 0:1], scale=st_rstd[:, 0:1]), ac)
        P.w("sp", ac, cxn)
        P.o("sp", lambda sp, t=tc: sp.dma_start(
            d_x1.ap()[t, :, :], stream[:, t, :]), sq)
        P.w("v", ac, cxn)
        cg1 = P.o("v", lambda v: v.reduce_max(st_g[:, 0:1], xn_st[:, :], AX.X,
                                              apply_absolute_value=True), ve)
        P.w("s", ve, cg1)
        P.o("s", lambda s: s.activation(st_gp[:, 0:1], st_g[:, 0:1],
                                        AF.Identity, scale=CLIP_C), ac)
        cg2 = P.o("s", lambda s: s.activation(st_gn[:, 0:1], st_g[:, 0:1],
                                              AF.Identity, scale=-CLIP_C), ac)
        P.w("v", ac, cg2)
        if tc >= 2:
            P.w("v", q_xh[tc % 2], cp[("xhat_dma", tc - 2)])
        cxh = P.o("v", lambda v, t=tc: v.tensor_scalar(
            xh_st[:, t % 2, :], xn_st[:, :], st_gn[:, 0:1], st_gp[:, 0:1],
            op0=ALU.max, op1=ALU.min), ve)
        cxh_prev = cxh
        P.w("sp", ve, cxh)
        cp[("xhat_dma", tc)] = P.o("sp", lambda sp, t=tc: sp.dma_start(
            d_xhat.ap()[t * 128:(t + 1) * 128, :], xh_st[:, t % 2, :]),
            q_xh[tc % 2])
    X1_SPILLED = sq.v

    P.replay(nc)
    ctxw.close()

    ctxh = ExitStack()
    htens = ctxh.enter_context(nc.sbuf_tensor("htens", [128, TC, DFF], BF16))
    hsump = ctxh.enter_context(nc.sbuf_tensor("hsump", [128, TC, FG], F32))
    hsqp = ctxh.enter_context(nc.sbuf_tensor("hsqp", [128, TC, FG], F32))

    ctxf1 = ExitStack()
    sbf1 = lambda n, s, d: ctxf1.enter_context(nc.sbuf_tensor(n, s, d))
    xhatT = sbf1("xhatT", [128, DC, 1024], BF16)
    ternf = sbf1("ternf", [128, 2, DC, 512], BF16)
    wch = sbf1("wch", [128, 2, 1024], F32)
    tsa_t = sbf1("tsa_t", [128, 1024], F32)
    tsa = tsa_t[:, :]

    P = Prog(ve, ac)
    P.w("sp", q_xh[0], q_xh[0].v)
    P.w("sp", q_xh[1], q_xh[1].v)
    P.w("s", sq, X1_SPILLED)
    P.w("s", q_xh[0], q_xh[0].v)
    P.w("s", q_xh[1], q_xh[1].v)
    for dc in range(DC):
        XT_LD = P.o("sp", lambda sp, d=dc: sp.dma_start(
            xhatT[:, d, :], d_xhat.ap()[:, d * 128:(d + 1) * 128],
            transpose=True), q_tr)

    for fg in range(FG):
        for chh in range(8):
            gidx = fg * 8 + chh
            slot = gidx % 2
            if gidx >= 2:
                P.w("sp", ve, cp[("ternf_ch", gidx - 2)])
            ldv = P.o("sp", lambda sp, f=fg, c=chh, b=slot: sp.dma_start(
                wch[:, b, :].rearrange("p (a f) -> p a f", a=2),
                wfc.ap()[f, 2 * c:2 * c + 2, :, :].rearrange(
                    "dc p f -> p dc f")), q_wc[slot])
            P.w("v", q_wc[slot], ldv)
            if fg >= 2 and chh == 0:
                P.w("v", mm, cp[("fc_mm", fg - 2)])
            P.o("v", lambda v, b=slot: v.tensor_scalar(
                tsa, wch[:, b, :], tpos_f[:, 0:1], None,
                op0=ALU.is_ge), ve)
            cp[("ternf_ch", gidx)] = P.o(
                "v", lambda v, b=slot, f=fg, c=chh:
                v.scalar_tensor_tensor(
                    ternf[:, f % 2, 2 * c:2 * c + 2, :].rearrange(
                        "p a f -> p (a f)"),
                    wch[:, b, :], tneg_f[:, 0:1], tsa,
                    op0=ALU.is_le, op1=ALU.subtract), ve)
        tern_done = ve.v

        P.w("t", ve, tern_done)
        if fg == 0:
            P.w("t", q_tr, XT_LD)
        for tc in range(TC):
            bank = (pC, pD)[tc % 2]
            P.w("t", ac, cp.get(("fc_evac", id(bank))))
            for dc in range(DC):
                P.o("t", lambda t, bk=bank, d=dc, tt=tc, f=fg:
                    t.matmul(bk[:, 0:512],
                             xhatT[:, d, tt * 128:(tt + 1) * 128],
                             ternf[:, f % 2, d, :],
                             start=(d == 0), stop=(d == DC - 1)),
                    mm if dc == DC - 1 else None)
            P.w("s", mm, mm.v)
            cp[("fc_evac", id(bank))] = P.o(
                "s", lambda s, bk=bank, tt=tc, f=fg: s.activation(
                    htens[:, tt, f * 512:(f + 1) * 512], bk[:, 0:512],
                    AF.Gelu, scale=gneg_f[:, 0:1],
                    accum_out=hsump[:, tt, f:f + 1]), ac)
            csq = P.o("s", lambda s, tt=tc, f=fg: s.activation(
                pbig[:, 0:512], htens[:, tt, f * 512:(f + 1) * 512],
                AF.Square, accum_out=hsqp[:, tt, f:f + 1]), ac)

        cp[("fc_mm", fg)] = mm.v
    F1_DONE_AC = ac.v
    F1_DONE_VE = ve.v

    P.replay(nc)
    ctxf1.close()

    ctx15 = ExitStack()
    hn_st = ctx15.enter_context(nc.sbuf_tensor("hn_st", [128, DFF], F32))
    hh_st = ctx15.enter_context(nc.sbuf_tensor("hh_st", [128, DFF], BF16))
    q_tr2 = Sem(nc, "q_tr2", dma=True)

    P = Prog(ve, ac)
    P.w("v", ac, F1_DONE_AC)
    for tc in range(TC):
        P.o("v", lambda v, t=tc: v.reduce_sum(
            st8_m[:, t:t + 1], hsump[:, t, :], AX.X), ve)
        P.o("v", lambda v, t=tc: v.reduce_sum(
            st8_sq[:, t:t + 1], hsqp[:, t, :], AX.X), ve)
    c2 = ve.v
    P.w("s", ve, c2)
    P.o("s", lambda s: s.activation(
        st8_m[:, :], st8_m[:, :], AF.Identity, scale=1.0 / DFF), ac)
    P.o("s", lambda s: s.activation(
        st8_sq[:, :], st8_sq[:, :], AF.Identity, scale=1.0 / DFF), ac)
    c4 = P.o("s", lambda s: s.activation(
        st8_var[:, :], st8_m[:, :], AF.Square), ac)
    P.w("v", ac, c4)
    c5 = P.o("v", lambda v: v.tensor_sub(st8_var[:, :], st8_var[:, :],
                                         st8_sq[:, :]), ve)
    P.w("s", ve, c5)
    P.o("s", lambda s: s.activation(
        st8_r[:, :], st8_var[:, :], AF.Ln,
        bias=eps_bl[:, 0:1], scale=-1.0), ac)
    c6 = P.o("s", lambda s: s.activation(
        st8_r[:, :], st8_r[:, :], AF.Exp, scale=-0.5), ac)
    P.w("v", ac, c6)
    c_nm = P.o("v", lambda v: v.scalar_tensor_tensor(
        st8_nm[:, :], st8_m[:, :], -1.0, st8_r[:, :],
        op0=ALU.mult, op1=ALU.mult), ve)

    for tc in range(TC):
        if tc == 0:
            P.w("s", ve, c_nm)
        if tc >= 1:
            P.w("s", ve, cp[("f15_clip", tc - 1)])
        chn = P.o("s", lambda s, t=tc: s.activation(
            hn_st[:, :], htens[:, t, :], AF.Identity,
            bias=st8_nm[:, t:t + 1], scale=st8_r[:, t:t + 1]), ac)
        P.w("v", ac, chn)
        cg1 = P.o("v", lambda v: v.reduce_max(st_g[:, 0:1], hn_st[:, :], AX.X,
                                              apply_absolute_value=True), ve)
        P.w("s", ve, cg1)
        P.o("s", lambda s: s.activation(st_gp[:, 0:1], st_g[:, 0:1],
                                        AF.Identity, scale=CLIP_C), ac)
        cg2 = P.o("s", lambda s: s.activation(st_gn[:, 0:1], st_g[:, 0:1],
                                              AF.Identity, scale=-CLIP_C), ac)
        P.w("v", ac, cg2)
        if tc >= 1:
            P.w("v", q_hh, cp[("hhat_dma", tc - 1)])
        chh2 = P.o("v", lambda v: v.tensor_scalar(
            hh_st[:, :], hn_st[:, :], st_gn[:, 0:1], st_gp[:, 0:1],
            op0=ALU.max, op1=ALU.min), ve)
        cp[("f15_clip", tc)] = chh2
        P.w("sp", ve, chh2)
        chd = P.o("sp", lambda sp, t=tc: sp.dma_start(
            d_hhat.ap()[t, :, :], hh_st[:, :]), q_hh)
        cp[("hhat_dma", tc)] = chd
        P.w("sp", ac, chn)
        P.w("sp", q_hh, chd)
        P.w("s", q_hh, chd)
        for fc in range(64):
            eng = "sp" if fc % 2 == 0 else "s"
            qq_tr = q_tr if fc % 2 == 0 else q_tr2
            P.o(eng, lambda E, t=tc, f=fc: E.dma_start(
                htens[:, t, f * 128:(f + 1) * 128],
                d_hhat.ap()[t, :, f * 128:(f + 1) * 128],
                transpose=True), qq_tr)
    F15_DONE = q_tr.v
    F15_DONE2 = q_tr2.v

    P.replay(nc)
    ctx15.close()

    ctxf2 = ExitStack()
    sbf2 = lambda n, s, d: ctxf2.enter_context(nc.sbuf_tensor(n, s, d))
    ternp = sbf2("ternp", [128, 2, 64, 256], BF16)
    wch2 = sbf2("wch2", [128, 2, 1024], F32)
    pst = sbf2("pst", [128, 2, 512], F32)
    tsa2 = pbig[:, 1024:2048]

    P = Prog(ve, ac)
    P.w("v", q_hh, q_hh.v)
    pst_hist = []
    for pg in range(OG // 2):
        for chh in range(32):
            og = 2 * pg + chh // 16
            c = chh % 16
            tslot = chh // 16
            gidx = pg * 32 + chh
            wslot = gidx % 2
            if gidx >= 2:
                P.w("sp", ve, cp[("ternp_ch", gidx - 2)])
            if pg >= 1 and chh == 0:
                P.w("sp", mm, cp[("proj_mm", pg - 1)])
            ldv = P.o("sp", lambda sp, o=og, cx=c, b=wslot: sp.dma_start(
                wch2[:, b, :].rearrange("p (a f) -> p a f", a=4),
                wproj.ap()[o, 4 * cx:4 * cx + 4, :, :].rearrange(
                    "fc p f -> p fc f")), q_wc[wslot])
            P.w("v", q_wc[wslot], ldv)
            P.o("v", lambda v, b=wslot: v.tensor_scalar(
                tsa2, wch2[:, b, :], tpos_p[:, 0:1], None,
                op0=ALU.is_ge), ve)
            cp[("ternp_ch", gidx)] = P.o(
                "v", lambda v, b=wslot, ts=tslot, cx=c:
                v.scalar_tensor_tensor(
                    ternp[:, ts, 4 * cx:4 * cx + 4, :].rearrange(
                        "p a f -> p (a f)"),
                    wch2[:, b, :], tneg_p[:, 0:1], tsa2,
                    op0=ALU.is_le, op1=ALU.subtract), ve)
        ternp_done = ve.v

        P.w("t", ve, ternp_done)
        if pg == 0:
            P.w("t", q_tr, F15_DONE)
            P.w("t", q_tr2, F15_DONE2)
        for tc in range(TC):
            bank = (pC, pD)[tc % 2]
            P.w("t", ve, cp.get(("pj_evac", id(bank))))
            for fc in range(64):
                P.o("t", lambda t, bk=bank, f=fc, tt=tc:
                    t.matmul(bk[:, 0:512],
                             htens[:, tt, f * 128:(f + 1) * 128],
                             ternp[:, :, f, :],
                             start=(f == 0), stop=(f == 63)),
                    mm if fc == 63 else None)
            P.w("v", mm, mm.v)
            if len(pst_hist) >= 2:
                P.w("v", pst_hist[-2][0], pst_hist[-2][1])
            cpe = P.o("v", lambda v, bk=bank, tt=tc: v.tensor_scalar(
                pst[:, tt % 2, :], bk[:, 0:512], gneg_p[:, 0:1], None,
                op0=ALU.mult), ve)
            cp[("pj_evac", id(bank))] = cpe
            P.w("sp", ve, cpe)
            pslot = tc % 2
            dv = P.o("sp", lambda sp, tt=tc, p=pg: sp.dma_start(
                d_proj.ap()[tt, :, p * 512:(p + 1) * 512],
                pst[:, tt % 2, :]), q_ps[pslot])
            pst_hist.append((q_ps[pslot], dv))
        cp[("proj_mm", pg)] = mm.v

    P.replay(nc)
    ctxf2.close()
    ctxh.close()

    ctxn = ExitStack()
    sbn = lambda n, s, d: ctxn.enter_context(nc.sbuf_tensor(n, s, d))
    fa = sbn("fa", [128, 2, D], F32)
    fb = sbn("fb", [128, 2, D], F32)
    fsq = sbn("fsq", [128, D], F32)
    fo = sbn("fo", [128, 2, D], F32)

    P = Prog(ve, ac)
    P.w("sp", q_ps[0], q_ps[0].v)
    P.w("sp", q_ps[1], q_ps[1].v)
    P.w("sp", sq, X1_SPILLED)
    for tc in range(TC):
        fslot = tc % 2
        if tc >= 2:
            P.w("sp", ac, cp[("fin_ln", tc - 2)])
        P.o("sp", lambda sp, t=tc: sp.dma_start(
            fa[:, t % 2, :], d_proj.ap()[t, :, :]), q_fn[fslot])
        ldv = P.o("sp", lambda sp, t=tc: sp.dma_start(
            fb[:, t % 2, :], d_x1.ap()[t, :, :]), q_fn[fslot])
        P.w("v", q_fn[fslot], ldv)
        cadd = P.o("v", lambda v, t=tc: v.tensor_add(
            fa[:, t % 2, :], fa[:, t % 2, :], fb[:, t % 2, :]), ve)
        cmb = ln_stats(fa[:, tc % 2, :], eps_ln, fsq[:, :], ve, cadd)
        P.w("s", ve, cmb)
        if tc >= 2:
            P.w("s", q_st[tc % 2], cp[("out_dma", tc - 2)])
        cfo = P.o("s", lambda s, t=tc: s.activation(
            fo[:, t % 2, :], fa[:, t % 2, :], AF.Identity,
            bias=st_mb[:, 0:1], scale=st_rstd[:, 0:1]), ac)
        cp[("fin_ln", tc)] = cfo
        P.w("sp", ac, cfo)
        cp[("out_dma", tc)] = P.o("sp", lambda sp, t=tc: sp.dma_start(
            out.ap()[t, :, :], fo[:, t % 2, :]), q_st[tc % 2])
    P.w("sp", q_st[0], q_st[0].v)
    P.w("sp", q_st[1], q_st[1].v)

    P.replay(nc)
    ctxn.close()
    psum.close()
    ctx0.close()
    return nc


_NC_CACHE = None


def _get_nc():
    global _NC_CACHE
    if _NC_CACHE is None:
        _NC_CACHE = build_nc()
    return _NC_CACHE


def kernel(L0_attn_x, L0_attn_w_qkv, L0_attn_w_out, L0_ffn_w_fc, L0_ffn_w_proj):
    x = np.ascontiguousarray(np.asarray(L0_attn_x, dtype=np.float32))
    w_qkv = np.ascontiguousarray(np.asarray(L0_attn_w_qkv, dtype=np.float32))
    w_out = np.ascontiguousarray(np.asarray(L0_attn_w_out, dtype=np.float32))
    w_fc = np.ascontiguousarray(np.asarray(L0_ffn_w_fc, dtype=np.float32))
    w_proj = np.ascontiguousarray(np.asarray(L0_ffn_w_proj, dtype=np.float32))

    xt = np.ascontiguousarray(
        x.transpose(0, 2, 1).reshape(NB, DC, 128, 4, 512).transpose(0, 3, 2, 1, 4))
    wout_dev = np.ascontiguousarray(w_out.reshape(DC, 128, D))
    wfc_dev = np.ascontiguousarray(
        w_fc.T.reshape(DC, 128, FG, 512).transpose(2, 0, 1, 3))
    wproj_dev = np.ascontiguousarray(
        w_proj.T.reshape(64, 128, OG, 256).transpose(2, 0, 1, 3))
    tri_m = np.ascontiguousarray(
        (np.arange(128)[None, :] >= np.arange(128)[:, None]).astype(BF))

    in_maps = []
    for c in range(8):
        b, hf = divmod(c, 2)
        xo_c = np.ascontiguousarray(
            x[b, hf * 1024:(hf + 1) * 1024].reshape(TC, 128, D))
        heads = [2 * c, 2 * c + 1]
        wqk_c = np.ascontiguousarray(np.stack([
            np.concatenate([w_qkv[:, h * 128:(h + 1) * 128],
                            w_qkv[:, D + h * 128:D + (h + 1) * 128]], axis=1)
            for h in heads]).reshape(H2, DC, 128, 256))
        wv_c = np.ascontiguousarray(np.concatenate(
            [w_qkv[:, 2 * D + h * 128:2 * D + (h + 1) * 128] for h in heads],
            axis=1).reshape(DC, 128, 256))
        in_maps.append({
            "xt": xt, "xo": xo_c, "wqk": wqk_c, "wv": wv_c,
            "wout": wout_dev, "wfc": wfc_dev, "wproj": wproj_dev, "tri": tri_m,
        })

    nc = _get_nc()
    res = run_bass_kernel_spmd(nc, in_maps, core_ids=list(range(8)))

    y = np.empty((NB, S, D), dtype=np.float32)
    for c in range(8):
        b, hf = divmod(c, 2)
        y[b, hf * 1024:(hf + 1) * 1024] = res.results[c]["out"].reshape(1024, D)
    return y


# version 13
# speedup vs baseline: 1.1556x; 1.0015x over previous
import numpy as np
import ml_dtypes
from contextlib import ExitStack

import concourse.bass as bass
import concourse.mybir as mybir
from concourse.bass_utils import run_bass_kernel_spmd
from concourse.aot_env import allow_ici_all_to_all

F32 = mybir.dt.float32
BF16 = mybir.dt.bfloat16
AF = mybir.ActivationFunctionType
ALU = mybir.AluOpType
AX = mybir.AxisListType

BF = ml_dtypes.bfloat16

NB = 4
S = 2048
D = 2048
DC = 16
H2 = 2
TC = 8
DFF = 8192
FG = 16
OG = 8
INV_SQRT_HD = 0.08838834764831845
EPS_LN = 1e-5
EPS_BL = 1e-6
CLIP_C = (128.0 - 1e-6) / 128.0
NWELEM = float(DFF * D)


class Sem:
    __slots__ = ("h", "v", "name", "dma")

    def __init__(self, nc, name, dma=False):
        self.h = nc.alloc_semaphore(name)
        self.v = 0
        self.name = name
        self.dma = dma


class Prog:

    ENGS = ("t", "v", "s", "g", "sp")

    def __init__(self, ve_sem, ac_sem):
        self.ops = {e: [] for e in self.ENGS}
        self.ve = ve_sem
        self.ac = ac_sem

    def w(self, e, sem, v):
        if v is None or v <= 0:
            return
        self.ops[e].append(lambda E, s=sem, vv=v: E.wait_ge(s.h, vv))

    def o(self, e, build, sem=None, n=1):
        if e == "v" and not (sem is not None and sem.dma):
            if self.ve.v > 0:
                self.w("v", self.ve, self.ve.v)
            if sem is None:
                sem = self.ve
            assert sem is self.ve
        elif e == "s" and not (sem is not None and sem.dma):
            if self.ac.v > 0:
                self.w("s", self.ac, self.ac.v)
            if sem is None:
                sem = self.ac
            assert sem is self.ac
        if sem is None:
            self.ops[e].append(build)
            return None
        if sem.dma:
            n = 16
        sem.v += n
        self.ops[e].append(lambda E, b=build, s=sem, nn=n: b(E).then_inc(s.h, nn))
        return sem.v

    def replay(self, nc):
        with nc.Block() as blk:
            for name, attr in (("t", "tensor"), ("v", "vector"), ("s", "scalar"),
                               ("g", "gpsimd"), ("sp", "sync")):
                if self.ops[name]:
                    fns = self.ops[name]

                    def body(E, fns=fns):
                        for f in fns:
                            f(E)
                    getattr(blk, attr)(body)


def build_nc():
    nc = bass.Bass("TRN2", target_bir_lowering=False)

    xt = nc.dram_tensor("xt", [NB, 4, 128, DC, 512], F32, kind="ExternalInput")
    xo = nc.dram_tensor("xo", [TC, 128, D], F32, kind="ExternalInput")
    wqk = nc.dram_tensor("wqk", [H2, DC, 128, 256], F32, kind="ExternalInput")
    wv = nc.dram_tensor("wv", [DC, 128, 256], F32, kind="ExternalInput")
    wout = nc.dram_tensor("wout", [DC, 128, D], F32, kind="ExternalInput")
    wfc = nc.dram_tensor("wfc", [FG, DC, 128, 512], F32, kind="ExternalInput")
    wproj = nc.dram_tensor("wproj", [OG, 64, 128, 256], F32, kind="ExternalInput")
    tri = nc.dram_tensor("tri", [128, 128], BF16, kind="ExternalInput")
    out = nc.dram_tensor("out", [TC, 128, D], F32, kind="ExternalOutput")

    d_a2a_in = nc.dram_tensor("d_a2a_in", [8, H2, 128, 1024], BF16)
    d_a2a_out = nc.dram_tensor("d_a2a_out", [8, H2, 128, 1024], BF16)
    d_xhat = nc.dram_tensor("d_xhat", [1024, D], BF16)
    d_hhat = nc.dram_tensor("d_hhat", [TC, 128, DFF], BF16)
    d_x1 = nc.dram_tensor("d_x1", [TC, 128, D], F32)
    d_proj = nc.dram_tensor("d_proj", [TC, 128, D], F32)

    mm = Sem(nc, "mm")
    ve = Sem(nc, "ve")
    ac = Sem(nc, "ac")
    cc = Sem(nc, "cc")
    po = Sem(nc, "po")
    sq = Sem(nc, "sq", dma=True)
    q_wq = Sem(nc, "q_wq", dma=True)
    q_tri = Sem(nc, "q_tri", dma=True)
    q_xq = [Sem(nc, "q_xq0", dma=True), Sem(nc, "q_xq1", dma=True)]
    q_gb = [Sem(nc, "q_gb0", dma=True), Sem(nc, "q_gb1", dma=True)]
    q_av = [Sem(nc, "q_av0", dma=True), Sem(nc, "q_av1", dma=True)]
    q_lda = Sem(nc, "q_lda", dma=True)
    q_ldx = Sem(nc, "q_ldx", dma=True)
    q_wo = [Sem(nc, "q_wo0", dma=True), Sem(nc, "q_wo1", dma=True)]
    q_xh = [Sem(nc, "q_xh0", dma=True), Sem(nc, "q_xh1", dma=True)]
    q_tr = Sem(nc, "q_tr", dma=True)
    q_wc = [Sem(nc, "q_wc0", dma=True), Sem(nc, "q_wc1", dma=True)]
    q_hh = Sem(nc, "q_hh", dma=True)
    q_ps = [Sem(nc, "q_ps0", dma=True), Sem(nc, "q_ps1", dma=True)]
    q_fn = [Sem(nc, "q_fn0", dma=True), Sem(nc, "q_fn1", dma=True)]
    q_st = [Sem(nc, "q_st0", dma=True), Sem(nc, "q_st1", dma=True)]

    ctx0 = ExitStack()
    sb = lambda n, s, d: ctx0.enter_context(nc.sbuf_tensor(n, s, d))

    ones_b = sb("ones_b", [128, 128], BF16)
    onesf_col = sb("onesf_col", [128, 1], F32)
    onesf_row = sb("onesf_row", [1, 128], F32)
    tri_s = sb("tri_s", [128, 128], BF16)
    eps_ln = sb("eps_ln", [128, 1], F32)
    eps_bl = sb("eps_bl", [128, 1], F32)
    gacc_f = sb("gacc_f", [128, 1], F32)
    gacc_p = sb("gacc_p", [128, 1], F32)
    gpart = sb("gpart", [128, 2], F32)
    tpos_f = sb("tpos_f", [128, 1], F32)
    tneg_f = sb("tneg_f", [128, 1], F32)
    gneg_f = sb("gneg_f", [128, 1], F32)
    tpos_p = sb("tpos_p", [128, 1], F32)
    tneg_p = sb("tneg_p", [128, 1], F32)
    gneg_p = sb("gneg_p", [128, 1], F32)
    sc11 = sb("sc11", [1, 1], F32)
    st_m = sb("st_m", [128, 1], F32)
    st_sq = sb("st_sq", [128, 1], F32)
    st_var = sb("st_var", [128, 1], F32)
    st_rstd = sb("st_rstd", [128, 1], F32)
    st_mb = sb("st_mb", [128, 1], F32)
    st_g = sb("st_g", [128, 1], F32)
    st_gp = sb("st_gp", [128, 1], F32)
    st_gn = sb("st_gn", [128, 1], F32)
    st_lo = sb("st_lo", [128, 1], F32)
    st_hi = sb("st_hi", [128, 1], F32)
    st_d1 = sb("st_d1", [128, 1], F32)
    st_d2 = sb("st_d2", [128, 1], F32)
    st8_m = sb("st8_m", [128, TC], F32)
    st8_sq = sb("st8_sq", [128, TC], F32)
    st8_var = sb("st8_var", [128, TC], F32)
    st8_d1 = sb("st8_d1", [128, TC], F32)
    st8_d2 = sb("st8_d2", [128, TC], F32)
    st8_r = sb("st8_r", [128, TC], F32)
    st8_c = sb("st8_c", [128, TC], F32)
    st8_lo = sb("st8_lo", [128, TC], F32)
    st8_hi = sb("st8_hi", [128, TC], F32)
    st8_nm = sb("st8_nm", [128, TC], F32)

    psum = ExitStack()
    pbig = psum.enter_context(nc.psum_tensor("pbig", [128, 2048], F32))
    pA = psum.enter_context(nc.psum_tensor("pA", [128, 512], F32))
    pB = psum.enter_context(nc.psum_tensor("pB", [128, 512], F32))
    pC = psum.enter_context(nc.psum_tensor("pC", [128, 512], F32))
    pD = psum.enter_context(nc.psum_tensor("pD", [128, 512], F32))

    cp = {}
    P = None

    def ln_stats(src_ap, eps_tile, dump_ap, rsem, rval):
        n = src_ap.shape[-1]
        P.w("v", rsem, rval)
        P.o("v", lambda v, a=src_ap: v.reduce_sum(st_m[:, 0:1], a, AX.X), ve)
        c2 = P.o("v", lambda v, a=src_ap, d=dump_ap: v.scalar_tensor_tensor(
            d, a, 1.0, a, op0=ALU.bypass, op1=ALU.mult,
            accum_out=st_sq[:, 0:1]), ve)
        P.w("s", ve, c2)
        P.o("s", lambda s, nn=n: s.activation(
            st_m[:, 0:1], st_m[:, 0:1], AF.Identity, scale=1.0 / nn), ac)
        P.o("s", lambda s, nn=n: s.activation(
            st_sq[:, 0:1], st_sq[:, 0:1], AF.Identity, scale=1.0 / nn), ac)
        c4 = P.o("s", lambda s: s.activation(
            st_var[:, 0:1], st_m[:, 0:1], AF.Square), ac)
        P.w("v", ac, c4)
        c5 = P.o("v", lambda v: v.tensor_sub(
            st_var[:, 0:1], st_var[:, 0:1], st_sq[:, 0:1]), ve)
        P.w("s", ve, c5)
        P.o("s", lambda s, e=eps_tile: s.activation(
            st_rstd[:, 0:1], st_var[:, 0:1], AF.Ln,
            bias=e[:, 0:1], scale=-1.0), ac)
        crs = P.o("s", lambda s: s.activation(
            st_rstd[:, 0:1], st_rstd[:, 0:1], AF.Exp, scale=-0.5), ac)
        P.w("v", ac, crs)
        cmb = P.o("v", lambda v: v.tensor_scalar(
            st_mb[:, 0:1], st_m[:, 0:1], st_rstd[:, 0:1], -1.0,
            op0=ALU.mult, op1=ALU.mult), ve)
        return cmb

    ghist = []

    def gamma_chunk(gacc, gbuf_t, gdump_t, src_ap):
        idx = len(ghist)
        slot = idx % 2
        if idx >= 2:
            P.w("sp", ac, ghist[-2][0])
        ldv = P.o("sp", lambda sp, a=src_ap, b=slot, g=gbuf_t:
                  sp.dma_start(g[:, b, :].rearrange(
                      "p (a f) -> p a f", a=a.shape[1]), a), q_gb[slot])
        P.w("s", q_gb[slot], ldv)
        if idx >= 2:
            P.w("s", ve, ghist[-2][1])
        accp = P.o(
            "s", lambda s, b=slot, g=gbuf_t, d=gdump_t, ps=slot:
            s.activation(d[:, :], g[:, b, :], AF.Abs,
                         accum_out=gpart[:, ps:ps + 1]), ac)
        P.w("v", ac, accp)
        addp = P.o(
            "v", lambda v, ps=slot, ga=gacc:
            v.tensor_add(ga[:, 0:1], ga[:, 0:1], gpart[:, ps:ps + 1]), ve)
        ghist.append((accp, addp))
        cp[("gsum", id(gacc))] = ve.v

    def gamma_final(gacc, tpos, tneg, gneg):
        P.w("t", ve, ve.v)
        P.o("t", lambda t, ga=gacc: t.matmul(
            pD[0:1, 0:1], ga[:, 0:1], onesf_col[:, 0:1],
            start=True, stop=True), mm)
        s1 = mm.v
        P.w("s", mm, s1)
        s2 = P.o("s", lambda s: s.copy(sc11[:, :], pD[0:1, 0:1]), ac)
        P.w("t", ac, s2)
        P.o("t", lambda t: t.matmul(pD[:, 1:2], onesf_row[:, :], sc11[:, :],
                                    start=True, stop=True), mm)
        s3 = mm.v
        P.w("v", mm, s3)
        P.o("v", lambda v, tp=tpos: v.tensor_scalar(
            tp[:, 0:1], pD[:, 1:2], 0.5 / NWELEM, 0.5 * EPS_BL,
            op0=ALU.mult, op1=ALU.add), ve)
        P.o("v", lambda v, gn=gneg: v.tensor_scalar(
            gn[:, 0:1], pD[:, 1:2], -1.0 / NWELEM, None, op0=ALU.mult), ve)
        cfin = P.o("v", lambda v, tp=tpos, tn=tneg: v.tensor_scalar(
            tn[:, 0:1], tp[:, 0:1], -1.0, None, op0=ALU.mult), ve)
        cp[("gfin", id(gacc))] = cfin
        return cfin

    ctxqw = ExitStack()
    woutb = ctxqw.enter_context(nc.sbuf_tensor("woutb", [128, 4, DC, 512], BF16))
    ctxq = ExitStack()
    sbq = lambda n, s, d: ctxq.enter_context(nc.sbuf_tensor(n, s, d))
    xq = sbq("xq", [128, 2, DC, 512], BF16)
    wqk_s = sbq("wqk_s", [128, H2, DC, 256], BF16)
    wv_s = sbq("wv_s", [128, DC, 256], BF16)
    kT_s = sbq("kT_s", [128, H2, S], BF16)
    qT_s = sbq("qT_s", [128, H2, S], BF16)
    v_s = sbq("v_s", [128, 16, 256], BF16)
    pt = sbq("pt", [128, 4, 512], BF16)
    lln = sbq("lln", [128, 512], F32)
    lrec = sbq("lrec", [128, 512], F32)
    avst = sbq("avst", [128, 2, 512], BF16)
    gbuf = sbq("gbuf", [128, 2, 2048], F32)
    gdump = sbq("gdump", [128, 2048], F32)

    P = Prog(ve, ac)

    for f in (lambda g: g.memset(ones_b[:, :], 1.0),
              lambda g: g.memset(onesf_col[:, :], 1.0),
              lambda g: g.memset(onesf_row[:, :], 1.0),
              lambda g: g.memset(eps_ln[:, :], EPS_LN),
              lambda g: g.memset(eps_bl[:, :], EPS_BL),
              lambda g: g.memset(gacc_f[:, :], 0.0)):
        P.o("g", f)
    CONST_READY = P.o("g", lambda g: g.memset(gacc_p[:, :], 0.0), cc)
    P.o("g", lambda g: g.dma_start(
        wqk_s[:, :, :, :], wqk.ap().rearrange("h dc p f -> p h dc f")), q_wq)
    WQKV_LD = P.o("g", lambda g: g.dma_start(
        wv_s[:, :, :], wv.ap().rearrange("dc p f -> p dc f")), q_wq)
    TRI_LD = P.o("sp", lambda sp: sp.dma_start(tri_s[:, :], tri.ap()), q_tri)
    for e in ("t", "v", "s"):
        P.w(e, cc, CONST_READY)
    P.w("v", q_tri, TRI_LD)

    pair_hist = []
    avdma_hist = []
    qkv_ping = [0]

    NGCH = 64

    def wfc_chunk_ap(ch):
        fg, pr = divmod(ch, 4)
        return wfc.ap()[fg, 4 * pr:4 * pr + 4, :, :].rearrange(
            "dc p f -> p dc f")

    def wproj_chunk_ap(ch):
        og, fr = divmod(ch, 8)
        return wproj.ap()[og, 8 * fr:8 * fr + 8, :, :].rearrange(
            "fc p f -> p fc f")

    gamma_pos = [0]

    def emit_gamma(n):
        for _ in range(n):
            ch = gamma_pos[0]
            if ch >= 2 * NGCH:
                return
            if ch % 2 == 0:
                gamma_chunk(gacc_f, gbuf, gdump, wfc_chunk_ap(ch // 2))
            else:
                gamma_chunk(gacc_p, gbuf, gdump, wproj_chunk_ap(ch // 2))
            gamma_pos[0] += 1

    def q_batch(b):
        for i in range(4):
            gidx = b * 4 + i
            slot = gidx % 2
            if gidx >= 2:
                P.w("g", mm, cp[("xq_done", gidx - 2)])
            ldv = P.o("g", lambda g, bb=b, ii=i: g.dma_start(
                xq[:, ii % 2, :, :], xt.ap()[bb, ii, :, :, :]), q_xq[slot])
            P.w("t", q_xq[slot], ldv)
            if gidx == 0:
                P.w("t", q_wq, WQKV_LD)
            for h2 in range(H2):
                for qk in range(2):
                    dst = (qT_s, kT_s)[qk]
                    bank = (pC, pD)[qkv_ping[0] % 2]
                    qkv_ping[0] += 1
                    P.w("t", ve, cp.get(("evac", id(bank))))
                    for dc in range(DC):
                        P.o("t", lambda t, bk=bank, h=h2, q=qk, d=dc, ii=i:
                            t.matmul(bk[:, 0:512],
                                     wqk_s[:, h, d, q * 128:(q + 1) * 128],
                                     xq[:, ii % 2, d, :],
                                     start=(d == 0), stop=(d == DC - 1)),
                            mm if dc == DC - 1 else None)
                    P.w("v", mm, mm.v)
                    cp[("evac", id(bank))] = P.o(
                        "v", lambda v, bk=bank, ds=dst, h=h2, ii=i:
                        v.tensor_copy(ds[:, h, ii * 512:(ii + 1) * 512],
                                      bk[:, 0:512]), ve)
            for t128 in range(4):
                bank = (pC, pD)[qkv_ping[0] % 2]
                qkv_ping[0] += 1
                P.w("t", ve, cp.get(("evac", id(bank))))
                for dc in range(DC):
                    P.o("t", lambda t, bk=bank, d=dc, ii=i, tk=t128:
                        t.matmul(bk[:, 0:256],
                                 xq[:, ii % 2, d, tk * 128:(tk + 1) * 128],
                                 wv_s[:, d, :],
                                 start=(d == 0), stop=(d == DC - 1)),
                        mm if dc == DC - 1 else None)
                P.w("v", mm, mm.v)
                cp[("evac", id(bank))] = P.o(
                    "v", lambda v, bk=bank, kb=i * 4 + t128:
                    v.tensor_copy(v_s[:, kb, :], bk[:, 0:256]), ve)
            cp[("xq_done", gidx)] = mm.v
        qkv_done = ve.v

        for h2 in range(H2):
            P.w("t", ve, qkv_done)
            for s4 in range(4):
                kbN = 4 * (s4 + 1)
                for pr in range(kbN // 2):
                    k0, k1 = 2 * pr, 2 * pr + 1
                    if len(pair_hist) >= 2:
                        P.w("t", ac, pair_hist[-2][0])
                    for kb in (k0, k1):
                        qq = kb - 4 * s4
                        sfx = 128 * qq if qq >= 0 else 0
                        P.o("t", lambda t, h=h2, s=s4, kkb=kb, sf=sfx:
                            t.matmul(pbig[:, (kkb % 4) * 512 + sf:
                                             (kkb % 4) * 512 + 512],
                                     kT_s[:, h, kkb * 128:(kkb + 1) * 128],
                                     qT_s[:, h, s * 512 + sf:(s + 1) * 512],
                                     start=True, stop=True),
                            mm if kb == k1 else None)
                    cps = mm.v
                    P.w("s", mm, cps)
                    if len(pair_hist) >= 2:
                        P.w("s", mm, pair_hist[-2][1])
                    cpe = P.o("s", lambda s, kk=k0: s.activation(
                        pt[:, kk % 4:kk % 4 + 2, :],
                        pbig[:, (kk % 4) * 512:(kk % 4) * 512 + 1024],
                        AF.Exp, scale=INV_SQRT_HD), ac)
                    cpt = None
                    for kb in (k0, k1):
                        qq = kb - 4 * s4
                        if 0 <= qq <= 3:
                            P.w("v", ac, cpe)
                            cpt = P.o("v", lambda v, kkb=kb, q=qq:
                                      v.tensor_mul(
                                          pt[:, kkb % 4, q * 128:(q + 1) * 128],
                                          pt[:, kkb % 4, q * 128:(q + 1) * 128],
                                          tri_s[:, :]), ve)
                    P.w("t", ac, cpe)
                    if cpt is not None:
                        P.w("t", ve, cpt)
                    if pr == 0:
                        P.w("t", ve, cp.get("avl_used"))
                        P.w("t", ac, cp.get("l_used"))
                    for kb in (k0, k1):
                        qq = kb - 4 * s4
                        sfx = 128 * qq if qq >= 0 else 0
                        fst, lst = (kb == 0), (kb == kbN - 1)
                        P.o("t", lambda t, kkb=kb, h=h2, sf=sfx, f=fst, l=lst:
                            t.matmul(pA[:, sf:512],
                                     v_s[:, kkb, h * 128:(h + 1) * 128],
                                     pt[:, kkb % 4, sf:512],
                                     start=f, stop=l), None)
                        P.o("t", lambda t, kkb=kb, sf=sfx, f=fst, l=lst:
                            t.matmul(pB[:, sf:512], ones_b[:, :],
                                     pt[:, kkb % 4, sf:512],
                                     start=f, stop=l),
                            mm if kb == k1 else None)
                    pair_hist.append((cpe, mm.v))
                P.w("s", mm, pair_hist[-1][1])
                cp["l_used"] = P.o("s", lambda s: s.activation(
                    lln[:, :], pB[:, 0:512], AF.Ln), ac)
                clr = P.o("s", lambda s: s.activation(
                    lrec[:, :], lln[:, :], AF.Exp, scale=-1.0), ac)
                P.w("v", ac, clr)
                if len(avdma_hist) >= 2:
                    P.w("v", avdma_hist[-2][0], avdma_hist[-2][1])
                cp["avl_used"] = P.o("v", lambda v, s=s4: v.tensor_mul(
                    avst[:, s % 2, :], pA[:, 0:512], lrec[:, :]), ve)
                P.w("sp", ve, cp["avl_used"])
                j = 2 * b + (s4 // 2)
                off = (s4 % 2) * 512
                aslot = len(avdma_hist) % 2
                dv = P.o("sp", lambda sp, jj=j, h=h2, of=off, s=s4:
                         sp.dma_start(d_a2a_in.ap()[jj, h, :, of:of + 512],
                                      avst[:, s % 2, :]), q_av[aslot])
                avdma_hist.append((q_av[aslot], dv))

    for b in range(NB):
        q_batch(b)
        P.o("g", lambda g, o=b: g.dma_start(
            woutb[:, o, :, :],
            wout.ap()[:, :, o * 512:(o + 1) * 512].rearrange(
                "dvc p f -> p dvc f")), q_wo[b % 2])
        emit_gamma(32)
    emit_gamma(2 * NGCH)
    gamma_final(gacc_f, tpos_f, tneg_f, gneg_f)
    gamma_final(gacc_p, tpos_p, tneg_p, gneg_p)
    GFIN_P = cp[("gfin", id(gacc_p))]

    P.replay(nc)
    ctxq.close()
    GBF_END = (q_gb[0].v, q_gb[1].v)
    GAM_ABS_END = ghist[-1][0]
    VE_Q_END = ve.v
    AC_Q_END = ac.v

    ctxw = ExitStack()
    sbw = lambda n, s, d: ctxw.enter_context(nc.sbuf_tensor(n, s, d))
    avT_all = sbw("avT_all", [128, 16, 1024], BF16)
    stream = sbw("stream", [128, TC, D], F32)
    xn_st = sbw("xn_st", [128, D], F32)
    xh_st = sbw("xh_st", [128, 2, D], BF16)

    P = Prog(ve, ac)

    P.w("g", q_av[0], q_av[0].v)
    P.w("g", q_av[1], q_av[1].v)

    def _cc_call(g):
        with allow_ici_all_to_all():
            return g.collective_compute(
                "AllToAll", ALU.bypass, replica_groups=[list(range(8))],
                ins=[d_a2a_in.ap().rearrange("a b c d -> a (b c d)")],
                outs=[d_a2a_out.ap().rearrange("a b c d -> a (b c d)")])

    CC_DONE = P.o("g", _cc_call, cc)

    cp[("evac", id(pD))] = cp[("gfin", id(gacc_p))]
    cp[("evac", id(pC))] = None

    P.w("sp", cc, CC_DONE)
    P.w("sp", q_gb[0], GBF_END[0])
    P.w("sp", q_gb[1], GBF_END[1])
    P.w("sp", ac, GAM_ABS_END)
    for j in range(8):
        for h2 in range(H2):
            AVT_LD = P.o("sp", lambda sp, jj=j, h=h2: sp.dma_start(
                avT_all[:, jj * 2 + h, :], d_a2a_out.ap()[jj, h, :, :]), q_lda)
    for tc in range(TC):
        XO_LD = P.o("sp", lambda sp, t=tc: sp.dma_start(
            stream[:, t, :], xo.ap()[t, :, :]), q_ldx)

    P.w("t", ve, VE_Q_END)
    P.w("t", ac, AC_Q_END)
    P.w("t", q_lda, AVT_LD)
    P.w("t", q_wo[0], q_wo[0].v)
    P.w("t", q_wo[1], q_wo[1].v)
    cxh_prev = None
    for tc in range(TC):
        even = (tc % 2 == 0)
        banks = ([lambda o: pbig[:, o * 512:(o + 1) * 512]] if even
                 else [lambda o: (pA, pB, pC, pD)[o][:, 0:512]])[0]
        P.w("t", ve, cp.get(("wevac", even)))
        for og in range(4):
            for dvc in range(DC):
                P.o("t", lambda t, bk=banks(og), d=dvc, tt=tc, o=og:
                    t.matmul(bk,
                             avT_all[:, d, tt * 128:(tt + 1) * 128],
                             woutb[:, o, d, :],
                             start=(d == 0), stop=(d == DC - 1)),
                    mm if (og == 3 and dvc == DC - 1) else None)
        mv = mm.v
        P.w("v", mm, mv)
        if tc == 0:
            P.w("v", q_ldx, XO_LD)
        if even:
            cev = P.o("v", lambda v, tt=tc: v.tensor_add(
                stream[:, tt, :], pbig[:, 0:2048], stream[:, tt, :]), ve)
        else:
            for og in range(4):
                cev = P.o("v", lambda v, tt=tc, o=og: v.tensor_add(
                    stream[:, tt, o * 512:(o + 1) * 512],
                    (pA, pB, pC, pD)[o][:, 0:512],
                    stream[:, tt, o * 512:(o + 1) * 512]), ve)
        cp[("wevac", even)] = cev

        if cxh_prev is not None:
            P.w("s", ve, cxh_prev)
        cmb = ln_stats(stream[:, tc, :], eps_ln, xn_st[:, :], ve, cev)
        P.w("s", ve, cmb)
        cx1 = P.o("s", lambda s, t=tc: s.activation(
            stream[:, t, :], stream[:, t, :], AF.Identity,
            bias=st_mb[:, 0:1], scale=st_rstd[:, 0:1]), ac)
        cmb2 = ln_stats(stream[:, tc, :], eps_bl, xn_st[:, :], ac, cx1)
        P.w("s", ve, cmb2)
        cxn = P.o("s", lambda s, t=tc: s.activation(
            xn_st[:, :], stream[:, t, :], AF.Identity,
            bias=st_mb[:, 0:1], scale=st_rstd[:, 0:1]), ac)
        P.w("sp", ac, cxn)
        P.o("sp", lambda sp, t=tc: sp.dma_start(
            d_x1.ap()[t, :, :], stream[:, t, :]), sq)
        P.w("v", ac, cxn)
        cg1 = P.o("v", lambda v: v.reduce_max(st_g[:, 0:1], xn_st[:, :], AX.X,
                                              apply_absolute_value=True), ve)
        P.w("s", ve, cg1)
        P.o("s", lambda s: s.activation(st_gp[:, 0:1], st_g[:, 0:1],
                                        AF.Identity, scale=CLIP_C), ac)
        cg2 = P.o("s", lambda s: s.activation(st_gn[:, 0:1], st_g[:, 0:1],
                                              AF.Identity, scale=-CLIP_C), ac)
        P.w("v", ac, cg2)
        if tc >= 2:
            P.w("v", q_xh[tc % 2], cp[("xhat_dma", tc - 2)])
        cxh = P.o("v", lambda v, t=tc: v.tensor_scalar(
            xh_st[:, t % 2, :], xn_st[:, :], st_gn[:, 0:1], st_gp[:, 0:1],
            op0=ALU.max, op1=ALU.min), ve)
        cxh_prev = cxh
        P.w("sp", ve, cxh)
        cp[("xhat_dma", tc)] = P.o("sp", lambda sp, t=tc: sp.dma_start(
            d_xhat.ap()[t * 128:(t + 1) * 128, :], xh_st[:, t % 2, :]),
            q_xh[tc % 2])
    X1_SPILLED = sq.v

    P.replay(nc)
    ctxw.close()
    ctxqw.close()

    ctxh = ExitStack()
    htens = ctxh.enter_context(nc.sbuf_tensor("htens", [128, TC, DFF], BF16))
    hsump = ctxh.enter_context(nc.sbuf_tensor("hsump", [128, TC, FG], F32))
    hsqp = ctxh.enter_context(nc.sbuf_tensor("hsqp", [128, TC, FG], F32))

    ctxf1 = ExitStack()
    sbf1 = lambda n, s, d: ctxf1.enter_context(nc.sbuf_tensor(n, s, d))
    xhatT = sbf1("xhatT", [128, DC, 1024], BF16)
    ternf = sbf1("ternf", [128, 2, DC, 512], BF16)
    wch = sbf1("wch", [128, 2, 1024], F32)
    tsa_t = sbf1("tsa_t", [128, 1024], F32)
    tsa = tsa_t[:, :]

    P = Prog(ve, ac)
    P.w("sp", q_xh[0], q_xh[0].v)
    P.w("sp", q_xh[1], q_xh[1].v)
    P.w("s", sq, X1_SPILLED)
    P.w("s", q_xh[0], q_xh[0].v)
    P.w("s", q_xh[1], q_xh[1].v)
    for dc in range(DC):
        XT_LD = P.o("sp", lambda sp, d=dc: sp.dma_start(
            xhatT[:, d, :], d_xhat.ap()[:, d * 128:(d + 1) * 128],
            transpose=True), q_tr)

    for fg in range(FG):
        for chh in range(8):
            gidx = fg * 8 + chh
            slot = gidx % 2
            if gidx >= 2:
                P.w("sp", ve, cp[("ternf_ch", gidx - 2)])
            ldv = P.o("sp", lambda sp, f=fg, c=chh, b=slot: sp.dma_start(
                wch[:, b, :].rearrange("p (a f) -> p a f", a=2),
                wfc.ap()[f, 2 * c:2 * c + 2, :, :].rearrange(
                    "dc p f -> p dc f")), q_wc[slot])
            P.w("v", q_wc[slot], ldv)
            if fg >= 2 and chh == 0:
                P.w("v", mm, cp[("fc_mm", fg - 2)])
            P.o("v", lambda v, b=slot: v.tensor_scalar(
                tsa, wch[:, b, :], tpos_f[:, 0:1], None,
                op0=ALU.is_ge), ve)
            cp[("ternf_ch", gidx)] = P.o(
                "v", lambda v, b=slot, f=fg, c=chh:
                v.scalar_tensor_tensor(
                    ternf[:, f % 2, 2 * c:2 * c + 2, :].rearrange(
                        "p a f -> p (a f)"),
                    wch[:, b, :], tneg_f[:, 0:1], tsa,
                    op0=ALU.is_le, op1=ALU.subtract), ve)
        tern_done = ve.v

        P.w("t", ve, tern_done)
        if fg == 0:
            P.w("t", q_tr, XT_LD)
        for tc in range(TC):
            bank = (pC, pD)[tc % 2]
            P.w("t", ac, cp.get(("fc_evac", id(bank))))
            for dc in range(DC):
                P.o("t", lambda t, bk=bank, d=dc, tt=tc, f=fg:
                    t.matmul(bk[:, 0:512],
                             xhatT[:, d, tt * 128:(tt + 1) * 128],
                             ternf[:, f % 2, d, :],
                             start=(d == 0), stop=(d == DC - 1)),
                    mm if dc == DC - 1 else None)
            P.w("s", mm, mm.v)
            cp[("fc_evac", id(bank))] = P.o(
                "s", lambda s, bk=bank, tt=tc, f=fg: s.activation(
                    htens[:, tt, f * 512:(f + 1) * 512], bk[:, 0:512],
                    AF.Gelu, scale=gneg_f[:, 0:1],
                    accum_out=hsump[:, tt, f:f + 1]), ac)
            csq = P.o("s", lambda s, tt=tc, f=fg: s.activation(
                pbig[:, 0:512], htens[:, tt, f * 512:(f + 1) * 512],
                AF.Square, accum_out=hsqp[:, tt, f:f + 1]), ac)

        cp[("fc_mm", fg)] = mm.v
    F1_DONE_AC = ac.v
    F1_DONE_VE = ve.v

    P.replay(nc)
    ctxf1.close()

    ctx15 = ExitStack()
    hn_st = ctx15.enter_context(nc.sbuf_tensor("hn_st", [128, DFF], F32))
    hh_st = ctx15.enter_context(nc.sbuf_tensor("hh_st", [128, DFF], BF16))
    q_tr2 = Sem(nc, "q_tr2", dma=True)

    P = Prog(ve, ac)
    P.w("v", ac, F1_DONE_AC)
    for tc in range(TC):
        P.o("v", lambda v, t=tc: v.reduce_sum(
            st8_m[:, t:t + 1], hsump[:, t, :], AX.X), ve)
        P.o("v", lambda v, t=tc: v.reduce_sum(
            st8_sq[:, t:t + 1], hsqp[:, t, :], AX.X), ve)
    c2 = ve.v
    P.w("s", ve, c2)
    P.o("s", lambda s: s.activation(
        st8_m[:, :], st8_m[:, :], AF.Identity, scale=1.0 / DFF), ac)
    P.o("s", lambda s: s.activation(
        st8_sq[:, :], st8_sq[:, :], AF.Identity, scale=1.0 / DFF), ac)
    c4 = P.o("s", lambda s: s.activation(
        st8_var[:, :], st8_m[:, :], AF.Square), ac)
    P.w("v", ac, c4)
    c5 = P.o("v", lambda v: v.tensor_sub(st8_var[:, :], st8_var[:, :],
                                         st8_sq[:, :]), ve)
    P.w("s", ve, c5)
    P.o("s", lambda s: s.activation(
        st8_r[:, :], st8_var[:, :], AF.Ln,
        bias=eps_bl[:, 0:1], scale=-1.0), ac)
    c6 = P.o("s", lambda s: s.activation(
        st8_r[:, :], st8_r[:, :], AF.Exp, scale=-0.5), ac)
    P.w("v", ac, c6)
    c_nm = P.o("v", lambda v: v.scalar_tensor_tensor(
        st8_nm[:, :], st8_m[:, :], -1.0, st8_r[:, :],
        op0=ALU.mult, op1=ALU.mult), ve)

    for tc in range(TC):
        if tc == 0:
            P.w("s", ve, c_nm)
        if tc >= 1:
            P.w("s", ve, cp[("f15_clip", tc - 1)])
        chn = P.o("s", lambda s, t=tc: s.activation(
            hn_st[:, :], htens[:, t, :], AF.Identity,
            bias=st8_nm[:, t:t + 1], scale=st8_r[:, t:t + 1]), ac)
        P.w("v", ac, chn)
        cg1 = P.o("v", lambda v: v.reduce_max(st_g[:, 0:1], hn_st[:, :], AX.X,
                                              apply_absolute_value=True), ve)
        P.w("s", ve, cg1)
        P.o("s", lambda s: s.activation(st_gp[:, 0:1], st_g[:, 0:1],
                                        AF.Identity, scale=CLIP_C), ac)
        cg2 = P.o("s", lambda s: s.activation(st_gn[:, 0:1], st_g[:, 0:1],
                                              AF.Identity, scale=-CLIP_C), ac)
        P.w("v", ac, cg2)
        if tc >= 1:
            P.w("v", q_hh, cp[("hhat_dma", tc - 1)])
        chh2 = P.o("v", lambda v: v.tensor_scalar(
            hh_st[:, :], hn_st[:, :], st_gn[:, 0:1], st_gp[:, 0:1],
            op0=ALU.max, op1=ALU.min), ve)
        cp[("f15_clip", tc)] = chh2
        P.w("sp", ve, chh2)
        chd = P.o("sp", lambda sp, t=tc: sp.dma_start(
            d_hhat.ap()[t, :, :], hh_st[:, :]), q_hh)
        cp[("hhat_dma", tc)] = chd
        P.w("sp", ac, chn)
        P.w("sp", q_hh, chd)
        P.w("s", q_hh, chd)
        for fc in range(64):
            eng = "sp" if fc % 2 == 0 else "s"
            qq_tr = q_tr if fc % 2 == 0 else q_tr2
            P.o(eng, lambda E, t=tc, f=fc: E.dma_start(
                htens[:, t, f * 128:(f + 1) * 128],
                d_hhat.ap()[t, :, f * 128:(f + 1) * 128],
                transpose=True), qq_tr)
    F15_DONE = q_tr.v
    F15_DONE2 = q_tr2.v

    P.replay(nc)
    ctx15.close()

    ctxf2 = ExitStack()
    sbf2 = lambda n, s, d: ctxf2.enter_context(nc.sbuf_tensor(n, s, d))
    ternp = sbf2("ternp", [128, 2, 64, 256], BF16)
    wch2 = sbf2("wch2", [128, 2, 1024], F32)
    pst = sbf2("pst", [128, 2, 512], F32)
    tsa2 = pbig[:, 1024:2048]

    P = Prog(ve, ac)
    P.w("v", q_hh, q_hh.v)
    pst_hist = []
    for pg in range(OG // 2):
        for chh in range(32):
            og = 2 * pg + chh // 16
            c = chh % 16
            tslot = chh // 16
            gidx = pg * 32 + chh
            wslot = gidx % 2
            if gidx >= 2:
                P.w("sp", ve, cp[("ternp_ch", gidx - 2)])
            if pg >= 1 and chh == 0:
                P.w("sp", mm, cp[("proj_mm", pg - 1)])
            ldv = P.o("sp", lambda sp, o=og, cx=c, b=wslot: sp.dma_start(
                wch2[:, b, :].rearrange("p (a f) -> p a f", a=4),
                wproj.ap()[o, 4 * cx:4 * cx + 4, :, :].rearrange(
                    "fc p f -> p fc f")), q_wc[wslot])
            P.w("v", q_wc[wslot], ldv)
            P.o("v", lambda v, b=wslot: v.tensor_scalar(
                tsa2, wch2[:, b, :], tpos_p[:, 0:1], None,
                op0=ALU.is_ge), ve)
            cp[("ternp_ch", gidx)] = P.o(
                "v", lambda v, b=wslot, ts=tslot, cx=c:
                v.scalar_tensor_tensor(
                    ternp[:, ts, 4 * cx:4 * cx + 4, :].rearrange(
                        "p a f -> p (a f)"),
                    wch2[:, b, :], tneg_p[:, 0:1], tsa2,
                    op0=ALU.is_le, op1=ALU.subtract), ve)
        ternp_done = ve.v

        P.w("t", ve, ternp_done)
        if pg == 0:
            P.w("t", q_tr, F15_DONE)
            P.w("t", q_tr2, F15_DONE2)
        for tc in range(TC):
            bank = (pC, pD)[tc % 2]
            P.w("t", ve, cp.get(("pj_evac", id(bank))))
            for fc in range(64):
                P.o("t", lambda t, bk=bank, f=fc, tt=tc:
                    t.matmul(bk[:, 0:512],
                             htens[:, tt, f * 128:(f + 1) * 128],
                             ternp[:, :, f, :],
                             start=(f == 0), stop=(f == 63)),
                    mm if fc == 63 else None)
            P.w("v", mm, mm.v)
            if len(pst_hist) >= 2:
                P.w("v", pst_hist[-2][0], pst_hist[-2][1])
            cpe = P.o("v", lambda v, bk=bank, tt=tc: v.tensor_scalar(
                pst[:, tt % 2, :], bk[:, 0:512], gneg_p[:, 0:1], None,
                op0=ALU.mult), ve)
            cp[("pj_evac", id(bank))] = cpe
            P.w("sp", ve, cpe)
            pslot = tc % 2
            dv = P.o("sp", lambda sp, tt=tc, p=pg: sp.dma_start(
                d_proj.ap()[tt, :, p * 512:(p + 1) * 512],
                pst[:, tt % 2, :]), q_ps[pslot])
            pst_hist.append((q_ps[pslot], dv))
        cp[("proj_mm", pg)] = mm.v

    P.replay(nc)
    ctxf2.close()
    ctxh.close()

    ctxn = ExitStack()
    sbn = lambda n, s, d: ctxn.enter_context(nc.sbuf_tensor(n, s, d))
    fa = sbn("fa", [128, 2, D], F32)
    fb = sbn("fb", [128, 2, D], F32)
    fsq = sbn("fsq", [128, D], F32)
    fo = sbn("fo", [128, 2, D], F32)

    P = Prog(ve, ac)
    P.w("sp", q_ps[0], q_ps[0].v)
    P.w("sp", q_ps[1], q_ps[1].v)
    P.w("sp", sq, X1_SPILLED)
    for tc in range(TC):
        fslot = tc % 2
        if tc >= 2:
            P.w("sp", ac, cp[("fin_ln", tc - 2)])
        P.o("sp", lambda sp, t=tc: sp.dma_start(
            fa[:, t % 2, :], d_proj.ap()[t, :, :]), q_fn[fslot])
        ldv = P.o("sp", lambda sp, t=tc: sp.dma_start(
            fb[:, t % 2, :], d_x1.ap()[t, :, :]), q_fn[fslot])
        P.w("v", q_fn[fslot], ldv)
        cadd = P.o("v", lambda v, t=tc: v.tensor_add(
            fa[:, t % 2, :], fa[:, t % 2, :], fb[:, t % 2, :]), ve)
        cmb = ln_stats(fa[:, tc % 2, :], eps_ln, fsq[:, :], ve, cadd)
        P.w("s", ve, cmb)
        if tc >= 2:
            P.w("s", q_st[tc % 2], cp[("out_dma", tc - 2)])
        cfo = P.o("s", lambda s, t=tc: s.activation(
            fo[:, t % 2, :], fa[:, t % 2, :], AF.Identity,
            bias=st_mb[:, 0:1], scale=st_rstd[:, 0:1]), ac)
        cp[("fin_ln", tc)] = cfo
        P.w("sp", ac, cfo)
        cp[("out_dma", tc)] = P.o("sp", lambda sp, t=tc: sp.dma_start(
            out.ap()[t, :, :], fo[:, t % 2, :]), q_st[tc % 2])
    P.w("sp", q_st[0], q_st[0].v)
    P.w("sp", q_st[1], q_st[1].v)

    P.replay(nc)
    ctxn.close()
    psum.close()
    ctx0.close()
    return nc


_NC_CACHE = None


def _get_nc():
    global _NC_CACHE
    if _NC_CACHE is None:
        _NC_CACHE = build_nc()
    return _NC_CACHE


def kernel(L0_attn_x, L0_attn_w_qkv, L0_attn_w_out, L0_ffn_w_fc, L0_ffn_w_proj):
    x = np.ascontiguousarray(np.asarray(L0_attn_x, dtype=np.float32))
    w_qkv = np.ascontiguousarray(np.asarray(L0_attn_w_qkv, dtype=np.float32))
    w_out = np.ascontiguousarray(np.asarray(L0_attn_w_out, dtype=np.float32))
    w_fc = np.ascontiguousarray(np.asarray(L0_ffn_w_fc, dtype=np.float32))
    w_proj = np.ascontiguousarray(np.asarray(L0_ffn_w_proj, dtype=np.float32))

    xt = np.ascontiguousarray(
        x.transpose(0, 2, 1).reshape(NB, DC, 128, 4, 512).transpose(0, 3, 2, 1, 4))
    wout_dev = np.ascontiguousarray(w_out.reshape(DC, 128, D))
    wfc_dev = np.ascontiguousarray(
        w_fc.T.reshape(DC, 128, FG, 512).transpose(2, 0, 1, 3))
    wproj_dev = np.ascontiguousarray(
        w_proj.T.reshape(64, 128, OG, 256).transpose(2, 0, 1, 3))
    tri_m = np.ascontiguousarray(
        (np.arange(128)[None, :] >= np.arange(128)[:, None]).astype(BF))

    in_maps = []
    for c in range(8):
        b, hf = divmod(c, 2)
        xo_c = np.ascontiguousarray(
            x[b, hf * 1024:(hf + 1) * 1024].reshape(TC, 128, D))
        heads = [2 * c, 2 * c + 1]
        wqk_c = np.ascontiguousarray(np.stack([
            np.concatenate([w_qkv[:, h * 128:(h + 1) * 128],
                            w_qkv[:, D + h * 128:D + (h + 1) * 128]], axis=1)
            for h in heads]).reshape(H2, DC, 128, 256))
        wv_c = np.ascontiguousarray(np.concatenate(
            [w_qkv[:, 2 * D + h * 128:2 * D + (h + 1) * 128] for h in heads],
            axis=1).reshape(DC, 128, 256))
        in_maps.append({
            "xt": xt, "xo": xo_c, "wqk": wqk_c, "wv": wv_c,
            "wout": wout_dev, "wfc": wfc_dev, "wproj": wproj_dev, "tri": tri_m,
        })

    nc = _get_nc()
    res = run_bass_kernel_spmd(nc, in_maps, core_ids=list(range(8)))

    y = np.empty((NB, S, D), dtype=np.float32)
    for c in range(8):
        b, hf = divmod(c, 2)
        y[b, hf * 1024:(hf + 1) * 1024] = res.results[c]["out"].reshape(1024, D)
    return y


# version 14
# speedup vs baseline: 1.1601x; 1.0039x over previous
import numpy as np
import ml_dtypes
from contextlib import ExitStack

import concourse.bass as bass
import concourse.mybir as mybir
from concourse.bass_utils import run_bass_kernel_spmd
from concourse.aot_env import allow_ici_all_to_all

F32 = mybir.dt.float32
BF16 = mybir.dt.bfloat16
AF = mybir.ActivationFunctionType
ALU = mybir.AluOpType
AX = mybir.AxisListType

BF = ml_dtypes.bfloat16

NB = 4
S = 2048
D = 2048
DC = 16
H2 = 2
TC = 8
DFF = 8192
FG = 16
OG = 8
INV_SQRT_HD = 0.08838834764831845
EPS_LN = 1e-5
EPS_BL = 1e-6
CLIP_C = (128.0 - 1e-6) / 128.0
NWELEM = float(DFF * D)


class Sem:
    __slots__ = ("h", "v", "name", "dma")

    def __init__(self, nc, name, dma=False):
        self.h = nc.alloc_semaphore(name)
        self.v = 0
        self.name = name
        self.dma = dma


class Prog:

    ENGS = ("t", "v", "s", "g", "sp")

    def __init__(self, ve_sem, ac_sem):
        self.ops = {e: [] for e in self.ENGS}
        self.ve = ve_sem
        self.ac = ac_sem

    def w(self, e, sem, v):
        if v is None or v <= 0:
            return
        self.ops[e].append(lambda E, s=sem, vv=v: E.wait_ge(s.h, vv))

    def o(self, e, build, sem=None, n=1):
        if e == "v" and not (sem is not None and sem.dma):
            if self.ve.v > 0:
                self.w("v", self.ve, self.ve.v)
            if sem is None:
                sem = self.ve
            assert sem is self.ve
        elif e == "s" and not (sem is not None and sem.dma):
            if self.ac.v > 0:
                self.w("s", self.ac, self.ac.v)
            if sem is None:
                sem = self.ac
            assert sem is self.ac
        if sem is None:
            self.ops[e].append(build)
            return None
        if sem.dma:
            n = 16
        sem.v += n
        self.ops[e].append(lambda E, b=build, s=sem, nn=n: b(E).then_inc(s.h, nn))
        return sem.v

    def replay(self, nc):
        with nc.Block() as blk:
            for name, attr in (("t", "tensor"), ("v", "vector"), ("s", "scalar"),
                               ("g", "gpsimd"), ("sp", "sync")):
                if self.ops[name]:
                    fns = self.ops[name]

                    def body(E, fns=fns):
                        for f in fns:
                            f(E)
                    getattr(blk, attr)(body)


def build_nc():
    nc = bass.Bass("TRN2", target_bir_lowering=False)

    xt = nc.dram_tensor("xt", [NB, 4, 128, DC, 512], F32, kind="ExternalInput")
    xo = nc.dram_tensor("xo", [TC, 128, D], F32, kind="ExternalInput")
    wqk = nc.dram_tensor("wqk", [H2, DC, 128, 256], F32, kind="ExternalInput")
    wv = nc.dram_tensor("wv", [DC, 128, 256], F32, kind="ExternalInput")
    wout = nc.dram_tensor("wout", [DC, 128, D], F32, kind="ExternalInput")
    wfc = nc.dram_tensor("wfc", [FG, DC, 128, 512], F32, kind="ExternalInput")
    wproj = nc.dram_tensor("wproj", [OG, 64, 128, 256], F32, kind="ExternalInput")
    tri = nc.dram_tensor("tri", [128, 128], BF16, kind="ExternalInput")
    out = nc.dram_tensor("out", [TC, 128, D], F32, kind="ExternalOutput")

    d_a2a_in = nc.dram_tensor("d_a2a_in", [8, H2, 128, 1024], BF16)
    d_a2a_out = nc.dram_tensor("d_a2a_out", [8, H2, 128, 1024], BF16)
    d_xhat = nc.dram_tensor("d_xhat", [1024, D], BF16)
    d_hhat = nc.dram_tensor("d_hhat", [TC, 128, DFF], BF16)
    d_x1 = nc.dram_tensor("d_x1", [TC, 128, D], F32)
    d_proj = nc.dram_tensor("d_proj", [TC, 128, D], F32)

    mm = Sem(nc, "mm")
    ve = Sem(nc, "ve")
    ac = Sem(nc, "ac")
    cc = Sem(nc, "cc")
    po = Sem(nc, "po")
    sq = Sem(nc, "sq", dma=True)
    q_wq = Sem(nc, "q_wq", dma=True)
    q_tri = Sem(nc, "q_tri", dma=True)
    q_xq = [Sem(nc, "q_xq0", dma=True), Sem(nc, "q_xq1", dma=True)]
    q_gb = [Sem(nc, "q_gb0", dma=True), Sem(nc, "q_gb1", dma=True)]
    q_av = [Sem(nc, "q_av0", dma=True), Sem(nc, "q_av1", dma=True)]
    q_lda = Sem(nc, "q_lda", dma=True)
    q_ldx = Sem(nc, "q_ldx", dma=True)
    q_wo = [Sem(nc, "q_wo0", dma=True), Sem(nc, "q_wo1", dma=True)]
    q_xh = [Sem(nc, "q_xh0", dma=True), Sem(nc, "q_xh1", dma=True)]
    q_tr = Sem(nc, "q_tr", dma=True)
    q_wc = [Sem(nc, "q_wc0", dma=True), Sem(nc, "q_wc1", dma=True)]
    q_hh = Sem(nc, "q_hh", dma=True)
    q_ps = [Sem(nc, "q_ps0", dma=True), Sem(nc, "q_ps1", dma=True)]
    q_fn = [Sem(nc, "q_fn0", dma=True), Sem(nc, "q_fn1", dma=True)]
    q_st = [Sem(nc, "q_st0", dma=True), Sem(nc, "q_st1", dma=True)]

    ctx0 = ExitStack()
    sb = lambda n, s, d: ctx0.enter_context(nc.sbuf_tensor(n, s, d))

    ones_b = sb("ones_b", [128, 128], BF16)
    onesf_col = sb("onesf_col", [128, 1], F32)
    onesf_row = sb("onesf_row", [1, 128], F32)
    tri_s = sb("tri_s", [128, 128], BF16)
    eps_ln = sb("eps_ln", [128, 1], F32)
    eps_bl = sb("eps_bl", [128, 1], F32)
    gacc_f = sb("gacc_f", [128, 1], F32)
    gacc_p = sb("gacc_p", [128, 1], F32)
    gpart = sb("gpart", [128, 2], F32)
    tpos_f = sb("tpos_f", [128, 1], F32)
    tneg_f = sb("tneg_f", [128, 1], F32)
    gneg_f = sb("gneg_f", [128, 1], F32)
    tpos_p = sb("tpos_p", [128, 1], F32)
    tneg_p = sb("tneg_p", [128, 1], F32)
    gneg_p = sb("gneg_p", [128, 1], F32)
    sc11 = sb("sc11", [1, 1], F32)
    st_m = sb("st_m", [128, 1], F32)
    st_sq = sb("st_sq", [128, 1], F32)
    st_var = sb("st_var", [128, 1], F32)
    st_rstd = sb("st_rstd", [128, 1], F32)
    st_mb = sb("st_mb", [128, 1], F32)
    st_g = sb("st_g", [128, 1], F32)
    st_gp = sb("st_gp", [128, 1], F32)
    st_gn = sb("st_gn", [128, 1], F32)
    st_lo = sb("st_lo", [128, 1], F32)
    st_hi = sb("st_hi", [128, 1], F32)
    st_d1 = sb("st_d1", [128, 1], F32)
    st_d2 = sb("st_d2", [128, 1], F32)
    st8_m = sb("st8_m", [128, TC], F32)
    st8_sq = sb("st8_sq", [128, TC], F32)
    st8_var = sb("st8_var", [128, TC], F32)
    st8_d1 = sb("st8_d1", [128, TC], F32)
    st8_d2 = sb("st8_d2", [128, TC], F32)
    st8_r = sb("st8_r", [128, TC], F32)
    st8_c = sb("st8_c", [128, TC], F32)
    st8_lo = sb("st8_lo", [128, TC], F32)
    st8_hi = sb("st8_hi", [128, TC], F32)
    st8_nm = sb("st8_nm", [128, TC], F32)

    psum = ExitStack()
    pbig = psum.enter_context(nc.psum_tensor("pbig", [128, 2048], F32))
    pA = psum.enter_context(nc.psum_tensor("pA", [128, 512], F32))
    pB = psum.enter_context(nc.psum_tensor("pB", [128, 512], F32))
    pC = psum.enter_context(nc.psum_tensor("pC", [128, 512], F32))
    pD = psum.enter_context(nc.psum_tensor("pD", [128, 512], F32))

    cp = {}
    P = None

    def ln_stats(src_ap, eps_tile, dump_ap, rsem, rval):
        n = src_ap.shape[-1]
        P.w("v", rsem, rval)
        P.o("v", lambda v, a=src_ap: v.reduce_sum(st_m[:, 0:1], a, AX.X), ve)
        c2 = P.o("v", lambda v, a=src_ap, d=dump_ap: v.scalar_tensor_tensor(
            d, a, 1.0, a, op0=ALU.bypass, op1=ALU.mult,
            accum_out=st_sq[:, 0:1]), ve)
        P.w("s", ve, c2)
        P.o("s", lambda s, nn=n: s.activation(
            st_m[:, 0:1], st_m[:, 0:1], AF.Identity, scale=1.0 / nn), ac)
        P.o("s", lambda s, nn=n: s.activation(
            st_sq[:, 0:1], st_sq[:, 0:1], AF.Identity, scale=1.0 / nn), ac)
        c4 = P.o("s", lambda s: s.activation(
            st_var[:, 0:1], st_m[:, 0:1], AF.Square), ac)
        P.w("v", ac, c4)
        c5 = P.o("v", lambda v: v.tensor_sub(
            st_var[:, 0:1], st_var[:, 0:1], st_sq[:, 0:1]), ve)
        P.w("s", ve, c5)
        P.o("s", lambda s, e=eps_tile: s.activation(
            st_rstd[:, 0:1], st_var[:, 0:1], AF.Ln,
            bias=e[:, 0:1], scale=-1.0), ac)
        crs = P.o("s", lambda s: s.activation(
            st_rstd[:, 0:1], st_rstd[:, 0:1], AF.Exp, scale=-0.5), ac)
        P.w("v", ac, crs)
        cmb = P.o("v", lambda v: v.tensor_scalar(
            st_mb[:, 0:1], st_m[:, 0:1], st_rstd[:, 0:1], -1.0,
            op0=ALU.mult, op1=ALU.mult), ve)
        return cmb

    ghist = []

    def gamma_chunk(gacc, gbuf_t, gdump_t, src_ap):
        idx = len(ghist)
        slot = idx % 2
        if idx >= 2:
            P.w("sp", ac, ghist[-2][0])
        ldv = P.o("sp", lambda sp, a=src_ap, b=slot, g=gbuf_t:
                  sp.dma_start(g[:, b, :].rearrange(
                      "p (a f) -> p a f", a=a.shape[1]), a), q_gb[slot])
        P.w("s", q_gb[slot], ldv)
        if idx >= 2:
            P.w("s", ve, ghist[-2][1])
        accp = P.o(
            "s", lambda s, b=slot, g=gbuf_t, d=gdump_t, ps=slot:
            s.activation(d[:, :], g[:, b, :], AF.Abs,
                         accum_out=gpart[:, ps:ps + 1]), ac)
        P.w("v", ac, accp)
        addp = P.o(
            "v", lambda v, ps=slot, ga=gacc:
            v.tensor_add(ga[:, 0:1], ga[:, 0:1], gpart[:, ps:ps + 1]), ve)
        ghist.append((accp, addp))
        cp[("gsum", id(gacc))] = ve.v

    def gamma_final(gacc, tpos, tneg, gneg):
        P.w("t", ve, ve.v)
        P.o("t", lambda t, ga=gacc: t.matmul(
            pD[0:1, 0:1], ga[:, 0:1], onesf_col[:, 0:1],
            start=True, stop=True), mm)
        s1 = mm.v
        P.w("s", mm, s1)
        s2 = P.o("s", lambda s: s.copy(sc11[:, :], pD[0:1, 0:1]), ac)
        P.w("t", ac, s2)
        P.o("t", lambda t: t.matmul(pD[:, 1:2], onesf_row[:, :], sc11[:, :],
                                    start=True, stop=True), mm)
        s3 = mm.v
        P.w("v", mm, s3)
        P.o("v", lambda v, tp=tpos: v.tensor_scalar(
            tp[:, 0:1], pD[:, 1:2], 0.5 / NWELEM, 0.5 * EPS_BL,
            op0=ALU.mult, op1=ALU.add), ve)
        P.o("v", lambda v, gn=gneg: v.tensor_scalar(
            gn[:, 0:1], pD[:, 1:2], -1.0 / NWELEM, None, op0=ALU.mult), ve)
        cfin = P.o("v", lambda v, tp=tpos, tn=tneg: v.tensor_scalar(
            tn[:, 0:1], tp[:, 0:1], -1.0, None, op0=ALU.mult), ve)
        cp[("gfin", id(gacc))] = cfin
        return cfin

    ctxqw = ExitStack()
    woutb = ctxqw.enter_context(nc.sbuf_tensor("woutb", [128, 4, DC, 512], BF16))
    ctxq = ExitStack()
    sbq = lambda n, s, d: ctxq.enter_context(nc.sbuf_tensor(n, s, d))
    xq = sbq("xq", [128, 2, DC, 512], BF16)
    wqk_s = sbq("wqk_s", [128, H2, DC, 256], BF16)
    wv_s = sbq("wv_s", [128, DC, 256], BF16)
    kT_s = sbq("kT_s", [128, H2, S], BF16)
    qT_s = sbq("qT_s", [128, H2, S], BF16)
    v_s = sbq("v_s", [128, 16, 256], BF16)
    pt = sbq("pt", [128, 4, 512], BF16)
    lln = sbq("lln", [128, 512], F32)
    lrec = sbq("lrec", [128, 512], F32)
    avst = sbq("avst", [128, 2, 512], BF16)
    gbuf = sbq("gbuf", [128, 2, 2048], F32)
    gdump = sbq("gdump", [128, 2048], F32)

    P = Prog(ve, ac)

    for f in (lambda g: g.memset(ones_b[:, :], 1.0),
              lambda g: g.memset(onesf_col[:, :], 1.0),
              lambda g: g.memset(onesf_row[:, :], 1.0),
              lambda g: g.memset(eps_ln[:, :], EPS_LN),
              lambda g: g.memset(eps_bl[:, :], EPS_BL),
              lambda g: g.memset(gacc_f[:, :], 0.0)):
        P.o("g", f)
    CONST_READY = P.o("g", lambda g: g.memset(gacc_p[:, :], 0.0), cc)
    P.o("g", lambda g: g.dma_start(
        wqk_s[:, :, :, :], wqk.ap().rearrange("h dc p f -> p h dc f")), q_wq)
    WQKV_LD = P.o("g", lambda g: g.dma_start(
        wv_s[:, :, :], wv.ap().rearrange("dc p f -> p dc f")), q_wq)
    TRI_LD = P.o("sp", lambda sp: sp.dma_start(tri_s[:, :], tri.ap()), q_tri)
    for e in ("t", "v", "s"):
        P.w(e, cc, CONST_READY)
    P.w("v", q_tri, TRI_LD)

    pair_hist = []
    avdma_hist = []
    qkv_ping = [0]
    QKV_BANKS = (pC, pD, pA, pB)

    NGCH = 64

    def wfc_chunk_ap(ch):
        fg, pr = divmod(ch, 4)
        return wfc.ap()[fg, 4 * pr:4 * pr + 4, :, :].rearrange(
            "dc p f -> p dc f")

    def wproj_chunk_ap(ch):
        og, fr = divmod(ch, 8)
        return wproj.ap()[og, 8 * fr:8 * fr + 8, :, :].rearrange(
            "fc p f -> p fc f")

    gamma_pos = [0]

    def emit_gamma(n):
        for _ in range(n):
            ch = gamma_pos[0]
            if ch >= 2 * NGCH:
                return
            if ch % 2 == 0:
                gamma_chunk(gacc_f, gbuf, gdump, wfc_chunk_ap(ch // 2))
            else:
                gamma_chunk(gacc_p, gbuf, gdump, wproj_chunk_ap(ch // 2))
            gamma_pos[0] += 1

    def q_batch(b):
        for i in range(4):
            gidx = b * 4 + i
            slot = gidx % 2
            if gidx >= 2:
                P.w("g", mm, cp[("xq_done", gidx - 2)])
            ldv = P.o("g", lambda g, bb=b, ii=i: g.dma_start(
                xq[:, ii % 2, :, :], xt.ap()[bb, ii, :, :, :]), q_xq[slot])
            P.w("t", q_xq[slot], ldv)
            if gidx == 0:
                P.w("t", q_wq, WQKV_LD)
            for h2 in range(H2):
                for qk in range(2):
                    dst = (qT_s, kT_s)[qk]
                    bank = QKV_BANKS[qkv_ping[0] % 4]
                    qkv_ping[0] += 1
                    P.w("t", ve, cp.get(("evac", id(bank))))
                    if bank in (pA, pB):
                        P.w("t", ve, cp.get("avl_used"))
                        P.w("t", ac, cp.get("l_used"))
                    for dc in range(DC):
                        P.o("t", lambda t, bk=bank, h=h2, q=qk, d=dc, ii=i:
                            t.matmul(bk[:, 0:512],
                                     wqk_s[:, h, d, q * 128:(q + 1) * 128],
                                     xq[:, ii % 2, d, :],
                                     start=(d == 0), stop=(d == DC - 1)),
                            mm if dc == DC - 1 else None)
                    P.w("v", mm, mm.v)
                    cp[("evac", id(bank))] = P.o(
                        "v", lambda v, bk=bank, ds=dst, h=h2, ii=i:
                        v.tensor_copy(ds[:, h, ii * 512:(ii + 1) * 512],
                                      bk[:, 0:512]), ve)
            for t128 in range(4):
                bank = QKV_BANKS[qkv_ping[0] % 4]
                qkv_ping[0] += 1
                P.w("t", ve, cp.get(("evac", id(bank))))
                if bank in (pA, pB):
                    P.w("t", ve, cp.get("avl_used"))
                    P.w("t", ac, cp.get("l_used"))
                for dc in range(DC):
                    P.o("t", lambda t, bk=bank, d=dc, ii=i, tk=t128:
                        t.matmul(bk[:, 0:256],
                                 xq[:, ii % 2, d, tk * 128:(tk + 1) * 128],
                                 wv_s[:, d, :],
                                 start=(d == 0), stop=(d == DC - 1)),
                        mm if dc == DC - 1 else None)
                P.w("v", mm, mm.v)
                cp[("evac", id(bank))] = P.o(
                    "v", lambda v, bk=bank, kb=i * 4 + t128:
                    v.tensor_copy(v_s[:, kb, :], bk[:, 0:256]), ve)
            cp[("xq_done", gidx)] = mm.v
        qkv_done = ve.v

        for h2 in range(H2):
            P.w("t", ve, qkv_done)
            for s4 in range(4):
                kbN = 4 * (s4 + 1)
                for pr in range(kbN // 2):
                    k0, k1 = 2 * pr, 2 * pr + 1
                    if len(pair_hist) >= 2:
                        P.w("t", ac, pair_hist[-2][0])
                    for kb in (k0, k1):
                        qq = kb - 4 * s4
                        sfx = 128 * qq if qq >= 0 else 0
                        P.o("t", lambda t, h=h2, s=s4, kkb=kb, sf=sfx:
                            t.matmul(pbig[:, (kkb % 4) * 512 + sf:
                                             (kkb % 4) * 512 + 512],
                                     kT_s[:, h, kkb * 128:(kkb + 1) * 128],
                                     qT_s[:, h, s * 512 + sf:(s + 1) * 512],
                                     start=True, stop=True),
                            mm if kb == k1 else None)
                    cps = mm.v
                    P.w("s", mm, cps)
                    if len(pair_hist) >= 2:
                        P.w("s", mm, pair_hist[-2][1])
                    cpe = P.o("s", lambda s, kk=k0: s.activation(
                        pt[:, kk % 4:kk % 4 + 2, :],
                        pbig[:, (kk % 4) * 512:(kk % 4) * 512 + 1024],
                        AF.Exp, scale=INV_SQRT_HD), ac)
                    cpt = None
                    for kb in (k0, k1):
                        qq = kb - 4 * s4
                        if 0 <= qq <= 3:
                            P.w("v", ac, cpe)
                            cpt = P.o("v", lambda v, kkb=kb, q=qq:
                                      v.tensor_mul(
                                          pt[:, kkb % 4, q * 128:(q + 1) * 128],
                                          pt[:, kkb % 4, q * 128:(q + 1) * 128],
                                          tri_s[:, :]), ve)
                    P.w("t", ac, cpe)
                    if cpt is not None:
                        P.w("t", ve, cpt)
                    if pr == 0:
                        P.w("t", ve, cp.get("avl_used"))
                        P.w("t", ac, cp.get("l_used"))
                        P.w("t", ve, cp.get(("evac", id(pA))))
                        P.w("t", ve, cp.get(("evac", id(pB))))
                    for kb in (k0, k1):
                        qq = kb - 4 * s4
                        sfx = 128 * qq if qq >= 0 else 0
                        fst, lst = (kb == 0), (kb == kbN - 1)
                        P.o("t", lambda t, kkb=kb, h=h2, sf=sfx, f=fst, l=lst:
                            t.matmul(pA[:, sf:512],
                                     v_s[:, kkb, h * 128:(h + 1) * 128],
                                     pt[:, kkb % 4, sf:512],
                                     start=f, stop=l), None)
                        P.o("t", lambda t, kkb=kb, sf=sfx, f=fst, l=lst:
                            t.matmul(pB[:, sf:512], ones_b[:, :],
                                     pt[:, kkb % 4, sf:512],
                                     start=f, stop=l),
                            mm if kb == k1 else None)
                    pair_hist.append((cpe, mm.v))
                P.w("s", mm, pair_hist[-1][1])
                cp["l_used"] = P.o("s", lambda s: s.activation(
                    lln[:, :], pB[:, 0:512], AF.Ln), ac)
                clr = P.o("s", lambda s: s.activation(
                    lrec[:, :], lln[:, :], AF.Exp, scale=-1.0), ac)
                P.w("v", ac, clr)
                if len(avdma_hist) >= 2:
                    P.w("v", avdma_hist[-2][0], avdma_hist[-2][1])
                cp["avl_used"] = P.o("v", lambda v, s=s4: v.tensor_mul(
                    avst[:, s % 2, :], pA[:, 0:512], lrec[:, :]), ve)
                P.w("sp", ve, cp["avl_used"])
                j = 2 * b + (s4 // 2)
                off = (s4 % 2) * 512
                aslot = len(avdma_hist) % 2
                dv = P.o("sp", lambda sp, jj=j, h=h2, of=off, s=s4:
                         sp.dma_start(d_a2a_in.ap()[jj, h, :, of:of + 512],
                                      avst[:, s % 2, :]), q_av[aslot])
                avdma_hist.append((q_av[aslot], dv))

    for b in range(NB):
        q_batch(b)
        P.o("g", lambda g, o=b: g.dma_start(
            woutb[:, o, :, :],
            wout.ap()[:, :, o * 512:(o + 1) * 512].rearrange(
                "dvc p f -> p dvc f")), q_wo[b % 2])
        emit_gamma(32)
    emit_gamma(2 * NGCH)
    gamma_final(gacc_f, tpos_f, tneg_f, gneg_f)
    gamma_final(gacc_p, tpos_p, tneg_p, gneg_p)
    GFIN_P = cp[("gfin", id(gacc_p))]

    P.replay(nc)
    ctxq.close()
    GBF_END = (q_gb[0].v, q_gb[1].v)
    GAM_ABS_END = ghist[-1][0]
    VE_Q_END = ve.v
    AC_Q_END = ac.v

    ctxw = ExitStack()
    sbw = lambda n, s, d: ctxw.enter_context(nc.sbuf_tensor(n, s, d))
    avT_all = sbw("avT_all", [128, 16, 1024], BF16)
    stream = sbw("stream", [128, TC, D], F32)
    xn_st = sbw("xn_st", [128, D], F32)
    xh_st = sbw("xh_st", [128, 2, D], BF16)

    P = Prog(ve, ac)

    P.w("g", q_av[0], q_av[0].v)
    P.w("g", q_av[1], q_av[1].v)

    def _cc_call(g):
        with allow_ici_all_to_all():
            return g.collective_compute(
                "AllToAll", ALU.bypass, replica_groups=[list(range(8))],
                ins=[d_a2a_in.ap().rearrange("a b c d -> a (b c d)")],
                outs=[d_a2a_out.ap().rearrange("a b c d -> a (b c d)")])

    CC_DONE = P.o("g", _cc_call, cc)

    cp[("evac", id(pD))] = cp[("gfin", id(gacc_p))]
    cp[("evac", id(pC))] = None

    P.w("sp", cc, CC_DONE)
    P.w("sp", q_gb[0], GBF_END[0])
    P.w("sp", q_gb[1], GBF_END[1])
    P.w("sp", ac, GAM_ABS_END)
    for j in range(8):
        for h2 in range(H2):
            AVT_LD = P.o("sp", lambda sp, jj=j, h=h2: sp.dma_start(
                avT_all[:, jj * 2 + h, :], d_a2a_out.ap()[jj, h, :, :]), q_lda)
    for tc in range(TC):
        XO_LD = P.o("sp", lambda sp, t=tc: sp.dma_start(
            stream[:, t, :], xo.ap()[t, :, :]), q_ldx)

    P.w("t", ve, VE_Q_END)
    P.w("t", ac, AC_Q_END)
    P.w("t", q_lda, AVT_LD)
    P.w("t", q_wo[0], q_wo[0].v)
    P.w("t", q_wo[1], q_wo[1].v)
    cxh_prev = None
    for tc in range(TC):
        even = (tc % 2 == 0)
        banks = ([lambda o: pbig[:, o * 512:(o + 1) * 512]] if even
                 else [lambda o: (pA, pB, pC, pD)[o][:, 0:512]])[0]
        P.w("t", ve, cp.get(("wevac", even)))
        for og in range(4):
            for dvc in range(DC):
                P.o("t", lambda t, bk=banks(og), d=dvc, tt=tc, o=og:
                    t.matmul(bk,
                             avT_all[:, d, tt * 128:(tt + 1) * 128],
                             woutb[:, o, d, :],
                             start=(d == 0), stop=(d == DC - 1)),
                    mm if (og == 3 and dvc == DC - 1) else None)
        mv = mm.v
        P.w("v", mm, mv)
        if tc == 0:
            P.w("v", q_ldx, XO_LD)
        if even:
            cev = P.o("v", lambda v, tt=tc: v.tensor_add(
                stream[:, tt, :], pbig[:, 0:2048], stream[:, tt, :]), ve)
        else:
            for og in range(4):
                cev = P.o("v", lambda v, tt=tc, o=og: v.tensor_add(
                    stream[:, tt, o * 512:(o + 1) * 512],
                    (pA, pB, pC, pD)[o][:, 0:512],
                    stream[:, tt, o * 512:(o + 1) * 512]), ve)
        cp[("wevac", even)] = cev

        if cxh_prev is not None:
            P.w("s", ve, cxh_prev)
        cmb = ln_stats(stream[:, tc, :], eps_ln, xn_st[:, :], ve, cev)
        P.w("s", ve, cmb)
        cx1 = P.o("s", lambda s, t=tc: s.activation(
            stream[:, t, :], stream[:, t, :], AF.Identity,
            bias=st_mb[:, 0:1], scale=st_rstd[:, 0:1]), ac)
        cmb2 = ln_stats(stream[:, tc, :], eps_bl, xn_st[:, :], ac, cx1)
        P.w("s", ve, cmb2)
        cxn = P.o("s", lambda s, t=tc: s.activation(
            xn_st[:, :], stream[:, t, :], AF.Identity,
            bias=st_mb[:, 0:1], scale=st_rstd[:, 0:1]), ac)
        P.w("sp", ac, cxn)
        P.o("sp", lambda sp, t=tc: sp.dma_start(
            d_x1.ap()[t, :, :], stream[:, t, :]), sq)
        P.w("v", ac, cxn)
        cg1 = P.o("v", lambda v: v.reduce_max(st_g[:, 0:1], xn_st[:, :], AX.X,
                                              apply_absolute_value=True), ve)
        P.w("s", ve, cg1)
        P.o("s", lambda s: s.activation(st_gp[:, 0:1], st_g[:, 0:1],
                                        AF.Identity, scale=CLIP_C), ac)
        cg2 = P.o("s", lambda s: s.activation(st_gn[:, 0:1], st_g[:, 0:1],
                                              AF.Identity, scale=-CLIP_C), ac)
        P.w("v", ac, cg2)
        if tc >= 2:
            P.w("v", q_xh[tc % 2], cp[("xhat_dma", tc - 2)])
        cxh = P.o("v", lambda v, t=tc: v.tensor_scalar(
            xh_st[:, t % 2, :], xn_st[:, :], st_gn[:, 0:1], st_gp[:, 0:1],
            op0=ALU.max, op1=ALU.min), ve)
        cxh_prev = cxh
        P.w("sp", ve, cxh)
        cp[("xhat_dma", tc)] = P.o("sp", lambda sp, t=tc: sp.dma_start(
            d_xhat.ap()[t * 128:(t + 1) * 128, :], xh_st[:, t % 2, :]),
            q_xh[tc % 2])
    X1_SPILLED = sq.v

    P.replay(nc)
    ctxw.close()
    ctxqw.close()

    ctxh = ExitStack()
    htens = ctxh.enter_context(nc.sbuf_tensor("htens", [128, TC, DFF], BF16))
    hsump = ctxh.enter_context(nc.sbuf_tensor("hsump", [128, TC, FG], F32))
    hsqp = ctxh.enter_context(nc.sbuf_tensor("hsqp", [128, TC, FG], F32))

    ctxf1 = ExitStack()
    sbf1 = lambda n, s, d: ctxf1.enter_context(nc.sbuf_tensor(n, s, d))
    xhatT = sbf1("xhatT", [128, DC, 1024], BF16)
    ternf = sbf1("ternf", [128, 2, DC, 512], BF16)
    wch = sbf1("wch", [128, 2, 1024], F32)
    tsa_t = sbf1("tsa_t", [128, 1024], F32)
    tsa = tsa_t[:, :]

    P = Prog(ve, ac)
    P.w("sp", q_xh[0], q_xh[0].v)
    P.w("sp", q_xh[1], q_xh[1].v)
    P.w("s", sq, X1_SPILLED)
    P.w("s", q_xh[0], q_xh[0].v)
    P.w("s", q_xh[1], q_xh[1].v)
    for dc in range(DC):
        XT_LD = P.o("sp", lambda sp, d=dc: sp.dma_start(
            xhatT[:, d, :], d_xhat.ap()[:, d * 128:(d + 1) * 128],
            transpose=True), q_tr)

    for fg in range(FG):
        for chh in range(8):
            gidx = fg * 8 + chh
            slot = gidx % 2
            if gidx >= 2:
                P.w("sp", ve, cp[("ternf_ch", gidx - 2)])
            ldv = P.o("sp", lambda sp, f=fg, c=chh, b=slot: sp.dma_start(
                wch[:, b, :].rearrange("p (a f) -> p a f", a=2),
                wfc.ap()[f, 2 * c:2 * c + 2, :, :].rearrange(
                    "dc p f -> p dc f")), q_wc[slot])
            P.w("v", q_wc[slot], ldv)
            if fg >= 2 and chh == 0:
                P.w("v", mm, cp[("fc_mm", fg - 2)])
            P.o("v", lambda v, b=slot: v.tensor_scalar(
                tsa, wch[:, b, :], tpos_f[:, 0:1], None,
                op0=ALU.is_ge), ve)
            cp[("ternf_ch", gidx)] = P.o(
                "v", lambda v, b=slot, f=fg, c=chh:
                v.scalar_tensor_tensor(
                    ternf[:, f % 2, 2 * c:2 * c + 2, :].rearrange(
                        "p a f -> p (a f)"),
                    wch[:, b, :], tneg_f[:, 0:1], tsa,
                    op0=ALU.is_le, op1=ALU.subtract), ve)
        tern_done = ve.v

        P.w("t", ve, tern_done)
        if fg == 0:
            P.w("t", q_tr, XT_LD)
        for tc in range(TC):
            bank = (pC, pD)[tc % 2]
            P.w("t", ac, cp.get(("fc_evac", id(bank))))
            for dc in range(DC):
                P.o("t", lambda t, bk=bank, d=dc, tt=tc, f=fg:
                    t.matmul(bk[:, 0:512],
                             xhatT[:, d, tt * 128:(tt + 1) * 128],
                             ternf[:, f % 2, d, :],
                             start=(d == 0), stop=(d == DC - 1)),
                    mm if dc == DC - 1 else None)
            P.w("s", mm, mm.v)
            cp[("fc_evac", id(bank))] = P.o(
                "s", lambda s, bk=bank, tt=tc, f=fg: s.activation(
                    htens[:, tt, f * 512:(f + 1) * 512], bk[:, 0:512],
                    AF.Gelu, scale=gneg_f[:, 0:1],
                    accum_out=hsump[:, tt, f:f + 1]), ac)
            csq = P.o("s", lambda s, tt=tc, f=fg: s.activation(
                pbig[:, 0:512], htens[:, tt, f * 512:(f + 1) * 512],
                AF.Square, accum_out=hsqp[:, tt, f:f + 1]), ac)

        cp[("fc_mm", fg)] = mm.v
    F1_DONE_AC = ac.v
    F1_DONE_VE = ve.v

    P.replay(nc)
    ctxf1.close()

    ctx15 = ExitStack()
    hn_st = ctx15.enter_context(nc.sbuf_tensor("hn_st", [128, DFF], F32))
    hh_st = ctx15.enter_context(nc.sbuf_tensor("hh_st", [128, 2, DFF], BF16))
    q_tr2 = Sem(nc, "q_tr2", dma=True)

    P = Prog(ve, ac)
    P.w("v", ac, F1_DONE_AC)
    for tc in range(TC):
        P.o("v", lambda v, t=tc: v.reduce_sum(
            st8_m[:, t:t + 1], hsump[:, t, :], AX.X), ve)
        P.o("v", lambda v, t=tc: v.reduce_sum(
            st8_sq[:, t:t + 1], hsqp[:, t, :], AX.X), ve)
    c2 = ve.v
    P.w("s", ve, c2)
    P.o("s", lambda s: s.activation(
        st8_m[:, :], st8_m[:, :], AF.Identity, scale=1.0 / DFF), ac)
    P.o("s", lambda s: s.activation(
        st8_sq[:, :], st8_sq[:, :], AF.Identity, scale=1.0 / DFF), ac)
    c4 = P.o("s", lambda s: s.activation(
        st8_var[:, :], st8_m[:, :], AF.Square), ac)
    P.w("v", ac, c4)
    c5 = P.o("v", lambda v: v.tensor_sub(st8_var[:, :], st8_var[:, :],
                                         st8_sq[:, :]), ve)
    P.w("s", ve, c5)
    P.o("s", lambda s: s.activation(
        st8_r[:, :], st8_var[:, :], AF.Ln,
        bias=eps_bl[:, 0:1], scale=-1.0), ac)
    c6 = P.o("s", lambda s: s.activation(
        st8_r[:, :], st8_r[:, :], AF.Exp, scale=-0.5), ac)
    P.w("v", ac, c6)
    c_nm = P.o("v", lambda v: v.scalar_tensor_tensor(
        st8_nm[:, :], st8_m[:, :], -1.0, st8_r[:, :],
        op0=ALU.mult, op1=ALU.mult), ve)

    for tc in range(TC):
        if tc == 0:
            P.w("s", ve, c_nm)
        if tc >= 1:
            P.w("s", ve, cp[("f15_clip", tc - 1)])
        chn = P.o("s", lambda s, t=tc: s.activation(
            hn_st[:, :], htens[:, t, :], AF.Identity,
            bias=st8_nm[:, t:t + 1], scale=st8_r[:, t:t + 1]), ac)
        P.w("v", ac, chn)
        cg1 = P.o("v", lambda v: v.reduce_max(st_g[:, 0:1], hn_st[:, :], AX.X,
                                              apply_absolute_value=True), ve)
        P.w("s", ve, cg1)
        P.o("s", lambda s: s.activation(st_gp[:, 0:1], st_g[:, 0:1],
                                        AF.Identity, scale=CLIP_C), ac)
        cg2 = P.o("s", lambda s: s.activation(st_gn[:, 0:1], st_g[:, 0:1],
                                              AF.Identity, scale=-CLIP_C), ac)
        P.w("v", ac, cg2)
        if tc >= 2:
            P.w("v", q_hh, cp[("hhat_dma", tc - 2)])
        chh2 = P.o("v", lambda v, t=tc: v.tensor_scalar(
            hh_st[:, t % 2, :], hn_st[:, :], st_gn[:, 0:1], st_gp[:, 0:1],
            op0=ALU.max, op1=ALU.min), ve)
        cp[("f15_clip", tc)] = chh2
        P.w("sp", ve, chh2)
        chd = P.o("sp", lambda sp, t=tc: sp.dma_start(
            d_hhat.ap()[t, :, :], hh_st[:, t % 2, :]), q_hh)
        cp[("hhat_dma", tc)] = chd
        P.w("sp", ac, chn)
        P.w("sp", q_hh, chd)
        P.w("s", q_hh, chd)
        for fc in range(64):
            eng = "sp" if fc % 2 == 0 else "s"
            qq_tr = q_tr if fc % 2 == 0 else q_tr2
            P.o(eng, lambda E, t=tc, f=fc: E.dma_start(
                htens[:, t, f * 128:(f + 1) * 128],
                d_hhat.ap()[t, :, f * 128:(f + 1) * 128],
                transpose=True), qq_tr)
    F15_DONE = q_tr.v
    F15_DONE2 = q_tr2.v

    P.replay(nc)
    ctx15.close()

    ctxf2 = ExitStack()
    sbf2 = lambda n, s, d: ctxf2.enter_context(nc.sbuf_tensor(n, s, d))
    ternp = sbf2("ternp", [128, 2, 64, 256], BF16)
    wch2 = sbf2("wch2", [128, 2, 1024], F32)
    pst = sbf2("pst", [128, 2, 512], F32)
    tsa2 = pbig[:, 1024:2048]

    P = Prog(ve, ac)
    P.w("v", q_hh, q_hh.v)
    pst_hist = []
    for pg in range(OG // 2):
        for chh in range(32):
            og = 2 * pg + chh // 16
            c = chh % 16
            tslot = chh // 16
            gidx = pg * 32 + chh
            wslot = gidx % 2
            if gidx >= 2:
                P.w("sp", ve, cp[("ternp_ch", gidx - 2)])
            if pg >= 1 and chh == 0:
                P.w("sp", mm, cp[("proj_mm", pg - 1)])
            ldv = P.o("sp", lambda sp, o=og, cx=c, b=wslot: sp.dma_start(
                wch2[:, b, :].rearrange("p (a f) -> p a f", a=4),
                wproj.ap()[o, 4 * cx:4 * cx + 4, :, :].rearrange(
                    "fc p f -> p fc f")), q_wc[wslot])
            P.w("v", q_wc[wslot], ldv)
            P.o("v", lambda v, b=wslot: v.tensor_scalar(
                tsa2, wch2[:, b, :], tpos_p[:, 0:1], None,
                op0=ALU.is_ge), ve)
            cp[("ternp_ch", gidx)] = P.o(
                "v", lambda v, b=wslot, ts=tslot, cx=c:
                v.scalar_tensor_tensor(
                    ternp[:, ts, 4 * cx:4 * cx + 4, :].rearrange(
                        "p a f -> p (a f)"),
                    wch2[:, b, :], tneg_p[:, 0:1], tsa2,
                    op0=ALU.is_le, op1=ALU.subtract), ve)
        ternp_done = ve.v

        P.w("t", ve, ternp_done)
        if pg == 0:
            P.w("t", q_tr, F15_DONE)
            P.w("t", q_tr2, F15_DONE2)
        for tc in range(TC):
            bank = (pC, pD)[tc % 2]
            P.w("t", ve, cp.get(("pj_evac", id(bank))))
            for fc in range(64):
                P.o("t", lambda t, bk=bank, f=fc, tt=tc:
                    t.matmul(bk[:, 0:512],
                             htens[:, tt, f * 128:(f + 1) * 128],
                             ternp[:, :, f, :],
                             start=(f == 0), stop=(f == 63)),
                    mm if fc == 63 else None)
            P.w("v", mm, mm.v)
            if len(pst_hist) >= 2:
                P.w("v", pst_hist[-2][0], pst_hist[-2][1])
            cpe = P.o("v", lambda v, bk=bank, tt=tc: v.tensor_scalar(
                pst[:, tt % 2, :], bk[:, 0:512], gneg_p[:, 0:1], None,
                op0=ALU.mult), ve)
            cp[("pj_evac", id(bank))] = cpe
            P.w("sp", ve, cpe)
            pslot = tc % 2
            dv = P.o("sp", lambda sp, tt=tc, p=pg: sp.dma_start(
                d_proj.ap()[tt, :, p * 512:(p + 1) * 512],
                pst[:, tt % 2, :]), q_ps[pslot])
            pst_hist.append((q_ps[pslot], dv))
        cp[("proj_mm", pg)] = mm.v

    P.replay(nc)
    ctxf2.close()
    ctxh.close()

    ctxn = ExitStack()
    sbn = lambda n, s, d: ctxn.enter_context(nc.sbuf_tensor(n, s, d))
    fa = sbn("fa", [128, 2, D], F32)
    fb = sbn("fb", [128, 2, D], F32)
    fsq = sbn("fsq", [128, D], F32)
    fo = sbn("fo", [128, 2, D], F32)

    P = Prog(ve, ac)
    P.w("sp", q_ps[0], q_ps[0].v)
    P.w("sp", q_ps[1], q_ps[1].v)
    P.w("sp", sq, X1_SPILLED)
    for tc in range(TC):
        fslot = tc % 2
        if tc >= 2:
            P.w("sp", ac, cp[("fin_ln", tc - 2)])
        P.o("sp", lambda sp, t=tc: sp.dma_start(
            fa[:, t % 2, :], d_proj.ap()[t, :, :]), q_fn[fslot])
        ldv = P.o("sp", lambda sp, t=tc: sp.dma_start(
            fb[:, t % 2, :], d_x1.ap()[t, :, :]), q_fn[fslot])
        P.w("v", q_fn[fslot], ldv)
        cadd = P.o("v", lambda v, t=tc: v.tensor_add(
            fa[:, t % 2, :], fa[:, t % 2, :], fb[:, t % 2, :]), ve)
        cmb = ln_stats(fa[:, tc % 2, :], eps_ln, fsq[:, :], ve, cadd)
        P.w("s", ve, cmb)
        if tc >= 2:
            P.w("s", q_st[tc % 2], cp[("out_dma", tc - 2)])
        cfo = P.o("s", lambda s, t=tc: s.activation(
            fo[:, t % 2, :], fa[:, t % 2, :], AF.Identity,
            bias=st_mb[:, 0:1], scale=st_rstd[:, 0:1]), ac)
        cp[("fin_ln", tc)] = cfo
        P.w("sp", ac, cfo)
        cp[("out_dma", tc)] = P.o("sp", lambda sp, t=tc: sp.dma_start(
            out.ap()[t, :, :], fo[:, t % 2, :]), q_st[tc % 2])
    P.w("sp", q_st[0], q_st[0].v)
    P.w("sp", q_st[1], q_st[1].v)

    P.replay(nc)
    ctxn.close()
    psum.close()
    ctx0.close()
    return nc


_NC_CACHE = None


def _get_nc():
    global _NC_CACHE
    if _NC_CACHE is None:
        _NC_CACHE = build_nc()
    return _NC_CACHE


def kernel(L0_attn_x, L0_attn_w_qkv, L0_attn_w_out, L0_ffn_w_fc, L0_ffn_w_proj):
    x = np.ascontiguousarray(np.asarray(L0_attn_x, dtype=np.float32))
    w_qkv = np.ascontiguousarray(np.asarray(L0_attn_w_qkv, dtype=np.float32))
    w_out = np.ascontiguousarray(np.asarray(L0_attn_w_out, dtype=np.float32))
    w_fc = np.ascontiguousarray(np.asarray(L0_ffn_w_fc, dtype=np.float32))
    w_proj = np.ascontiguousarray(np.asarray(L0_ffn_w_proj, dtype=np.float32))

    xt = np.ascontiguousarray(
        x.transpose(0, 2, 1).reshape(NB, DC, 128, 4, 512).transpose(0, 3, 2, 1, 4))
    wout_dev = np.ascontiguousarray(w_out.reshape(DC, 128, D))
    wfc_dev = np.ascontiguousarray(
        w_fc.T.reshape(DC, 128, FG, 512).transpose(2, 0, 1, 3))
    wproj_dev = np.ascontiguousarray(
        w_proj.T.reshape(64, 128, OG, 256).transpose(2, 0, 1, 3))
    tri_m = np.ascontiguousarray(
        (np.arange(128)[None, :] >= np.arange(128)[:, None]).astype(BF))

    in_maps = []
    for c in range(8):
        b, hf = divmod(c, 2)
        xo_c = np.ascontiguousarray(
            x[b, hf * 1024:(hf + 1) * 1024].reshape(TC, 128, D))
        heads = [2 * c, 2 * c + 1]
        wqk_c = np.ascontiguousarray(np.stack([
            np.concatenate([w_qkv[:, h * 128:(h + 1) * 128],
                            w_qkv[:, D + h * 128:D + (h + 1) * 128]], axis=1)
            for h in heads]).reshape(H2, DC, 128, 256))
        wv_c = np.ascontiguousarray(np.concatenate(
            [w_qkv[:, 2 * D + h * 128:2 * D + (h + 1) * 128] for h in heads],
            axis=1).reshape(DC, 128, 256))
        in_maps.append({
            "xt": xt, "xo": xo_c, "wqk": wqk_c, "wv": wv_c,
            "wout": wout_dev, "wfc": wfc_dev, "wproj": wproj_dev, "tri": tri_m,
        })

    nc = _get_nc()
    res = run_bass_kernel_spmd(nc, in_maps, core_ids=list(range(8)))

    y = np.empty((NB, S, D), dtype=np.float32)
    for c in range(8):
        b, hf = divmod(c, 2)
        y[b, hf * 1024:(hf + 1) * 1024] = res.results[c]["out"].reshape(1024, D)
    return y


# version 15
# speedup vs baseline: 1.1624x; 1.0020x over previous
import numpy as np
import ml_dtypes
from contextlib import ExitStack

import concourse.bass as bass
import concourse.mybir as mybir
from concourse.bass_utils import run_bass_kernel_spmd
from concourse.aot_env import allow_ici_all_to_all

F32 = mybir.dt.float32
BF16 = mybir.dt.bfloat16
AF = mybir.ActivationFunctionType
ALU = mybir.AluOpType
AX = mybir.AxisListType

BF = ml_dtypes.bfloat16

NB = 4
S = 2048
D = 2048
DC = 16
H2 = 2
TC = 8
DFF = 8192
FG = 16
OG = 8
INV_SQRT_HD = 0.08838834764831845
EPS_LN = 1e-5
EPS_BL = 1e-6
CLIP_C = (128.0 - 1e-6) / 128.0
NWELEM = float(DFF * D)


class Sem:
    __slots__ = ("h", "v", "name", "dma")

    def __init__(self, nc, name, dma=False):
        self.h = nc.alloc_semaphore(name)
        self.v = 0
        self.name = name
        self.dma = dma


class Prog:

    ENGS = ("t", "v", "s", "g", "sp")

    def __init__(self, ve_sem, ac_sem):
        self.ops = {e: [] for e in self.ENGS}
        self.ve = ve_sem
        self.ac = ac_sem

    def w(self, e, sem, v):
        if v is None or v <= 0:
            return
        self.ops[e].append(lambda E, s=sem, vv=v: E.wait_ge(s.h, vv))

    def o(self, e, build, sem=None, n=1):
        if e == "v" and not (sem is not None and sem.dma):
            if self.ve.v > 0:
                self.w("v", self.ve, self.ve.v)
            if sem is None:
                sem = self.ve
            assert sem is self.ve
        elif e == "s" and not (sem is not None and sem.dma):
            if self.ac.v > 0:
                self.w("s", self.ac, self.ac.v)
            if sem is None:
                sem = self.ac
            assert sem is self.ac
        if sem is None:
            self.ops[e].append(build)
            return None
        if sem.dma:
            n = 16
        sem.v += n
        self.ops[e].append(lambda E, b=build, s=sem, nn=n: b(E).then_inc(s.h, nn))
        return sem.v

    def replay(self, nc):
        with nc.Block() as blk:
            for name, attr in (("t", "tensor"), ("v", "vector"), ("s", "scalar"),
                               ("g", "gpsimd"), ("sp", "sync")):
                if self.ops[name]:
                    fns = self.ops[name]

                    def body(E, fns=fns):
                        for f in fns:
                            f(E)
                    getattr(blk, attr)(body)


def build_nc():
    nc = bass.Bass("TRN2", target_bir_lowering=False)

    xt = nc.dram_tensor("xt", [NB, 4, 128, DC, 512], F32, kind="ExternalInput")
    xo = nc.dram_tensor("xo", [TC, 128, D], F32, kind="ExternalInput")
    wqk = nc.dram_tensor("wqk", [H2, DC, 128, 256], F32, kind="ExternalInput")
    wv = nc.dram_tensor("wv", [DC, 128, 256], F32, kind="ExternalInput")
    wout = nc.dram_tensor("wout", [DC, 128, D], F32, kind="ExternalInput")
    wfc = nc.dram_tensor("wfc", [FG, DC, 128, 512], F32, kind="ExternalInput")
    wproj = nc.dram_tensor("wproj", [OG, 64, 128, 256], F32, kind="ExternalInput")
    tri = nc.dram_tensor("tri", [128, 128], BF16, kind="ExternalInput")
    out = nc.dram_tensor("out", [TC, 128, D], F32, kind="ExternalOutput")

    d_a2a_in = nc.dram_tensor("d_a2a_in", [8, H2, 128, 1024], BF16)
    d_a2a_out = nc.dram_tensor("d_a2a_out", [8, H2, 128, 1024], BF16)
    d_xhat = nc.dram_tensor("d_xhat", [1024, D], BF16)
    d_hhat = nc.dram_tensor("d_hhat", [TC, 128, DFF], BF16)
    d_x1 = nc.dram_tensor("d_x1", [TC, 128, D], F32)
    d_proj = nc.dram_tensor("d_proj", [TC, 128, D], F32)

    mm = Sem(nc, "mm")
    ve = Sem(nc, "ve")
    ac = Sem(nc, "ac")
    cc = Sem(nc, "cc")
    po = Sem(nc, "po")
    sq = Sem(nc, "sq", dma=True)
    q_wq = Sem(nc, "q_wq", dma=True)
    q_tri = Sem(nc, "q_tri", dma=True)
    q_xq = [Sem(nc, "q_xq0", dma=True), Sem(nc, "q_xq1", dma=True)]
    q_gb = [Sem(nc, "q_gb0", dma=True), Sem(nc, "q_gb1", dma=True)]
    q_av = [Sem(nc, "q_av0", dma=True), Sem(nc, "q_av1", dma=True)]
    q_lda = Sem(nc, "q_lda", dma=True)
    q_ldx = Sem(nc, "q_ldx", dma=True)
    q_wo = [Sem(nc, "q_wo0", dma=True), Sem(nc, "q_wo1", dma=True)]
    q_xh = [Sem(nc, "q_xh0", dma=True), Sem(nc, "q_xh1", dma=True)]
    q_tr = Sem(nc, "q_tr", dma=True)
    q_wc = [Sem(nc, "q_wc0", dma=True), Sem(nc, "q_wc1", dma=True)]
    q_hh = Sem(nc, "q_hh", dma=True)
    q_ps = [Sem(nc, "q_ps0", dma=True), Sem(nc, "q_ps1", dma=True)]
    q_fn = [Sem(nc, "q_fn0", dma=True), Sem(nc, "q_fn1", dma=True)]
    q_st = [Sem(nc, "q_st0", dma=True), Sem(nc, "q_st1", dma=True)]

    ctx0 = ExitStack()
    sb = lambda n, s, d: ctx0.enter_context(nc.sbuf_tensor(n, s, d))

    ones_b = sb("ones_b", [128, 128], BF16)
    onesf_col = sb("onesf_col", [128, 1], F32)
    onesf_row = sb("onesf_row", [1, 128], F32)
    tri_s = sb("tri_s", [128, 128], BF16)
    eps_ln = sb("eps_ln", [128, 1], F32)
    eps_bl = sb("eps_bl", [128, 1], F32)
    gacc_f = sb("gacc_f", [128, 1], F32)
    gacc_p = sb("gacc_p", [128, 1], F32)
    gpart = sb("gpart", [128, 2], F32)
    tpos_f = sb("tpos_f", [128, 1], F32)
    tneg_f = sb("tneg_f", [128, 1], F32)
    gneg_f = sb("gneg_f", [128, 1], F32)
    tpos_p = sb("tpos_p", [128, 1], F32)
    tneg_p = sb("tneg_p", [128, 1], F32)
    gneg_p = sb("gneg_p", [128, 1], F32)
    sc11 = sb("sc11", [1, 1], F32)
    st_m = sb("st_m", [128, 1], F32)
    st_sq = sb("st_sq", [128, 1], F32)
    st_var = sb("st_var", [128, 1], F32)
    st_rstd = sb("st_rstd", [128, 1], F32)
    st_mb = sb("st_mb", [128, 1], F32)
    st_g = sb("st_g", [128, 1], F32)
    st_gp = sb("st_gp", [128, 1], F32)
    st_gn = sb("st_gn", [128, 1], F32)
    st_lo = sb("st_lo", [128, 1], F32)
    st_hi = sb("st_hi", [128, 1], F32)
    st_d1 = sb("st_d1", [128, 1], F32)
    st_d2 = sb("st_d2", [128, 1], F32)
    st8_m = sb("st8_m", [128, TC], F32)
    st8_sq = sb("st8_sq", [128, TC], F32)
    st8_var = sb("st8_var", [128, TC], F32)
    st8_d1 = sb("st8_d1", [128, TC], F32)
    st8_d2 = sb("st8_d2", [128, TC], F32)
    st8_r = sb("st8_r", [128, TC], F32)
    st8_c = sb("st8_c", [128, TC], F32)
    st8_lo = sb("st8_lo", [128, TC], F32)
    st8_hi = sb("st8_hi", [128, TC], F32)
    st8_nm = sb("st8_nm", [128, TC], F32)

    psum = ExitStack()
    pbig = psum.enter_context(nc.psum_tensor("pbig", [128, 2048], F32))
    pA = psum.enter_context(nc.psum_tensor("pA", [128, 512], F32))
    pB = psum.enter_context(nc.psum_tensor("pB", [128, 512], F32))
    pC = psum.enter_context(nc.psum_tensor("pC", [128, 512], F32))
    pD = psum.enter_context(nc.psum_tensor("pD", [128, 512], F32))

    cp = {}
    P = None

    def ln_stats(src_ap, eps_tile, dump_ap, rsem, rval):
        n = src_ap.shape[-1]
        P.w("v", rsem, rval)
        P.o("v", lambda v, a=src_ap: v.reduce_sum(st_m[:, 0:1], a, AX.X), ve)
        c2 = P.o("v", lambda v, a=src_ap, d=dump_ap: v.scalar_tensor_tensor(
            d, a, 1.0, a, op0=ALU.bypass, op1=ALU.mult,
            accum_out=st_sq[:, 0:1]), ve)
        P.w("s", ve, c2)
        P.o("s", lambda s, nn=n: s.activation(
            st_m[:, 0:1], st_m[:, 0:1], AF.Identity, scale=1.0 / nn), ac)
        P.o("s", lambda s, nn=n: s.activation(
            st_sq[:, 0:1], st_sq[:, 0:1], AF.Identity, scale=1.0 / nn), ac)
        c4 = P.o("s", lambda s: s.activation(
            st_var[:, 0:1], st_m[:, 0:1], AF.Square), ac)
        P.w("v", ac, c4)
        c5 = P.o("v", lambda v: v.tensor_sub(
            st_var[:, 0:1], st_var[:, 0:1], st_sq[:, 0:1]), ve)
        P.w("s", ve, c5)
        P.o("s", lambda s, e=eps_tile: s.activation(
            st_rstd[:, 0:1], st_var[:, 0:1], AF.Ln,
            bias=e[:, 0:1], scale=-1.0), ac)
        crs = P.o("s", lambda s: s.activation(
            st_rstd[:, 0:1], st_rstd[:, 0:1], AF.Exp, scale=-0.5), ac)
        P.w("v", ac, crs)
        cmb = P.o("v", lambda v: v.tensor_scalar(
            st_mb[:, 0:1], st_m[:, 0:1], st_rstd[:, 0:1], -1.0,
            op0=ALU.mult, op1=ALU.mult), ve)
        return cmb

    ghist = []

    def gamma_chunk(gacc, gbuf_t, gdump_t, src_ap):
        idx = len(ghist)
        slot = idx % 2
        if idx >= 2:
            P.w("sp", ac, ghist[-2][0])
        ldv = P.o("sp", lambda sp, a=src_ap, b=slot, g=gbuf_t:
                  sp.dma_start(g[:, b, :].rearrange(
                      "p (a f) -> p a f", a=a.shape[1]), a), q_gb[slot])
        P.w("s", q_gb[slot], ldv)
        if idx >= 2:
            P.w("s", ve, ghist[-2][1])
        accp = P.o(
            "s", lambda s, b=slot, g=gbuf_t, d=gdump_t, ps=slot:
            s.activation(d[:, :], g[:, b, :], AF.Abs,
                         accum_out=gpart[:, ps:ps + 1]), ac)
        P.w("v", ac, accp)
        addp = P.o(
            "v", lambda v, ps=slot, ga=gacc:
            v.tensor_add(ga[:, 0:1], ga[:, 0:1], gpart[:, ps:ps + 1]), ve)
        ghist.append((accp, addp))
        cp[("gsum", id(gacc))] = ve.v

    def gamma_final(gacc, tpos, tneg, gneg):
        P.w("t", ve, ve.v)
        P.o("t", lambda t, ga=gacc: t.matmul(
            pD[0:1, 0:1], ga[:, 0:1], onesf_col[:, 0:1],
            start=True, stop=True), mm)
        s1 = mm.v
        P.w("s", mm, s1)
        s2 = P.o("s", lambda s: s.copy(sc11[:, :], pD[0:1, 0:1]), ac)
        P.w("t", ac, s2)
        P.o("t", lambda t: t.matmul(pD[:, 1:2], onesf_row[:, :], sc11[:, :],
                                    start=True, stop=True), mm)
        s3 = mm.v
        P.w("v", mm, s3)
        P.o("v", lambda v, tp=tpos: v.tensor_scalar(
            tp[:, 0:1], pD[:, 1:2], 0.5 / NWELEM, 0.5 * EPS_BL,
            op0=ALU.mult, op1=ALU.add), ve)
        P.o("v", lambda v, gn=gneg: v.tensor_scalar(
            gn[:, 0:1], pD[:, 1:2], -1.0 / NWELEM, None, op0=ALU.mult), ve)
        cfin = P.o("v", lambda v, tp=tpos, tn=tneg: v.tensor_scalar(
            tn[:, 0:1], tp[:, 0:1], -1.0, None, op0=ALU.mult), ve)
        cp[("gfin", id(gacc))] = cfin
        return cfin

    ctxqw = ExitStack()
    woutb = ctxqw.enter_context(nc.sbuf_tensor("woutb", [128, 4, DC, 512], BF16))
    ctxq = ExitStack()
    sbq = lambda n, s, d: ctxq.enter_context(nc.sbuf_tensor(n, s, d))
    xq = sbq("xq", [128, 2, DC, 512], BF16)
    wqk_s = sbq("wqk_s", [128, H2, DC, 256], BF16)
    wv_s = sbq("wv_s", [128, DC, 256], BF16)
    kT_s = sbq("kT_s", [128, H2, S], BF16)
    qT_s = sbq("qT_s", [128, H2, S], BF16)
    v_s = sbq("v_s", [128, 16, 256], BF16)
    pt = sbq("pt", [128, 4, 512], BF16)
    lln = sbq("lln", [128, 512], F32)
    lrec = sbq("lrec", [128, 512], F32)
    avst = sbq("avst", [128, 2, 512], BF16)
    gbuf = sbq("gbuf", [128, 2, 2048], F32)
    gdump = sbq("gdump", [128, 2048], F32)

    P = Prog(ve, ac)

    for f in (lambda g: g.memset(ones_b[:, :], 1.0),
              lambda g: g.memset(onesf_col[:, :], 1.0),
              lambda g: g.memset(onesf_row[:, :], 1.0),
              lambda g: g.memset(eps_ln[:, :], EPS_LN),
              lambda g: g.memset(eps_bl[:, :], EPS_BL),
              lambda g: g.memset(gacc_f[:, :], 0.0)):
        P.o("g", f)
    CONST_READY = P.o("g", lambda g: g.memset(gacc_p[:, :], 0.0), cc)
    P.o("g", lambda g: g.dma_start(
        wqk_s[:, :, :, :], wqk.ap().rearrange("h dc p f -> p h dc f")), q_wq)
    WQKV_LD = P.o("g", lambda g: g.dma_start(
        wv_s[:, :, :], wv.ap().rearrange("dc p f -> p dc f")), q_wq)
    TRI_LD = P.o("sp", lambda sp: sp.dma_start(tri_s[:, :], tri.ap()), q_tri)
    for e in ("t", "v", "s"):
        P.w(e, cc, CONST_READY)
    P.w("v", q_tri, TRI_LD)

    pair_hist = []
    avdma_hist = []
    qkv_ping = [0]
    QKV_BANKS = (pC, pD, pA, pB)

    NGCH = 64

    def wfc_chunk_ap(ch):
        fg, pr = divmod(ch, 4)
        return wfc.ap()[fg, 4 * pr:4 * pr + 4, :, :].rearrange(
            "dc p f -> p dc f")

    def wproj_chunk_ap(ch):
        og, fr = divmod(ch, 8)
        return wproj.ap()[og, 8 * fr:8 * fr + 8, :, :].rearrange(
            "fc p f -> p fc f")

    gamma_pos = [0]

    def emit_gamma(n):
        for _ in range(n):
            ch = gamma_pos[0]
            if ch >= 2 * NGCH:
                return
            if ch % 2 == 0:
                gamma_chunk(gacc_f, gbuf, gdump, wfc_chunk_ap(ch // 2))
            else:
                gamma_chunk(gacc_p, gbuf, gdump, wproj_chunk_ap(ch // 2))
            gamma_pos[0] += 1

    def q_batch(b):
        for i in range(4):
            gidx = b * 4 + i
            slot = gidx % 2
            if gidx >= 2:
                P.w("g", mm, cp[("xq_done", gidx - 2)])
            ldv = P.o("g", lambda g, bb=b, ii=i: g.dma_start(
                xq[:, ii % 2, :, :], xt.ap()[bb, ii, :, :, :]), q_xq[slot])
            P.w("t", q_xq[slot], ldv)
            if gidx == 0:
                P.w("t", q_wq, WQKV_LD)
            for h2 in range(H2):
                for qk in range(2):
                    dst = (qT_s, kT_s)[qk]
                    bank = QKV_BANKS[qkv_ping[0] % 4]
                    qkv_ping[0] += 1
                    P.w("t", ve, cp.get(("evac", id(bank))))
                    if bank in (pA, pB):
                        P.w("t", ve, cp.get("avl_used"))
                        P.w("t", ac, cp.get("l_used"))
                    for dc in range(DC):
                        P.o("t", lambda t, bk=bank, h=h2, q=qk, d=dc, ii=i:
                            t.matmul(bk[:, 0:512],
                                     wqk_s[:, h, d, q * 128:(q + 1) * 128],
                                     xq[:, ii % 2, d, :],
                                     start=(d == 0), stop=(d == DC - 1)),
                            mm if dc == DC - 1 else None)
                    P.w("v", mm, mm.v)
                    cp[("evac", id(bank))] = P.o(
                        "v", lambda v, bk=bank, ds=dst, h=h2, ii=i:
                        v.tensor_copy(ds[:, h, ii * 512:(ii + 1) * 512],
                                      bk[:, 0:512]), ve)
            for t128 in range(4):
                bank = QKV_BANKS[qkv_ping[0] % 4]
                qkv_ping[0] += 1
                P.w("t", ve, cp.get(("evac", id(bank))))
                if bank in (pA, pB):
                    P.w("t", ve, cp.get("avl_used"))
                    P.w("t", ac, cp.get("l_used"))
                for dc in range(DC):
                    P.o("t", lambda t, bk=bank, d=dc, ii=i, tk=t128:
                        t.matmul(bk[:, 0:256],
                                 xq[:, ii % 2, d, tk * 128:(tk + 1) * 128],
                                 wv_s[:, d, :],
                                 start=(d == 0), stop=(d == DC - 1)),
                        mm if dc == DC - 1 else None)
                P.w("v", mm, mm.v)
                cp[("evac", id(bank))] = P.o(
                    "v", lambda v, bk=bank, kb=i * 4 + t128:
                    v.tensor_copy(v_s[:, kb, :], bk[:, 0:256]), ve)
            cp[("xq_done", gidx)] = mm.v
        qkv_done = ve.v

        for h2 in range(H2):
            P.w("t", ve, qkv_done)
            for s4 in range(4):
                kbN = 4 * (s4 + 1)
                for pr in range(kbN // 2):
                    k0, k1 = 2 * pr, 2 * pr + 1
                    if len(pair_hist) >= 2:
                        P.w("t", ac, pair_hist[-2][0])
                    for kb in (k0, k1):
                        qq = kb - 4 * s4
                        sfx = 128 * qq if qq >= 0 else 0
                        P.o("t", lambda t, h=h2, s=s4, kkb=kb, sf=sfx:
                            t.matmul(pbig[:, (kkb % 4) * 512 + sf:
                                             (kkb % 4) * 512 + 512],
                                     kT_s[:, h, kkb * 128:(kkb + 1) * 128],
                                     qT_s[:, h, s * 512 + sf:(s + 1) * 512],
                                     start=True, stop=True),
                            mm if kb == k1 else None)
                    cps = mm.v
                    P.w("s", mm, cps)
                    if len(pair_hist) >= 2:
                        P.w("s", mm, pair_hist[-2][1])
                    cpe = P.o("s", lambda s, kk=k0: s.activation(
                        pt[:, kk % 4:kk % 4 + 2, :],
                        pbig[:, (kk % 4) * 512:(kk % 4) * 512 + 1024],
                        AF.Exp, scale=INV_SQRT_HD), ac)
                    cpt = None
                    for kb in (k0, k1):
                        qq = kb - 4 * s4
                        if 0 <= qq <= 3:
                            P.w("v", ac, cpe)
                            cpt = P.o("v", lambda v, kkb=kb, q=qq:
                                      v.tensor_mul(
                                          pt[:, kkb % 4, q * 128:(q + 1) * 128],
                                          pt[:, kkb % 4, q * 128:(q + 1) * 128],
                                          tri_s[:, :]), ve)
                    P.w("t", ac, cpe)
                    if cpt is not None:
                        P.w("t", ve, cpt)
                    if pr == 0:
                        P.w("t", ve, cp.get("avl_used"))
                        P.w("t", ac, cp.get("l_used"))
                        P.w("t", ve, cp.get(("evac", id(pA))))
                        P.w("t", ve, cp.get(("evac", id(pB))))
                    for kb in (k0, k1):
                        qq = kb - 4 * s4
                        sfx = 128 * qq if qq >= 0 else 0
                        fst, lst = (kb == 0), (kb == kbN - 1)
                        P.o("t", lambda t, kkb=kb, h=h2, sf=sfx, f=fst, l=lst:
                            t.matmul(pA[:, sf:512],
                                     v_s[:, kkb, h * 128:(h + 1) * 128],
                                     pt[:, kkb % 4, sf:512],
                                     start=f, stop=l), None)
                        P.o("t", lambda t, kkb=kb, sf=sfx, f=fst, l=lst:
                            t.matmul(pB[:, sf:512], ones_b[:, :],
                                     pt[:, kkb % 4, sf:512],
                                     start=f, stop=l),
                            mm if kb == k1 else None)
                    pair_hist.append((cpe, mm.v))
                P.w("s", mm, pair_hist[-1][1])
                cp["l_used"] = P.o("s", lambda s: s.activation(
                    lln[:, :], pB[:, 0:512], AF.Ln), ac)
                clr = P.o("s", lambda s: s.activation(
                    lrec[:, :], lln[:, :], AF.Exp, scale=-1.0), ac)
                P.w("v", ac, clr)
                if len(avdma_hist) >= 2:
                    P.w("v", avdma_hist[-2][0], avdma_hist[-2][1])
                cp["avl_used"] = P.o("v", lambda v, s=s4: v.tensor_mul(
                    avst[:, s % 2, :], pA[:, 0:512], lrec[:, :]), ve)
                P.w("sp", ve, cp["avl_used"])
                j = 2 * b + (s4 // 2)
                off = (s4 % 2) * 512
                aslot = len(avdma_hist) % 2
                dv = P.o("sp", lambda sp, jj=j, h=h2, of=off, s=s4:
                         sp.dma_start(d_a2a_in.ap()[jj, h, :, of:of + 512],
                                      avst[:, s % 2, :]), q_av[aslot])
                avdma_hist.append((q_av[aslot], dv))

    for b in range(NB):
        q_batch(b)
        P.o("g", lambda g, o=b: g.dma_start(
            woutb[:, o, :, :],
            wout.ap()[:, :, o * 512:(o + 1) * 512].rearrange(
                "dvc p f -> p dvc f")), q_wo[b % 2])
        emit_gamma(32)
    emit_gamma(2 * NGCH)
    gamma_final(gacc_f, tpos_f, tneg_f, gneg_f)
    gamma_final(gacc_p, tpos_p, tneg_p, gneg_p)
    GFIN_P = cp[("gfin", id(gacc_p))]

    P.replay(nc)
    ctxq.close()
    GBF_END = (q_gb[0].v, q_gb[1].v)
    GAM_ABS_END = ghist[-1][0]
    VE_Q_END = ve.v
    AC_Q_END = ac.v

    ctxw = ExitStack()
    sbw = lambda n, s, d: ctxw.enter_context(nc.sbuf_tensor(n, s, d))
    avT_all = sbw("avT_all", [128, 16, 1024], BF16)
    stream = sbw("stream", [128, TC, D], F32)
    xn_st = sbw("xn_st", [128, D], F32)
    xh_st = sbw("xh_st", [128, 2, D], BF16)

    P = Prog(ve, ac)

    P.w("g", q_av[0], q_av[0].v)
    P.w("g", q_av[1], q_av[1].v)

    def _cc_call(g):
        with allow_ici_all_to_all():
            return g.collective_compute(
                "AllToAll", ALU.bypass, replica_groups=[list(range(8))],
                ins=[d_a2a_in.ap().rearrange("a b c d -> a (b c d)")],
                outs=[d_a2a_out.ap().rearrange("a b c d -> a (b c d)")])

    CC_DONE = P.o("g", _cc_call, cc)

    cp[("evac", id(pD))] = cp[("gfin", id(gacc_p))]
    cp[("evac", id(pC))] = None

    P.w("sp", cc, CC_DONE)
    P.w("sp", q_gb[0], GBF_END[0])
    P.w("sp", q_gb[1], GBF_END[1])
    P.w("sp", ac, GAM_ABS_END)
    for j in range(8):
        for h2 in range(H2):
            AVT_LD = P.o("sp", lambda sp, jj=j, h=h2: sp.dma_start(
                avT_all[:, jj * 2 + h, :], d_a2a_out.ap()[jj, h, :, :]), q_lda)
    for tc in range(TC):
        XO_LD = P.o("sp", lambda sp, t=tc: sp.dma_start(
            stream[:, t, :], xo.ap()[t, :, :]), q_ldx)

    P.w("t", ve, VE_Q_END)
    P.w("t", ac, AC_Q_END)
    P.w("t", q_lda, AVT_LD)
    P.w("t", q_wo[0], q_wo[0].v)
    P.w("t", q_wo[1], q_wo[1].v)
    cxh_prev = None
    for tc in range(TC):
        even = (tc % 2 == 0)
        banks = ([lambda o: pbig[:, o * 512:(o + 1) * 512]] if even
                 else [lambda o: (pA, pB, pC, pD)[o][:, 0:512]])[0]
        P.w("t", ve, cp.get(("wevac", even)))
        for dvc in range(DC):
            for og in range(4):
                P.o("t", lambda t, bk=banks(og), d=dvc, tt=tc, o=og:
                    t.matmul(bk,
                             avT_all[:, d, tt * 128:(tt + 1) * 128],
                             woutb[:, o, d, :],
                             start=(d == 0), stop=(d == DC - 1)),
                    mm if (og == 3 and dvc == DC - 1) else None)
        mv = mm.v
        P.w("v", mm, mv)
        if tc == 0:
            P.w("v", q_ldx, XO_LD)
        if even:
            cev = P.o("v", lambda v, tt=tc: v.tensor_add(
                stream[:, tt, :], pbig[:, 0:2048], stream[:, tt, :]), ve)
        else:
            for og in range(4):
                cev = P.o("v", lambda v, tt=tc, o=og: v.tensor_add(
                    stream[:, tt, o * 512:(o + 1) * 512],
                    (pA, pB, pC, pD)[o][:, 0:512],
                    stream[:, tt, o * 512:(o + 1) * 512]), ve)
        cp[("wevac", even)] = cev

        if cxh_prev is not None:
            P.w("s", ve, cxh_prev)
        cmb = ln_stats(stream[:, tc, :], eps_ln, xn_st[:, :], ve, cev)
        P.w("s", ve, cmb)
        cx1 = P.o("s", lambda s, t=tc: s.activation(
            stream[:, t, :], stream[:, t, :], AF.Identity,
            bias=st_mb[:, 0:1], scale=st_rstd[:, 0:1]), ac)
        cmb2 = ln_stats(stream[:, tc, :], eps_bl, xn_st[:, :], ac, cx1)
        P.w("s", ve, cmb2)
        cxn = P.o("s", lambda s, t=tc: s.activation(
            xn_st[:, :], stream[:, t, :], AF.Identity,
            bias=st_mb[:, 0:1], scale=st_rstd[:, 0:1]), ac)
        P.w("sp", ac, cxn)
        P.o("sp", lambda sp, t=tc: sp.dma_start(
            d_x1.ap()[t, :, :], stream[:, t, :]), sq)
        P.w("v", ac, cxn)
        cg1 = P.o("v", lambda v: v.reduce_max(st_g[:, 0:1], xn_st[:, :], AX.X,
                                              apply_absolute_value=True), ve)
        P.w("s", ve, cg1)
        P.o("s", lambda s: s.activation(st_gp[:, 0:1], st_g[:, 0:1],
                                        AF.Identity, scale=CLIP_C), ac)
        cg2 = P.o("s", lambda s: s.activation(st_gn[:, 0:1], st_g[:, 0:1],
                                              AF.Identity, scale=-CLIP_C), ac)
        P.w("v", ac, cg2)
        if tc >= 2:
            P.w("v", q_xh[tc % 2], cp[("xhat_dma", tc - 2)])
        cxh = P.o("v", lambda v, t=tc: v.tensor_scalar(
            xh_st[:, t % 2, :], xn_st[:, :], st_gn[:, 0:1], st_gp[:, 0:1],
            op0=ALU.max, op1=ALU.min), ve)
        cxh_prev = cxh
        P.w("sp", ve, cxh)
        cp[("xhat_dma", tc)] = P.o("sp", lambda sp, t=tc: sp.dma_start(
            d_xhat.ap()[t * 128:(t + 1) * 128, :], xh_st[:, t % 2, :]),
            q_xh[tc % 2])
    X1_SPILLED = sq.v

    P.replay(nc)
    ctxw.close()
    ctxqw.close()

    ctxh = ExitStack()
    htens = ctxh.enter_context(nc.sbuf_tensor("htens", [128, TC, DFF], BF16))
    hsump = ctxh.enter_context(nc.sbuf_tensor("hsump", [128, TC, FG], F32))
    hsqp = ctxh.enter_context(nc.sbuf_tensor("hsqp", [128, TC, FG], F32))

    ctxf1 = ExitStack()
    sbf1 = lambda n, s, d: ctxf1.enter_context(nc.sbuf_tensor(n, s, d))
    xhatT = sbf1("xhatT", [128, DC, 1024], BF16)
    ternf = sbf1("ternf", [128, 2, DC, 512], BF16)
    wch = sbf1("wch", [128, 2, 1024], F32)
    tsa_t = sbf1("tsa_t", [128, 1024], F32)
    tsa = tsa_t[:, :]

    P = Prog(ve, ac)
    P.w("sp", q_xh[0], q_xh[0].v)
    P.w("sp", q_xh[1], q_xh[1].v)
    P.w("s", sq, X1_SPILLED)
    P.w("s", q_xh[0], q_xh[0].v)
    P.w("s", q_xh[1], q_xh[1].v)
    for dc in range(DC):
        XT_LD = P.o("sp", lambda sp, d=dc: sp.dma_start(
            xhatT[:, d, :], d_xhat.ap()[:, d * 128:(d + 1) * 128],
            transpose=True), q_tr)

    for fg in range(FG):
        for chh in range(8):
            gidx = fg * 8 + chh
            slot = gidx % 2
            if gidx >= 2:
                P.w("sp", ve, cp[("ternf_ch", gidx - 2)])
            ldv = P.o("sp", lambda sp, f=fg, c=chh, b=slot: sp.dma_start(
                wch[:, b, :].rearrange("p (a f) -> p a f", a=2),
                wfc.ap()[f, 2 * c:2 * c + 2, :, :].rearrange(
                    "dc p f -> p dc f")), q_wc[slot])
            P.w("v", q_wc[slot], ldv)
            if fg >= 2 and chh == 0:
                P.w("v", mm, cp[("fc_mm", fg - 2)])
            P.o("v", lambda v, b=slot: v.tensor_scalar(
                tsa, wch[:, b, :], tpos_f[:, 0:1], None,
                op0=ALU.is_ge), ve)
            cp[("ternf_ch", gidx)] = P.o(
                "v", lambda v, b=slot, f=fg, c=chh:
                v.scalar_tensor_tensor(
                    ternf[:, f % 2, 2 * c:2 * c + 2, :].rearrange(
                        "p a f -> p (a f)"),
                    wch[:, b, :], tneg_f[:, 0:1], tsa,
                    op0=ALU.is_le, op1=ALU.subtract), ve)
        tern_done = ve.v

        P.w("t", ve, tern_done)
        if fg == 0:
            P.w("t", q_tr, XT_LD)
        for tc in range(TC):
            bank = (pC, pD)[tc % 2]
            P.w("t", ac, cp.get(("fc_evac", id(bank))))
            for dc in range(DC):
                P.o("t", lambda t, bk=bank, d=dc, tt=tc, f=fg:
                    t.matmul(bk[:, 0:512],
                             xhatT[:, d, tt * 128:(tt + 1) * 128],
                             ternf[:, f % 2, d, :],
                             start=(d == 0), stop=(d == DC - 1)),
                    mm if dc == DC - 1 else None)
            P.w("s", mm, mm.v)
            cp[("fc_evac", id(bank))] = P.o(
                "s", lambda s, bk=bank, tt=tc, f=fg: s.activation(
                    htens[:, tt, f * 512:(f + 1) * 512], bk[:, 0:512],
                    AF.Gelu, scale=gneg_f[:, 0:1],
                    accum_out=hsump[:, tt, f:f + 1]), ac)
            csq = P.o("s", lambda s, tt=tc, f=fg: s.activation(
                pbig[:, 0:512], htens[:, tt, f * 512:(f + 1) * 512],
                AF.Square, accum_out=hsqp[:, tt, f:f + 1]), ac)

        cp[("fc_mm", fg)] = mm.v
    F1_DONE_AC = ac.v
    F1_DONE_VE = ve.v

    P.replay(nc)
    ctxf1.close()

    ctx15 = ExitStack()
    hn_st = ctx15.enter_context(nc.sbuf_tensor("hn_st", [128, 2, DFF], BF16))
    hh_st = ctx15.enter_context(nc.sbuf_tensor("hh_st", [128, 2, DFF], BF16))
    q_tr2 = Sem(nc, "q_tr2", dma=True)

    P = Prog(ve, ac)
    P.w("v", ac, F1_DONE_AC)
    for tc in range(TC):
        P.o("v", lambda v, t=tc: v.reduce_sum(
            st8_m[:, t:t + 1], hsump[:, t, :], AX.X), ve)
        P.o("v", lambda v, t=tc: v.reduce_sum(
            st8_sq[:, t:t + 1], hsqp[:, t, :], AX.X), ve)
    c2 = ve.v
    P.w("s", ve, c2)
    P.o("s", lambda s: s.activation(
        st8_m[:, :], st8_m[:, :], AF.Identity, scale=1.0 / DFF), ac)
    P.o("s", lambda s: s.activation(
        st8_sq[:, :], st8_sq[:, :], AF.Identity, scale=1.0 / DFF), ac)
    c4 = P.o("s", lambda s: s.activation(
        st8_var[:, :], st8_m[:, :], AF.Square), ac)
    P.w("v", ac, c4)
    c5 = P.o("v", lambda v: v.tensor_sub(st8_var[:, :], st8_var[:, :],
                                         st8_sq[:, :]), ve)
    P.w("s", ve, c5)
    P.o("s", lambda s: s.activation(
        st8_r[:, :], st8_var[:, :], AF.Ln,
        bias=eps_bl[:, 0:1], scale=-1.0), ac)
    c6 = P.o("s", lambda s: s.activation(
        st8_r[:, :], st8_r[:, :], AF.Exp, scale=-0.5), ac)
    P.w("v", ac, c6)
    c_nm = P.o("v", lambda v: v.scalar_tensor_tensor(
        st8_nm[:, :], st8_m[:, :], -1.0, st8_r[:, :],
        op0=ALU.mult, op1=ALU.mult), ve)

    for tc in range(TC):
        if tc == 0:
            P.w("s", ve, c_nm)
        if tc >= 2:
            P.w("s", ve, cp[("f15_clip", tc - 2)])
        chn = P.o("s", lambda s, t=tc: s.activation(
            hn_st[:, t % 2, :], htens[:, t, :], AF.Identity,
            bias=st8_nm[:, t:t + 1], scale=st8_r[:, t:t + 1]), ac)
        P.w("v", ac, chn)
        cg1 = P.o("v", lambda v, t=tc: v.reduce_max(
            st_g[:, 0:1], hn_st[:, t % 2, :], AX.X,
            apply_absolute_value=True), ve)
        P.w("s", ve, cg1)
        P.o("s", lambda s: s.activation(st_gp[:, 0:1], st_g[:, 0:1],
                                        AF.Identity, scale=CLIP_C), ac)
        cg2 = P.o("s", lambda s: s.activation(st_gn[:, 0:1], st_g[:, 0:1],
                                              AF.Identity, scale=-CLIP_C), ac)
        P.w("v", ac, cg2)
        if tc >= 2:
            P.w("v", q_hh, cp[("hhat_dma", tc - 2)])
        chh2 = P.o("v", lambda v, t=tc: v.tensor_scalar(
            hh_st[:, t % 2, :], hn_st[:, t % 2, :], st_gn[:, 0:1],
            st_gp[:, 0:1], op0=ALU.max, op1=ALU.min), ve)
        cp[("f15_clip", tc)] = chh2
        P.w("sp", ve, chh2)
        chd = P.o("sp", lambda sp, t=tc: sp.dma_start(
            d_hhat.ap()[t, :, :], hh_st[:, t % 2, :]), q_hh)
        cp[("hhat_dma", tc)] = chd
        P.w("sp", ac, chn)
        P.w("sp", q_hh, chd)
        P.w("s", q_hh, chd)
        for fc in range(64):
            eng = "sp" if fc % 2 == 0 else "s"
            qq_tr = q_tr if fc % 2 == 0 else q_tr2
            P.o(eng, lambda E, t=tc, f=fc: E.dma_start(
                htens[:, t, f * 128:(f + 1) * 128],
                d_hhat.ap()[t, :, f * 128:(f + 1) * 128],
                transpose=True), qq_tr)
    F15_DONE = q_tr.v
    F15_DONE2 = q_tr2.v

    P.replay(nc)
    ctx15.close()

    ctxf2 = ExitStack()
    sbf2 = lambda n, s, d: ctxf2.enter_context(nc.sbuf_tensor(n, s, d))
    ternp = sbf2("ternp", [128, 2, 64, 256], BF16)
    wch2 = sbf2("wch2", [128, 2, 1024], F32)
    pst = sbf2("pst", [128, 2, 512], F32)
    tsa2 = pbig[:, 1024:2048]

    P = Prog(ve, ac)
    P.w("v", q_hh, q_hh.v)
    pst_hist = []
    for pg in range(OG // 2):
        for chh in range(32):
            og = 2 * pg + chh // 16
            c = chh % 16
            tslot = chh // 16
            gidx = pg * 32 + chh
            wslot = gidx % 2
            if gidx >= 2:
                P.w("sp", ve, cp[("ternp_ch", gidx - 2)])
            if pg >= 1 and chh == 0:
                P.w("sp", mm, cp[("proj_mm", pg - 1)])
            ldv = P.o("sp", lambda sp, o=og, cx=c, b=wslot: sp.dma_start(
                wch2[:, b, :].rearrange("p (a f) -> p a f", a=4),
                wproj.ap()[o, 4 * cx:4 * cx + 4, :, :].rearrange(
                    "fc p f -> p fc f")), q_wc[wslot])
            P.w("v", q_wc[wslot], ldv)
            P.o("v", lambda v, b=wslot: v.tensor_scalar(
                tsa2, wch2[:, b, :], tpos_p[:, 0:1], None,
                op0=ALU.is_ge), ve)
            cp[("ternp_ch", gidx)] = P.o(
                "v", lambda v, b=wslot, ts=tslot, cx=c:
                v.scalar_tensor_tensor(
                    ternp[:, ts, 4 * cx:4 * cx + 4, :].rearrange(
                        "p a f -> p (a f)"),
                    wch2[:, b, :], tneg_p[:, 0:1], tsa2,
                    op0=ALU.is_le, op1=ALU.subtract), ve)
        ternp_done = ve.v

        P.w("t", ve, ternp_done)
        if pg == 0:
            P.w("t", q_tr, F15_DONE)
            P.w("t", q_tr2, F15_DONE2)
        for tc in range(TC):
            bank = (pC, pD)[tc % 2]
            P.w("t", ve, cp.get(("pj_evac", id(bank))))
            for fc in range(64):
                P.o("t", lambda t, bk=bank, f=fc, tt=tc:
                    t.matmul(bk[:, 0:512],
                             htens[:, tt, f * 128:(f + 1) * 128],
                             ternp[:, :, f, :],
                             start=(f == 0), stop=(f == 63)),
                    mm if fc == 63 else None)
            P.w("v", mm, mm.v)
            if len(pst_hist) >= 2:
                P.w("v", pst_hist[-2][0], pst_hist[-2][1])
            cpe = P.o("v", lambda v, bk=bank, tt=tc: v.tensor_scalar(
                pst[:, tt % 2, :], bk[:, 0:512], gneg_p[:, 0:1], None,
                op0=ALU.mult), ve)
            cp[("pj_evac", id(bank))] = cpe
            P.w("sp", ve, cpe)
            pslot = tc % 2
            dv = P.o("sp", lambda sp, tt=tc, p=pg: sp.dma_start(
                d_proj.ap()[tt, :, p * 512:(p + 1) * 512],
                pst[:, tt % 2, :]), q_ps[pslot])
            pst_hist.append((q_ps[pslot], dv))
        cp[("proj_mm", pg)] = mm.v

    P.replay(nc)
    ctxf2.close()
    ctxh.close()

    ctxn = ExitStack()
    sbn = lambda n, s, d: ctxn.enter_context(nc.sbuf_tensor(n, s, d))
    fa = sbn("fa", [128, 2, D], F32)
    fb = sbn("fb", [128, 2, D], F32)
    fsq = sbn("fsq", [128, D], F32)
    fo = sbn("fo", [128, 2, D], F32)

    P = Prog(ve, ac)
    P.w("sp", q_ps[0], q_ps[0].v)
    P.w("sp", q_ps[1], q_ps[1].v)
    P.w("sp", sq, X1_SPILLED)
    for tc in range(TC):
        fslot = tc % 2
        if tc >= 2:
            P.w("sp", ac, cp[("fin_ln", tc - 2)])
        P.o("sp", lambda sp, t=tc: sp.dma_start(
            fa[:, t % 2, :], d_proj.ap()[t, :, :]), q_fn[fslot])
        ldv = P.o("sp", lambda sp, t=tc: sp.dma_start(
            fb[:, t % 2, :], d_x1.ap()[t, :, :]), q_fn[fslot])
        P.w("v", q_fn[fslot], ldv)
        cadd = P.o("v", lambda v, t=tc: v.tensor_add(
            fa[:, t % 2, :], fa[:, t % 2, :], fb[:, t % 2, :]), ve)
        cmb = ln_stats(fa[:, tc % 2, :], eps_ln, fsq[:, :], ve, cadd)
        P.w("s", ve, cmb)
        if tc >= 2:
            P.w("s", q_st[tc % 2], cp[("out_dma", tc - 2)])
        cfo = P.o("s", lambda s, t=tc: s.activation(
            fo[:, t % 2, :], fa[:, t % 2, :], AF.Identity,
            bias=st_mb[:, 0:1], scale=st_rstd[:, 0:1]), ac)
        cp[("fin_ln", tc)] = cfo
        P.w("sp", ac, cfo)
        cp[("out_dma", tc)] = P.o("sp", lambda sp, t=tc: sp.dma_start(
            out.ap()[t, :, :], fo[:, t % 2, :]), q_st[tc % 2])
    P.w("sp", q_st[0], q_st[0].v)
    P.w("sp", q_st[1], q_st[1].v)

    P.replay(nc)
    ctxn.close()
    psum.close()
    ctx0.close()
    return nc


_NC_CACHE = None


def _get_nc():
    global _NC_CACHE
    if _NC_CACHE is None:
        _NC_CACHE = build_nc()
    return _NC_CACHE


def kernel(L0_attn_x, L0_attn_w_qkv, L0_attn_w_out, L0_ffn_w_fc, L0_ffn_w_proj):
    x = np.ascontiguousarray(np.asarray(L0_attn_x, dtype=np.float32))
    w_qkv = np.ascontiguousarray(np.asarray(L0_attn_w_qkv, dtype=np.float32))
    w_out = np.ascontiguousarray(np.asarray(L0_attn_w_out, dtype=np.float32))
    w_fc = np.ascontiguousarray(np.asarray(L0_ffn_w_fc, dtype=np.float32))
    w_proj = np.ascontiguousarray(np.asarray(L0_ffn_w_proj, dtype=np.float32))

    xt = np.ascontiguousarray(
        x.transpose(0, 2, 1).reshape(NB, DC, 128, 4, 512).transpose(0, 3, 2, 1, 4))
    wout_dev = np.ascontiguousarray(w_out.reshape(DC, 128, D))
    wfc_dev = np.ascontiguousarray(
        w_fc.T.reshape(DC, 128, FG, 512).transpose(2, 0, 1, 3))
    wproj_dev = np.ascontiguousarray(
        w_proj.T.reshape(64, 128, OG, 256).transpose(2, 0, 1, 3))
    tri_m = np.ascontiguousarray(
        (np.arange(128)[None, :] >= np.arange(128)[:, None]).astype(BF))

    in_maps = []
    for c in range(8):
        b, hf = divmod(c, 2)
        xo_c = np.ascontiguousarray(
            x[b, hf * 1024:(hf + 1) * 1024].reshape(TC, 128, D))
        heads = [2 * c, 2 * c + 1]
        wqk_c = np.ascontiguousarray(np.stack([
            np.concatenate([w_qkv[:, h * 128:(h + 1) * 128],
                            w_qkv[:, D + h * 128:D + (h + 1) * 128]], axis=1)
            for h in heads]).reshape(H2, DC, 128, 256))
        wv_c = np.ascontiguousarray(np.concatenate(
            [w_qkv[:, 2 * D + h * 128:2 * D + (h + 1) * 128] for h in heads],
            axis=1).reshape(DC, 128, 256))
        in_maps.append({
            "xt": xt, "xo": xo_c, "wqk": wqk_c, "wv": wv_c,
            "wout": wout_dev, "wfc": wfc_dev, "wproj": wproj_dev, "tri": tri_m,
        })

    nc = _get_nc()
    res = run_bass_kernel_spmd(nc, in_maps, core_ids=list(range(8)))

    y = np.empty((NB, S, D), dtype=np.float32)
    for c in range(8):
        b, hf = divmod(c, 2)
        y[b, hf * 1024:(hf + 1) * 1024] = res.results[c]["out"].reshape(1024, D)
    return y
